# version 01
# baseline (speedup 1.0000x reference)
import numpy as np
import ml_dtypes
import concourse.bass as bass
import concourse.mybir as mybir
from concourse.bass_utils import run_bass_kernel_spmd

F32 = mybir.dt.float32
BF16 = mybir.dt.bfloat16
FP8 = mybir.dt.float8e4
AX = mybir.AxisListType
ALU = mybir.AluOpType
AF = mybir.ActivationFunctionType

NCORES = 8
BL = 4
S = 2048
T = BL * S
E = 1024
NT = T // 128

BIG_NEG = np.float32(-3.4028235e38)
BF = ml_dtypes.bfloat16


class P:
    ENGINES = ("sync", "scalar", "vector", "tensor", "gpsimd")

    def __init__(self, nc):
        self.nc = nc
        self.ops = []

    def op(self, engine, fn, inc=1):
        self.ops.append((engine, fn, inc))

    def dma(self, out, in_, engine="sync", **kw):
        self.ops.append((engine, lambda e: e.dma_start(out=out, in_=in_, **kw), 16))

    def emit(self):
        nc = self.nc
        starts, c = [], 0
        for (_, _, inc) in self.ops:
            starts.append(c)
            c += inc
        total = c
        by_eng = {}
        for i, o in enumerate(self.ops):
            by_eng.setdefault(o[0], []).append((i, o))
        with nc.semaphore("tick") as tick, nc.Block() as block:
            def mk(ename):
                def body(eng):
                    for i, (_, fn, inc) in by_eng.get(ename, []):
                        if starts[i] > 0:
                            eng.wait_ge(tick, starts[i])
                        fn(eng).then_inc(tick, inc)
                    if ename == "sync":
                        eng.wait_ge(tick, total)
                body.__name__ = "body_" + ename
                return body
            for ename in self.ENGINES:
                if ename in by_eng or ename == "sync":
                    getattr(block, ename)(mk(ename))
        return total


class Bufs:

    def __init__(self, nc, stack):
        self.nc, self.stack, self.cache = nc, stack, {}

    def sb(self, name, shape, dtype=F32):
        if name not in self.cache:
            self.cache[name] = self.stack.enter_context(
                self.nc.sbuf_tensor(name, list(shape), dtype))
        return self.cache[name]

    def ps(self, name, shape, dtype=F32):
        if name not in self.cache:
            self.cache[name] = self.stack.enter_context(
                self.nc.psum_tensor(name, list(shape), dtype))
        return self.cache[name]


def make_constants():
    c = {}
    c["identb"] = np.eye(128).astype(BF)
    c["identf"] = np.eye(128, dtype=np.float32)
    e64 = np.eye(64)
    c["identb64"] = np.vstack([e64, e64]).astype(BF)
    row, col = np.mgrid[0:128, 0:128]
    c["triu_mult"] = (row <= col).astype(np.float32)
    NEGB = -1e38
    c["maskLb"] = np.where(col >= row, 0.0, NEGB).astype(BF)
    c["maskRb"] = np.where(col <= row, 0.0, NEGB).astype(BF)
    DH, WINDOW = 64, 512
    half = np.arange(0, DH, 2, dtype=np.float64)
    inv_freq = 1.0 / 10000.0 ** (half / DH)
    t = np.arange(S, dtype=np.float64)
    freqs = np.concatenate([t[:, None] * inv_freq[None, :]] * 2, -1)
    base = (half + 0.4 * DH) / (1.4 * DH)
    power = (t - S // 2) / float(WINDOW // 2)
    scale = np.concatenate([base[None, :] ** power[:, None]] * 2, -1)
    cos, sin = np.cos(freqs), np.sin(freqs)
    Sq, Sk = (sin * scale).T, (sin / scale).T
    Aq, Ak = (cos * scale).T, (cos / scale).T
    Bq = np.concatenate([-Sq[0:32], Sq[32:64]], 0)
    Bk = np.concatenate([-Sk[0:32], Sk[32:64]], 0)
    c["rot_Aq"] = np.vstack([Aq, Aq]).astype(np.float32)
    c["rot_Ak"] = np.vstack([Ak, Ak]).astype(np.float32)
    c["rot_Bq"] = np.vstack([Bq, Bq]).astype(np.float32)
    c["rot_Bk"] = np.vstack([Bk, Bk]).astype(np.float32)
    cols = np.zeros((128, 8), dtype=np.float32)
    cols[:, 0] = 1.0
    cols[0:64, 1] = 1.0
    cols[64:128, 2] = 1.0
    cols[0:64, 3] = 1.0 / 128.0
    cols[64:128, 4] = 1.0 / 128.0
    c["cols"] = cols
    c["one_row"] = np.ones((1, 128), dtype=np.float32)
    c["negone_row"] = np.full((1, 128), -1.0, dtype=np.float32)
    return c


CONST_SPECS = {
    "identb": BF16, "identf": F32, "identb64": BF16,
    "triu_mult": F32, "maskLb": BF16, "maskRb": BF16,
    "rot_Aq": F32, "rot_Bq": F32, "rot_Ak": F32, "rot_Bk": F32,
    "cols": F32, "one_row": F32, "negone_row": F32,
}


def load_constants(p, nc, bufs, cvals, names=None):
    tiles = {}
    for name, dt in CONST_SPECS.items():
        if names is not None and name not in names:
            continue
        arr = cvals[name]
        d = nc.dram_tensor("c_" + name, list(arr.shape), dt, kind="ExternalInput").ap()
        tsb = bufs.sb("sb_" + name, arr.shape, dt)
        p.dma(tsb[:, :], d[:, :])
        tiles[name] = tsb
    return tiles


def load_col(p, nc, bufs, dram_vec, lo, n, name):
    t = bufs.sb(name, (128, 1), F32)
    src = dram_vec[lo:lo + n].rearrange("(p one) -> p one", one=1)
    p.dma(t[0:n, :], src, allow_slow_non_contiguous=True)
    return t


def emit_layernorm_tile(p, nc, bufs, xt, xn, eps, n=E, pref=""):
    st = bufs.sb(pref + "ln_stat", (128, 8), F32)
    sq = bufs.sb(pref + "ln_sq", (128, n), F32)
    p.op("vector", lambda e: e.reduce_sum(st[:, 0:1], xt[:, 0:n], AX.X))
    p.op("vector", lambda e: e.tensor_scalar_mul(st[:, 1:2], st[:, 0:1], -1.0 / n))
    p.op("vector", lambda e: e.tensor_scalar_add(xn[:, 0:n], xt[:, 0:n], st[:, 1:2]))
    p.op("scalar", lambda e: e.activation(sq[:, 0:n], xn[:, 0:n], AF.Square,
                                          accum_out=st[:, 2:3]))
    p.op("scalar", lambda e: e.activation(st[:, 3:4], st[:, 2:3], AF.Sqrt,
                                          bias=np.float32(eps), scale=1.0 / n))
    p.op("vector", lambda e: e.reciprocal(st[:, 4:5], st[:, 3:4]))
    p.op("vector", lambda e: e.tensor_scalar_mul(xn[:, 0:n], xn[:, 0:n], st[:, 4:5]))


def emit_transpose_store(p, nc, bufs, srcb, dst_dram, row0, col0, nf, idb, pref=""):
    pst = bufs.ps(pref + "tr_ps", (128, 128), BF16)
    tb = bufs.sb(pref + "tr_out", (128, 128), BF16)
    for f in range(nf):
        p.op("tensor", lambda e, f=f: e.transpose(pst[:, :], srcb[:, f * 128:(f + 1) * 128], idb[:, :]))
        p.op("vector", lambda e: e.tensor_copy(tb[:, :], pst[:, :]))
        p.dma(dst_dram[row0 + f * 128:row0 + (f + 1) * 128, col0:col0 + 128], tb[:, :])


def emit_ln0(p, nc, bufs, consts, x_dram, xn_dram, xnT_dram, eps=1e-5):
    xt = bufs.sb("ln0_x", (128, E), F32)
    xb = bufs.sb("ln0_xb", (128, E), BF16)
    for i in range(NT):
        p.dma(xt[:, :], x_dram[i * 128:(i + 1) * 128, :])
        emit_layernorm_tile(p, nc, bufs, xt, xt, eps)
        if xn_dram is not None:
            p.dma(xn_dram[i * 128:(i + 1) * 128, :], xt[:, :])
        if xnT_dram is not None:
            p.op("vector", lambda e: e.tensor_copy(xb[:, :], xt[:, :]))
            emit_transpose_store(p, nc, bufs, xb, xnT_dram, 0, i * 128, 8,
                                 consts["identb"])


def np_layernorm(x, eps=1e-5):
    m = x.mean(-1, keepdims=True)
    v = x.var(-1, keepdims=True)
    return (x - m) / np.sqrt(v + eps)


M1 = 1572864.0


def emit_quant_tile(p, nc, bufs, xt, xd, n=E, pref="q_"):
    nb = n // 32
    am = bufs.sb(pref + "am", (128, E // 32), F32)
    y = bufs.sb(pref + "y", (128, E), F32)
    g = bufs.sb(pref + "g", (128, E), F32)
    f8 = bufs.sb(pref + "f8", (128, E), FP8)
    r1 = bufs.sb(pref + "r1", (128, E), F32)
    xr = xt[:, 0:n].rearrange("p (k c) -> p k c", c=32)
    p.op("vector", lambda e: e.tensor_reduce(am[:, 0:nb], xr, AX.X, ALU.max,
                                             apply_absolute_value=True))
    p.op("vector", lambda e: e.tensor_scalar(am[:, 0:nb], am[:, 0:nb],
                                             1.0 / 7.5, 1e-30, ALU.mult, ALU.max))
    rs = bufs.sb(pref + "rs", (128, E // 32), F32)
    p.op("vector", lambda e: e.reciprocal(rs[:, 0:nb], am[:, 0:nb]))
    bc = lambda t: t[:, 0:nb].unsqueeze(2).broadcast_to([128, nb, 32])
    yr = y[:, 0:n].rearrange("p (k c) -> p k c", c=32)
    p.op("vector", lambda e: e.tensor_tensor(yr, xr, bc(rs), ALU.mult))
    p.op("vector", lambda e: e.tensor_copy(f8[:, 0:n], y[:, 0:n]))
    p.op("vector", lambda e: e.tensor_copy(g[:, 0:n], f8[:, 0:n]))
    p.op("vector", lambda e: e.tensor_scalar_add(r1[:, 0:n], y[:, 0:n], M1))
    p.op("vector", lambda e: e.tensor_scalar_add(r1[:, 0:n], r1[:, 0:n], -M1))
    p.op("vector", lambda e: e.tensor_scalar(r1[:, 0:n], r1[:, 0:n], 1.0, -1.0,
                                             ALU.min, ALU.max))
    p.op("vector", lambda e: e.tensor_scalar(y[:, 0:n], g[:, 0:n], 1.0, -1.0,
                                             ALU.min, ALU.max))
    p.op("vector", lambda e: e.tensor_sub(g[:, 0:n], g[:, 0:n], y[:, 0:n]))
    p.op("vector", lambda e: e.tensor_add(g[:, 0:n], g[:, 0:n], r1[:, 0:n]))
    xdr = xd[:, 0:n].rearrange("p (k c) -> p k c", c=32)
    gr = g[:, 0:n].rearrange("p (k c) -> p k c", c=32)
    p.op("vector", lambda e: e.tensor_tensor(xdr, gr, bc(am), ALU.mult))


def np_quant(x):
    vals = set()
    for bits in range(64):
        s = bits >> 5 & 1
        ee = bits >> 3 & 3
        m = bits & 7
        v = m / 8.0 if ee == 0 else (1.0 + m / 8.0) * 2.0 ** (ee - 1)
        vals.add(-v if s else v)
    grid = np.array(sorted(vals), dtype=np.float32)
    rows, K = x.shape
    b = x.reshape(rows, K // 32, 32)
    scales = np.maximum(np.max(np.abs(b), -1) / 7.5, 1e-30).astype(np.float32)
    y = (b / scales[..., None]).astype(np.float32)
    idx = np.clip(np.searchsorted(grid, y), 0, grid.shape[0] - 1)
    lo = np.maximum(idx - 1, 0)
    vhi, vlo = grid[idx], grid[lo]
    q = np.where(np.abs(y - vlo) <= np.abs(y - vhi), vlo, vhi)
    return (q * scales[..., None]).reshape(rows, K).astype(np.float32)


def emit_cast_bf16(p, nc, bufs, src_dram, dst_dram, R, C, pref="cb_"):
    a = bufs.sb(pref + "a", (128, 2048), F32)
    b = bufs.sb(pref + "b", (128, 2048), BF16)
    for r0 in range(0, R, 128):
        nr = min(128, R - r0)
        for c0 in range(0, C, 2048):
            ncol = min(2048, C - c0)
            p.dma(a[0:nr, 0:ncol], src_dram[r0:r0 + nr, c0:c0 + ncol])
            p.op("vector", lambda e, nr=nr, ncol=ncol: e.tensor_copy(
                b[0:nr, 0:ncol], a[0:nr, 0:ncol]))
            p.dma(dst_dram[r0:r0 + nr, c0:c0 + ncol], b[0:nr, 0:ncol])


def emit_transpose_bf16(p, nc, bufs, src_dram, dst_dram, R, C, idb,
                        quant=False, scale=None, pref="tb_"):
    a = bufs.sb(pref + "a", (128, max(C, 32)), F32)
    b = bufs.sb(pref + "b", (128, max(C, 32)), BF16)
    pst = bufs.ps(pref + "ps", (128, 128), BF16)
    o = bufs.sb(pref + "o", (128, 128), BF16)
    assert C <= 8192
    for r0 in range(0, R, 128):
        nr = min(128, R - r0)
        p.dma(a[0:nr, 0:C], src_dram[r0:r0 + nr, 0:C])
        if quant:
            assert nr == 128
            emit_quant_tile(p, nc, bufs, a, a, n=C, pref=pref + "q_")
        if scale is not None:
            p.op("vector", lambda e, nr=nr: e.tensor_scalar_mul(
                a[0:nr, 0:C], a[0:nr, 0:C], float(scale)))
        p.op("vector", lambda e, nr=nr: e.tensor_copy(b[0:nr, 0:C], a[0:nr, 0:C]))
        for c0 in range(0, C, 128):
            ncol = min(128, C - c0)
            p.op("tensor", lambda e, nr=nr, c0=c0, ncol=ncol: e.transpose(
                pst[0:ncol, 0:nr], b[0:nr, c0:c0 + ncol], idb[0:nr, 0:nr]))
            p.op("vector", lambda e, nr=nr, ncol=ncol: e.tensor_copy(
                o[0:ncol, 0:nr], pst[0:ncol, 0:nr]))
            p.dma(dst_dram[c0:c0 + ncol, r0:r0 + nr], o[0:ncol, 0:nr])


def load_bias_cols(p, nc, bufs, dram_vec, n, name):
    nf = n // 128
    t = bufs.sb(name, (128, nf), F32)
    src = dram_vec[0:n].rearrange("(f p) -> p f", p=128)
    p.dma(t[:, 0:nf], src, allow_slow_non_contiguous=True)
    return t


def emit_l0_attn(p, nc, bufs, consts, xnT, qw, kw, vw, qb_d, kb_d, vb_d,
                 proj_d, attnT):
    idb = consts["identb"]
    H, D, Mf = 16, 64, 256
    NCH = S // 128

    qb = load_bias_cols(p, nc, bufs, qb_d, E, "l0_qb")
    kb = load_bias_cols(p, nc, bufs, kb_d, E, "l0_kb")
    vb = load_bias_cols(p, nc, bufs, vb_d, E, "l0_vb")

    projT = bufs.sb("l0_projT", (128, 256), BF16)
    pj = bufs.sb("l0_pj", (128, 64), F32)
    pps = bufs.ps("l0_psmisc", (128, 512), F32)[0:64, 0:128]
    for half in range(2):
        p.dma(pj[:, :], proj_d[half * 128:(half + 1) * 128, :])
        p.op("tensor", lambda e: e.transpose(pps, pj[:, :],
                                             consts["identf"][:, :]))
        p.op("scalar", lambda e, half=half: e.activation(
            projT[0:64, half * 128:(half + 1) * 128], pps, AF.Copy,
            scale=0.125))
        p.op("scalar", lambda e, half=half: e.activation(
            projT[64:128, half * 128:(half + 1) * 128], pps, AF.Copy,
            scale=0.125))

    xnb = [bufs.sb(f"l0_xn{k}", (128, S), BF16) for k in range(8)]
    wsb = bufs.sb("l0_w", (128, 3 * E), BF16)
    qkvT = [bufs.sb(f"l0_qkvT{j}", (128, S), BF16) for j in range(6)]
    corrK = [bufs.sb(f"l0_corrK{h}", (1, S), F32) for h in range(4)]
    sqt = bufs.sb("l0_sq", (128, 512), F32)
    qp = [bufs.sb(f"l0_qp{j}", (128, S), BF16) for j in range(2)]
    kp = [bufs.sb(f"l0_kp{j}", (128, S), BF16) for j in range(2)]
    vtok = bufs.sb("l0_vtok", (128, NCH * 64), BF16)
    kptok = bufs.sb("l0_kptok", (128, 256), BF16)
    smT = bufs.sb("l0_smT", (128, 128), BF16)
    kvsb = bufs.sb("l0_kvsb", (128, 2 * 65), BF16)
    numsb = bufs.sb("l0_numsb", (128, 65), F32)
    att = bufs.sb("l0_att", (128, 64), BF16)
    attT = bufs.sb("l0_attT", (64, 128), BF16)
    onesb = bufs.sb("l0_onesb", (128, 1), BF16)
    triub = bufs.sb("l0_triub", (128, 128), F32)
    p.op("vector", lambda e: e.memset(onesb[:, :], 1.0))
    p.op("vector", lambda e: e.tensor_copy(triub[:, :], consts["triu_mult"][:, :]))

    ps_mm = bufs.ps("l0_psmm", (128, 512), F32)
    ps_map = bufs.ps("l0_psmap", (128, 512), F32)
    ps_misc = bufs.ps("l0_psmisc", (128, 512), F32)
    ps_corr = ps_misc[0:1, 0:512]
    ps_sqk = ps_misc[:, 0:128]
    ps_nd = bufs.ps("l0_psnd", (128, 65), F32)
    ps_kv = [bufs.ps(f"l0_pskv{j}", (128, 65), F32) for j in range(2)]
    ps_trbf = bufs.ps("l0_pstrbf", (128, 128), BF16)
    ps_trb = ps_trbf[:, 0:128]
    ps_tr64 = ps_trbf[:, 0:64]

    for b in range(BL):
        for k in range(8):
            p.dma(xnb[k][:, :], xnT[k * 128:(k + 1) * 128, b * S:(b + 1) * S])
        for hg in range(4):
            for ft in range(2):
                frow = hg * 256 + ft * 128
                for wi, (wd, bias) in enumerate(((qw, qb), (kw, kb), (vw, vb))):
                    for kt in range(8):
                        p.dma(wsb[:, wi * E + kt * 128:wi * E + (kt + 1) * 128],
                              wd[kt * 128:(kt + 1) * 128, frow:frow + 128])
                for wi, (wd, bias) in enumerate(((qw, qb), (kw, kb), (vw, vb))):
                    bcol = bias[:, frow // 128:frow // 128 + 1]
                    for c in range(4):
                        for kt in range(8):
                            p.op("tensor", lambda e, wi=wi, kt=kt, c=c: e.matmul(
                                ps_mm[:, :],
                                wsb[:, wi * E + kt * 128:wi * E + (kt + 1) * 128],
                                xnb[kt][:, c * 512:(c + 1) * 512],
                                start=(kt == 0), stop=(kt == 7)))
                        dst = qkvT[wi * 2 + ft]
                        p.op("vector", lambda e, dst=dst, c=c, bcol=bcol: e.tensor_scalar_add(
                            dst[:, c * 512:(c + 1) * 512], ps_mm[:, :], bcol))
                        if wi == 1:
                            p.op("scalar", lambda e, bcol=bcol: e.activation(
                                sqt[:, :], ps_mm[:, :], AF.Square, bias=bcol))
                            for hh in range(2):
                                hloc = ft * 2 + hh
                                p.op("tensor", lambda e, hh=hh: e.matmul(
                                    ps_corr[:, :], consts["cols"][:, 3 + hh:4 + hh],
                                    sqt[:, :], start=True, stop=True))
                                p.op("vector", lambda e, hloc=hloc, c=c: e.tensor_copy(
                                    corrK[hloc][:, c * 512:(c + 1) * 512],
                                    ps_corr[:, :]))
            for h in range(4):
                ft, base = h // 2, (h % 2) * 64
                hglob = hg * 4 + h
                qh = qkvT[ft]
                kh = qkvT[2 + ft]
                vh = qkvT[4 + ft]
                for dst, srcT, corr in ((qp, qh, None), (kp, kh, corrK[h])):
                    for mt in range(2):
                        for c in range(4):
                            p.op("tensor", lambda e, mt=mt, c=c, srcT=srcT, base=base: e.matmul(
                                ps_map[:, :],
                                projT[base:base + 64, mt * 128:(mt + 1) * 128],
                                srcT[base:base + 64, c * 512:(c + 1) * 512],
                                start=True, stop=(corr is None)))
                            if corr is not None:
                                p.op("tensor", lambda e, c=c, corr=corr: e.matmul(
                                    ps_map[:, :], consts["negone_row"][:, :],
                                    corr[:, c * 512:(c + 1) * 512],
                                    start=False, stop=True))
                            p.op("scalar", lambda e, dst=dst, mt=mt, c=c: e.activation(
                                dst[mt][:, c * 512:(c + 1) * 512], ps_map[:, :],
                                AF.Exp))
                for c in range(NCH):
                    p.op("tensor", lambda e, c=c, vh=vh, base=base: e.transpose(
                        ps_tr64[:, :], vh[base:base + 64, c * 128:(c + 1) * 128],
                        consts["identb64"][base:base + 64, 0:64]))
                    p.op("vector", lambda e, c=c: e.tensor_copy(
                        vtok[:, c * 64:(c + 1) * 64], ps_tr64[:, :]))
                for c in range(NCH):
                    cs = slice(c * 128, (c + 1) * 128)
                    for mt in range(2):
                        p.op("tensor", lambda e, mt=mt, cs=cs: e.matmul(
                            ps_sqk[:, :], kp[mt][:, cs], qp[mt][:, cs],
                            start=(mt == 0), stop=(mt == 1)))
                    p.op("vector", lambda e: e.tensor_tensor(
                        smT[:, :], ps_sqk[:, :], triub[:, :], ALU.mult))
                    p.op("tensor", lambda e, c=c: e.matmul(
                        ps_nd[:, 0:64], smT[:, :], vtok[:, c * 64:(c + 1) * 64],
                        start=True, stop=False, skip_group_check=True))
                    p.op("tensor", lambda e: e.matmul(
                        ps_nd[:, 64:65], smT[:, :], onesb[:, :],
                        start=False, stop=False, skip_group_check=True))
                    for mt in range(2):
                        if c > 0:
                            p.op("tensor", lambda e, mt=mt, cs=cs: e.matmul(
                                ps_nd[:, 0:65], qp[mt][:, cs],
                                kvsb[:, mt * 65:(mt + 1) * 65],
                                start=False, stop=(mt == 1),
                                skip_group_check=True))
                    for mt in range(2):
                        p.op("tensor", lambda e, mt=mt, cs=cs: e.transpose(
                            ps_trb[:, :], kp[mt][:, cs], idb[:, :]))
                        p.op("vector", lambda e, mt=mt: e.tensor_copy(
                            kptok[:, mt * 128:(mt + 1) * 128], ps_trb[:, :]))
                    for mt in range(2):
                        p.op("tensor", lambda e, mt=mt, c=c: e.matmul(
                            ps_kv[mt][:, 0:64], kptok[:, mt * 128:(mt + 1) * 128],
                            vtok[:, c * 64:(c + 1) * 64],
                            start=(c == 0), stop=False, skip_group_check=True))
                        p.op("tensor", lambda e, mt=mt, c=c: e.matmul(
                            ps_kv[mt][:, 64:65], kptok[:, mt * 128:(mt + 1) * 128],
                            onesb[:, :],
                            start=False, stop=(c == NCH - 1),
                            skip_group_check=True))
                        p.op("vector", lambda e, mt=mt: e.tensor_copy(
                            kvsb[:, mt * 65:(mt + 1) * 65], ps_kv[mt][:, :]))
                    p.op("vector", lambda e: e.reciprocal(numsb[:, 64:65],
                                                          ps_nd[:, 64:65]))
                    p.op("vector", lambda e: e.tensor_scalar(
                        att[:, :], ps_nd[:, 0:64], numsb[:, 64:65], None,
                        ALU.mult))
                    p.op("tensor", lambda e: e.transpose(ps_trb[0:64, 0:128],
                                                         att[:, :], idb[:, :]))
                    p.op("vector", lambda e: e.tensor_copy(attT[:, :],
                                                           ps_trb[0:64, 0:128]))
                    p.dma(attnT[hglob * 64:(hglob + 1) * 64,
                                b * S + c * 128:b * S + (c + 1) * 128],
                          attT[:, :])




def emit_l0_post(p, nc, bufs, consts, attnT, outw, outb_d, xn_dram,
                 gamma_d, beta_d, a0_dram):
    wt = [bufs.sb(f"lp_w{k}", (128, E), BF16) for k in range(8)]
    for kt in range(8):
        p.dma(wt[kt][:, :], outw[kt * 128:(kt + 1) * 128, :])
    brow = bufs.sb("lp_brow", (1, E), F32)
    p.dma(brow[:, :], outb_d[0:E].rearrange("(one n) -> one n", one=1))
    grow = bufs.sb("lp_grow", (1, E), F32)
    berow = bufs.sb("lp_berow", (1, E), F32)
    p.dma(grow[:, :], gamma_d[0:E].rearrange("(one n) -> one n", one=1))
    p.dma(berow[:, :], beta_d[0:E].rearrange("(one n) -> one n", one=1))
    ps = [bufs.ps(f"lp_ps{k}", (128, 512), F32) for k in range(2)]
    gb = bufs.sb("lp_gb", (128, E), F32)
    bb = bufs.sb("lp_bb", (128, E), F32)
    for n in range(2):
        p.op("tensor", lambda e, n=n: e.matmul(
            ps[0][:, :], consts["one_row"][:, :], grow[:, n * 512:(n + 1) * 512],
            start=True, stop=True))
        p.op("vector", lambda e, n=n: e.tensor_copy(gb[:, n * 512:(n + 1) * 512],
                                                    ps[0][:, :]))
        p.op("tensor", lambda e, n=n: e.matmul(
            ps[0][:, :], consts["one_row"][:, :], berow[:, n * 512:(n + 1) * 512],
            start=True, stop=True))
        p.op("vector", lambda e, n=n: e.tensor_copy(bb[:, n * 512:(n + 1) * 512],
                                                    ps[0][:, :]))
    at = bufs.sb("lp_at", (128, E), BF16)
    res = bufs.sb("lp_res", (128, E), F32)
    for i in range(NT):
        for kt in range(8):
            p.dma(at[:, kt * 128:(kt + 1) * 128],
                  attnT[kt * 128:(kt + 1) * 128, i * 128:(i + 1) * 128])
        for n in range(2):
            for kt in range(8):
                p.op("tensor", lambda e, n=n, kt=kt: e.matmul(
                    ps[n][:, :], at[:, kt * 128:(kt + 1) * 128],
                    wt[kt][:, n * 512:(n + 1) * 512],
                    start=(kt == 0), stop=False, skip_group_check=True))
            p.op("tensor", lambda e, n=n: e.matmul(
                ps[n][:, :], consts["one_row"][:, :],
                brow[:, n * 512:(n + 1) * 512],
                start=False, stop=True, skip_group_check=True))
        p.dma(res[:, :], xn_dram[i * 128:(i + 1) * 128, :])
        for n in range(2):
            p.op("vector", lambda e, n=n: e.tensor_tensor(
                res[:, n * 512:(n + 1) * 512], ps[n][:, :],
                res[:, n * 512:(n + 1) * 512], ALU.add))
        emit_layernorm_tile(p, nc, bufs, res, res, 1e-6, pref="lp_")
        p.op("vector", lambda e: e.tensor_tensor(res[:, :], res[:, :], gb[:, :],
                                                 ALU.mult))
        p.op("vector", lambda e: e.tensor_tensor(res[:, :], res[:, :], bb[:, :],
                                                 ALU.add))
        p.dma(a0_dram[i * 128:(i + 1) * 128, :], res[:, :])


def np_l0_post(attn, xn, outw, outb, gamma, beta):
    main = attn.astype(np.float64) @ outw.astype(np.float64) + outb
    res = main + xn
    mu = res.mean(-1, keepdims=True)
    var = ((res - mu) ** 2).mean(-1, keepdims=True)
    return (gamma * (res - mu) / np.sqrt(var + 1e-6) + beta).astype(np.float32)


def emit_combine_ln(p, nc, bufs, consts, srcs, x_out, xn_out, xnT_out,
                    eps=1e-5):
    a = bufs.sb("cb_x", (128, E), F32)
    t = bufs.sb("cb_t", (128, E), F32)
    xb = bufs.sb("cb_xb", (128, E), BF16)
    for i in range(NT):
        p.dma(a[:, :], srcs[0][i * 128:(i + 1) * 128, :])
        for s in srcs[1:]:
            p.dma(t[:, :], s[i * 128:(i + 1) * 128, :])
            p.op("vector", lambda e: e.tensor_add(a[:, :], a[:, :], t[:, :]))
        if x_out is not None:
            p.dma(x_out[i * 128:(i + 1) * 128, :], a[:, :])
        emit_layernorm_tile(p, nc, bufs, a, a, eps, pref="cb_")
        if xn_out is not None:
            p.dma(xn_out[i * 128:(i + 1) * 128, :], a[:, :])
        if xnT_out is not None:
            p.op("vector", lambda e: e.tensor_copy(xb[:, :], a[:, :]))
            emit_transpose_store(p, nc, bufs, xb, xnT_out, 0, i * 128, 8,
                                 consts["identb"], pref="cb_")


def emit_sum_final(p, nc, bufs, srcs, out_dram):
    a = bufs.sb("fi_x", (128, E), F32)
    t = bufs.sb("fi_t", (128, E), F32)
    for i in range(NT):
        p.dma(a[:, :], srcs[0][i * 128:(i + 1) * 128, :])
        for s in srcs[1:]:
            p.dma(t[:, :], s[i * 128:(i + 1) * 128, :])
            p.op("vector", lambda e: e.tensor_add(a[:, :], a[:, :], t[:, :]))
        p.dma(out_dram[i * 128:(i + 1) * 128, :], a[:, :])


def emit_l1_attn(p, nc, bufs, consts, xnT, WinT, convw_d, convb_d, WxT,
                 WdtT, bdt_d, Alog_d, wgate_d, zT_dram, yw_dram, dbg=None):
    DI = 2048
    cb = load_bias_cols(p, nc, bufs, convb_d, DI, "m_cb")
    bdt = load_col(p, nc, bufs, bdt_d, 0, 128, "m_bdt")
    alog = load_col(p, nc, bufs, Alog_d, 0, 128, "m_alog")
    wg = bufs.sb("m_wg", (1, 1), F32)
    p.dma(wg[:, :], wgate_d[0:1].rearrange("(a b) -> a b", a=1))
    acol = bufs.sb("m_acol", (128, 1), F32)
    p.op("scalar", lambda e: e.activation(acol[:, :], alog[:, :], AF.Exp))
    p.op("vector", lambda e: e.tensor_scalar_mul(acol[:, :], acol[:, :], -1.0))

    xnb = [bufs.sb(f"m_xn{k}", (128, S), BF16) for k in range(8)]
    wsb = bufs.sb("m_w", (128, E), BF16)
    cw = bufs.sb("m_cw", (128, 4), F32)
    cbuf = bufs.sb("m_cbuf", (128, 4 + S), F32)
    acc = [bufs.sb(f"m_acc{k}", (128, S), F32) for k in range(2)]
    xcT = [bufs.sb(f"m_xc{k}", (128, S), BF16) for k in range(16)]
    zbf = bufs.sb("m_zbf", (128, 512), BF16)
    wx = bufs.sb("m_wx", (128, 2048), BF16)
    dtT = bufs.sb("m_dtT", (64, S), BF16)
    BmT = bufs.sb("m_BmT", (128, S), F32)
    CmT = bufs.sb("m_CmT", (128, S), F32)
    delta = bufs.sb("m_delta", (128, S), F32)
    u1 = bufs.sb("m_u1", (128, S), F32)
    u2 = bufs.sb("m_u2", (128, S), F32)
    hT = bufs.sb("m_hT", (128, S), F32)
    ywrow = bufs.sb("m_ywrow", (1, S), F32)
    wdt = bufs.sb("m_wdt", (64, 128), BF16)
    p.dma(wdt[:, :], WdtT[:, :])
    ps = bufs.ps("m_ps", (128, 512), F32)
    ps2 = bufs.ps("m_ps2", (128, 512), F32)
    psy = bufs.ps("m_psy", (1, 512), F32)

    for b in range(BL):
        for k in range(8):
            p.dma(xnb[k][:, :], xnT[k * 128:(k + 1) * 128, b * S:(b + 1) * S])
        for dt in range(16):
            for kt in range(8):
                p.dma(wsb[:, kt * 128:(kt + 1) * 128],
                      WinT[kt * 128:(kt + 1) * 128, dt * 128:(dt + 1) * 128])
            p.dma(cw[:, :], convw_d[dt * 128:(dt + 1) * 128, :])
            p.op("vector", lambda e: e.memset(cbuf[:, 0:4], 0.0))
            for c in range(4):
                for kt in range(8):
                    p.op("tensor", lambda e, kt=kt, c=c: e.matmul(
                        ps[:, :], wsb[:, kt * 128:(kt + 1) * 128],
                        xnb[kt][:, c * 512:(c + 1) * 512],
                        start=(kt == 0), stop=(kt == 7)))
                p.op("scalar", lambda e: e.activation(zbf[:, :], ps[:, :], AF.Silu))
                p.dma(zT_dram[dt * 128:(dt + 1) * 128,
                              b * S + c * 512:b * S + (c + 1) * 512], zbf[:, :])
                p.op("vector", lambda e, c=c: e.tensor_copy(
                    cbuf[:, 3 + c * 512:3 + (c + 1) * 512], ps[:, :]))
            p.op("vector", lambda e, dt=dt: e.tensor_scalar(
                acc[0][:, :], cbuf[:, 0:S], cw[:, 0:1],
                cb[:, dt:dt + 1], ALU.mult, ALU.add))
            for j in (1, 2, 3):
                src, dst = acc[(j + 1) % 2], acc[j % 2]
                p.op("vector", lambda e, j=j, src=src, dst=dst: e.scalar_tensor_tensor(
                    dst[:, :], cbuf[:, j:j + S], cw[:, j:j + 1], src[:, :],
                    ALU.mult, ALU.add))
            p.op("vector", lambda e, dt=dt: e.tensor_copy(xcT[dt][:, :],
                                                          acc[1][:, :]))
        for (mlo, msz, dst, dtyp) in ((0, 64, dtT, BF16), (64, 128, BmT, F32),
                                      (192, 128, CmT, F32)):
            for kt in range(16):
                p.dma(wx[0:128, kt * 128:kt * 128 + msz],
                      WxT[kt * 128:(kt + 1) * 128, mlo:mlo + msz])
            for c in range(4):
                for kt in range(16):
                    p.op("tensor", lambda e, kt=kt, c=c, msz=msz: e.matmul(
                        ps[0:msz, :], wx[:, kt * 128:kt * 128 + msz],
                        xcT[kt][:, c * 512:(c + 1) * 512],
                        start=(kt == 0), stop=(kt == 15)))
                p.op("vector", lambda e, dst=dst, c=c, msz=msz: e.tensor_copy(
                    dst[0:msz, c * 512:(c + 1) * 512], ps[0:msz, :]))
        for c in range(4):
            p.op("tensor", lambda e, c=c: e.matmul(
                ps2[:, :], wdt[:, :], dtT[:, c * 512:(c + 1) * 512],
                start=True, stop=True))
            p.op("vector", lambda e, c=c: e.tensor_scalar(
                u1[:, c * 512:(c + 1) * 512], ps2[:, :], bdt[:, 0:1], 0.0,
                ALU.add, ALU.max))
            p.op("scalar", lambda e, c=c: e.activation(
                u2[:, c * 512:(c + 1) * 512], ps2[:, :], AF.Abs, bias=bdt[:, 0:1]))
        p.op("scalar", lambda e: e.activation(u2[:, :], u2[:, :], AF.Exp, scale=-1.0))
        p.op("scalar", lambda e: e.activation(u2[:, :], u2[:, :], AF.Ln, bias=1.0))
        p.op("vector", lambda e: e.tensor_add(delta[:, :], u1[:, :], u2[:, :]))
        p.op("scalar", lambda e: e.activation(u1[:, :], delta[:, :], AF.Exp,
                                              scale=acol[:, 0:1]))
        p.op("vector", lambda e: e.tensor_mul(u2[:, :], delta[:, :], BmT[:, :]))
        p.op("vector", lambda e: e.tensor_tensor_scan(hT[:, :], u1[:, :],
                                                      u2[:, :], 0.0,
                                                      ALU.mult, ALU.add))
        p.op("vector", lambda e: e.tensor_mul(hT[:, :], hT[:, :], CmT[:, :]))
        for c in range(4):
            p.op("tensor", lambda e, c=c: e.matmul(
                psy[:, :], consts["cols"][:, 0:1], hT[:, c * 512:(c + 1) * 512],
                start=True, stop=True))
            p.op("vector", lambda e, c=c: e.tensor_scalar(
                ywrow[:, c * 512:(c + 1) * 512], psy[:, :], wg[0:1, 0:1], None,
                ALU.mult))
        p.dma(yw_dram[b:b + 1, :], ywrow[:, :])
        if dbg is not None and b == 0:
            for nm, tile in (("xc0", xcT[0]), ("dtT", dtT), ("BmT", BmT),
                             ("CmT", CmT), ("delta", delta), ("Abar", u1),
                             ("hT", hT)):
                if nm in dbg:
                    p.dma(dbg[nm][:, :], tile[:, :] if nm != "dtT"
                          else tile[0:64, :])


def np_l1_attn(xn, W_in, conv_w, conv_b, W_x, W_dt, b_dt, A_log, w_gate):
    DI = 2048
    z = np.zeros((T, DI), dtype=np.float32)
    yw = np.zeros((BL, S), dtype=np.float32)
    for b in range(BL):
        x = xn[b * S:(b + 1) * S].astype(np.float64)
        xi = x @ W_in.T.astype(np.float64)
        xc = np.zeros_like(xi)
        for j in range(4):
            lo = 3 - j
            xc[lo:, :] += xi[:S - lo, :] * conv_w[:, j]
        xc += conv_b
        ssm = xc @ W_x.T.astype(np.float64)
        dt, Bm, Cm = ssm[:, 0:64], ssm[:, 64:192], ssm[:, 192:320]
        da = dt @ W_dt.T.astype(np.float64) + b_dt
        delta = np.where(da > 30, da, np.log1p(np.exp(np.minimum(da, 30.0))))
        A = -np.exp(A_log.astype(np.float64))
        Ab = np.exp(delta * A)
        Bb = delta * Bm
        h = np.zeros((S, 128))
        hp = np.zeros(128)
        for t in range(S):
            hp = Ab[t] * hp + Bb[t]
            h[t] = hp
        y = (Cm * h).sum(-1)
        yw[b] = (y * w_gate[0]).astype(np.float32)
        z[b * S:(b + 1) * S] = (xi / (1 + np.exp(-xi))).astype(np.float32)
    return z, yw


def emit_l1_attn_out(p, nc, bufs, consts, zT_dram, yw_dram, WoutT, a1_dram):
    DI = 2048
    ywt = bufs.sb("ao_ywt", (64, 128), F32)
    ywc = bufs.sb("ao_ywc", (128, 64), F32)
    psy = bufs.ps("ao_psy", (128, 128), F32)
    p.dma(ywt[:, :], yw_dram.rearrange("b (ch p) -> (b ch) p", p=128))
    p.op("tensor", lambda e: e.transpose(psy[0:128, 0:64], ywt[:, :],
                                         consts["identf"][0:64, 0:64]))
    p.op("vector", lambda e: e.tensor_copy(ywc[:, :], psy[0:128, 0:64]))
    wt = [bufs.sb(f"ao_w{k}", (128, E), BF16) for k in range(16)]
    for kt in range(16):
        p.dma(wt[kt][:, :], WoutT[kt * 128:(kt + 1) * 128, :])
    zt = bufs.sb("ao_zt", (128, DI), BF16)
    res = bufs.sb("ao_res", (128, E), F32)
    ps = [bufs.ps(f"ao_ps{n}", (128, 512), F32) for n in range(2)]
    for i in range(NT):
        for kt in range(16):
            p.dma(zt[:, kt * 128:(kt + 1) * 128],
                  zT_dram[kt * 128:(kt + 1) * 128, i * 128:(i + 1) * 128])
        for n in range(2):
            for kt in range(16):
                p.op("tensor", lambda e, n=n, kt=kt: e.matmul(
                    ps[n][:, :], zt[:, kt * 128:(kt + 1) * 128],
                    wt[kt][:, n * 512:(n + 1) * 512],
                    start=(kt == 0), stop=(kt == 15)))
            p.op("vector", lambda e, n=n, i=i: e.tensor_scalar(
                res[:, n * 512:(n + 1) * 512], ps[n][:, :],
                ywc[:, i:i + 1], None, ALU.mult))
        p.dma(a1_dram[i * 128:(i + 1) * 128, :], res[:, :])



def load_bias_cols_partial(p, nc, bufs, dram_vec, lo, n, name):
    nf = (n + 127) // 128
    t = bufs.sb(name, (128, nf), F32)
    full = (n // 128) * 128
    src = dram_vec[lo:lo + full].rearrange("(f p) -> p f", p=128)
    p.dma(t[:, 0:n // 128], src, allow_slow_non_contiguous=True)
    if n % 128:
        src2 = dram_vec[lo + full:lo + n].rearrange("(p one) -> p one", one=1)
        p.dma(t[0:n % 128, nf - 1:nf], src2, allow_slow_non_contiguous=True)
    return t


def emit_l1_ffn(p, nc, bufs, consts, xn1_dram, wqkv, qs_d, ks_d, wo,
                lng_d, lnb_d, w1, b1_d, w2, b2_d, xf_dram, dbg=None):
    HEADS, DH, INNER, HID = 8, 64, 512, 2730
    W = 512
    NW = S // W
    NZ = (HID + 127) // 128
    idb = consts["identb"]
    hT_dram = nc.dram_tensor("ffn_hT", [E, T], BF16, kind="Internal").ap()

    h_t = bufs.sb("f_h", (128, E), F32)
    hb = bufs.sb("f_hb", (128, E), BF16)
    qsr = bufs.sb("f_qsr", (1, 2 * DH), F32)
    ksr = bufs.sb("f_ksr", (1, 2 * DH), F32)
    p.dma(qsr[:, :], qs_d.rearrange("l d -> (l d)").rearrange("(a n) -> a n", a=1))
    p.dma(ksr[:, :], ks_d.rearrange("l d -> (l d)").rearrange("(a n) -> a n", a=1))
    p.op("vector", lambda e: e.tensor_scalar_mul(qsr[:, :], qsr[:, :], 8.0))
    affr = bufs.sb("f_affr", (1, 3 * E), F32)
    b1a = [load_bias_cols_partial(p, nc, bufs, b1_d, l * 2 * HID, HID, f"f_b1a{l}")
           for l in range(2)]
    b1g = [load_bias_cols_partial(p, nc, bufs, b1_d, l * 2 * HID + HID, HID,
                                  f"f_b1g{l}") for l in range(2)]

    wq_sb = bufs.sb("f_wq", (128, E), BF16)
    hTc = [bufs.sb(f"f_hTc{k}", (128, 512), BF16) for k in range(8)]
    qkvT = [bufs.sb(f"f_qkvT{j}", (128, S), BF16) for j in range(12)]
    qf = bufs.sb("f_qf", (128, 512), F32)
    sq = bufs.sb("f_sq", (128, 512), F32)
    n2r = [bufs.sb(f"f_n2r{h}", (1, 512), F32) for h in range(2)]
    vtok = bufs.sb("f_vtok", (128, 8 * 64), BF16)
    pa = bufs.sb("f_pa", (128, 512), BF16)
    pb = bufs.sb("f_pb", (128, 128), BF16)
    pt = bufs.sb("f_pt", (128, 5 * 128), BF16)
    oTw = [bufs.sb(f"f_oTw{j}", (128, 512), BF16) for j in range(4)]
    den = bufs.sb("f_den", (128, 4), F32)
    denr = bufs.sb("f_denr", (1, 128), F32)
    bca = bufs.sb("f_bca", (128, 128), F32)
    wo_sb = [bufs.sb(f"f_wo{k}", (128, E), BF16) for k in range(4)]
    xres = bufs.sb("f_xres", (128, E), F32)
    zt = [bufs.sb(f"f_z{m}", (128, 512), BF16) for m in range(NZ)]
    hnT = [bufs.sb(f"f_hnT{k}", (128, 512), BF16) for k in range(8)]
    w1sb = bufs.sb("f_w1sb", (128, 1024), BF16)
    w2sb = bufs.sb("f_w2sb", (128, 1024), BF16)
    gg = bufs.sb("f_gg", (128, 512), F32)
    gbc = bufs.sb("f_gbc", (128, 2 * E), F32)

    dbgS_sb = bufs.sb("f_dbgS", (128, 128), F32) if dbg is not None else None
    ps1 = bufs.ps("f_ps1", (128, 512), F32)
    ps2 = bufs.ps("f_ps2", (128, 512), F32)
    psS = bufs.ps("f_psS", (128, 512), F32)
    psB = bufs.ps("f_psB", (128, 128), F32)
    pst = bufs.ps("f_pst", (128, 128), BF16)
    psmini = bufs.ps("f_psmini", (128, 512), F32)
    psav = bufs.ps("f_psav", (128, 128), F32)

    for l in range(2):
        src = xn1_dram if l == 0 else xf_dram
        for i in range(NT):
            p.dma(h_t[:, 0:512], src[i * 128:(i + 1) * 128, 0:512])
            if i % (S // 128) == 0:
                p.op("vector", lambda e: e.memset(h_t[0:1, 512:1024], 0.0))
                p.dma(h_t[1:128, 512:1024],
                      src[i * 128:(i + 1) * 128 - 1, 512:1024])
            else:
                p.dma(h_t[:, 512:1024],
                      src[i * 128 - 1:(i + 1) * 128 - 1, 512:1024])
            p.op("vector", lambda e: e.tensor_copy(hb[:, :], h_t[:, :]))
            emit_transpose_store(p, nc, bufs, hb, hT_dram, 0, i * 128, 8, idb,
                                 pref="f_")
        for b in range(BL):
            for c in range(4):
                for k in range(8):
                    p.dma(hTc[k][:, :], hT_dram[k * 128:(k + 1) * 128,
                                                b * S + c * 512:b * S + (c + 1) * 512])
                for fo in range(12):
                    for kt in range(8):
                        p.dma(wq_sb[:, kt * 128:(kt + 1) * 128],
                              wqkv[l * E + kt * 128:l * E + (kt + 1) * 128,
                                   fo * 128:(fo + 1) * 128])
                    for kt in range(8):
                        p.op("tensor", lambda e, kt=kt: e.matmul(
                            ps1[:, :], wq_sb[:, kt * 128:(kt + 1) * 128],
                            hTc[kt][:, :], start=(kt == 0), stop=(kt == 7)))
                    if fo >= 8:
                        p.op("vector", lambda e, fo=fo, c=c: e.tensor_copy(
                            qkvT[fo][:, c * 512:(c + 1) * 512], ps1[:, :]))
                        continue
                    p.op("vector", lambda e: e.tensor_copy(qf[:, :], ps1[:, :]))
                    isq = fo < 4
                    gr = qsr if isq else ksr
                    rA = consts["rot_Aq"] if isq else consts["rot_Ak"]
                    rB = consts["rot_Bq"] if isq else consts["rot_Bk"]
                    p.op("scalar", lambda e: e.activation(sq[:, :], qf[:, :],
                                                          AF.Square))
                    for hh in range(2):
                        p.op("tensor", lambda e, hh=hh: e.matmul(
                            psmini[0:1, :], consts["cols"][:, 1 + hh:2 + hh],
                            sq[:, :], start=True, stop=True))
                        p.op("vector", lambda e, hh=hh: e.tensor_copy(
                            n2r[hh][:, :], psmini[0:1, :]))
                        p.op("scalar", lambda e, hh=hh: e.activation(
                            n2r[hh][:, :], n2r[hh][:, :], AF.Sqrt))
                        p.op("vector", lambda e, hh=hh: e.tensor_scalar_max(
                            n2r[hh][:, :], n2r[hh][:, :], 1e-12))
                        p.op("vector", lambda e, hh=hh: e.reciprocal(
                            n2r[hh][:, :], n2r[hh][:, :]))
                        p.op("tensor", lambda e, hh=hh, gr=gr, l=l: e.matmul(
                            ps2[hh * 64:hh * 64 + 64, :],
                            gr[0:1, l * 64:(l + 1) * 64], n2r[hh][:, :],
                            start=True, stop=True))
                    cs = slice(c * 512, (c + 1) * 512)
                    p.op("vector", lambda e: e.tensor_tensor(
                        ps1[:, :], qf[:, :], ps2[:, :], ALU.mult))
                    p.op("vector", lambda e, cs=cs, rA=rA: e.tensor_tensor(
                        sq[:, :], ps1[:, :], rA[:, cs], ALU.mult))
                    for qr in range(4):
                        sw = (qr + 1) if qr % 2 == 0 else (qr - 1)
                        p.op("vector", lambda e, qr=qr, sw=sw, cs=cs, rB=rB: e.tensor_tensor(
                            qf[qr * 32:(qr + 1) * 32, :],
                            ps1[sw * 32:(sw + 1) * 32, :],
                            rB[qr * 32:(qr + 1) * 32, cs], ALU.mult))
                    p.op("vector", lambda e, fo=fo, cs=cs: e.tensor_tensor(
                        qkvT[fo][:, cs], sq[:, :], qf[:, :], ALU.add))
            for w in range(NW):
                jt0 = max(0, (w - 1) * 4)
                njt = 4 if w == 0 else 8
                for j in range(4):
                    p.op("vector", lambda e, j=j: e.memset(oTw[j][:, :], 0.0))
                for h in range(HEADS):
                    ft, base = h // 2, (h % 2) * 64
                    qT = qkvT[ft]
                    kT = qkvT[4 + ft]
                    vT = qkvT[8 + ft]
                    for s_ in range(njt):
                        jt = jt0 + s_
                        p.op("tensor", lambda e, jt=jt, vT=vT, base=base: e.transpose(
                            pst[0:128, 0:64],
                            vT[base:base + 64, jt * 128:(jt + 1) * 128],
                            consts["identb64"][base:base + 64, 0:64]))
                        p.op("vector", lambda e, s_=s_: e.tensor_copy(
                            vtok[:, s_ * 64:(s_ + 1) * 64], pst[0:128, 0:64]))
                    for r in range(4):
                        q0 = w * W + r * 128
                        if w == 0:
                            kj0l = 0
                            nk = (r + 1) * 128
                        else:
                            kj0l = r * 128
                            nk = 640
                        kjg = jt0 * 128 + kj0l
                        nk1 = min(nk, 512)
                        p.op("tensor", lambda e, base=base, q0=q0, kjg=kjg, nk1=nk1, qT=qT, kT=kT: e.matmul(
                            psS[:, 0:nk1], qT[base:base + 64, q0:q0 + 128],
                            kT[base:base + 64, kjg:kjg + nk1],
                            start=True, stop=False, skip_group_check=True))
                        if nk > 512:
                            p.op("tensor", lambda e, base=base, q0=q0, kjg=kjg, qT=qT, kT=kT: e.matmul(
                                psB[:, :], qT[base:base + 64, q0:q0 + 128],
                                kT[base:base + 64, kjg + 512:kjg + 640],
                                start=True, stop=False, skip_group_check=True))
                        if w > 0:
                            p.op("tensor", lambda e: e.matmul(
                                psS[:, 0:128], idb[:, :], consts["maskLb"][:, :],
                                start=False, stop=True, skip_group_check=True))
                            p.op("tensor", lambda e: e.matmul(
                                psB[:, :], idb[:, :], consts["maskRb"][:, :],
                                start=False, stop=True, skip_group_check=True))
                        else:
                            p.op("tensor", lambda e, r=r: e.matmul(
                                psS[:, r * 128:(r + 1) * 128], idb[:, :],
                                consts["maskRb"][:, :],
                                start=False, stop=True, skip_group_check=True))
                        if (dbg is not None and "S00" in dbg and l == 0
                                and b == 0 and w == 0 and h == 0 and r == 0):
                            p.op("vector", lambda e: e.tensor_copy(
                                dbgS_sb[:, :], psS[:, 0:128]))
                            p.dma(dbg["S00"][:, :], dbgS_sb[:, :])
                        p.op("vector", lambda e, nk1=nk1: e.reduce_max(
                            den[:, 0:1], psS[:, 0:nk1], AX.X))
                        if nk > 512:
                            p.op("vector", lambda e: e.reduce_max(
                                den[:, 1:2], psB[:, :], AX.X))
                            p.op("vector", lambda e: e.tensor_max(
                                den[:, 0:1], den[:, 0:1], den[:, 1:2]))
                        p.op("vector", lambda e: e.tensor_scalar_mul(
                            den[:, 2:3], den[:, 0:1], -1.0))
                        p.op("scalar", lambda e, nk1=nk1: e.activation(
                            pa[:, 0:nk1], psS[:, 0:nk1], AF.Exp,
                            bias=den[:, 2:3], accum_out=den[:, 0:1]))
                        if nk > 512:
                            p.op("scalar", lambda e: e.activation(
                                pb[:, :], psB[:, :], AF.Exp,
                                bias=den[:, 2:3], accum_out=den[:, 1:2]))
                            p.op("vector", lambda e: e.tensor_add(
                                den[:, 0:1], den[:, 0:1], den[:, 1:2]))
                        p.op("vector", lambda e: e.reciprocal(
                            den[:, 3:4], den[:, 0:1]))
                        ntile = nk // 128
                        for s_ in range(ntile):
                            srcP = pa[:, (s_ % 4) * 128:(s_ % 4) * 128 + 128] \
                                if s_ < 4 else pb[:, 0:128]
                            p.op("tensor", lambda e, srcP=srcP: e.transpose(
                                pst[:, :], srcP, idb[:, :]))
                            p.op("vector", lambda e, s_=s_: e.tensor_copy(
                                pt[:, s_ * 128:(s_ + 1) * 128], pst[:, :]))
                        p.op("tensor", lambda e: e.transpose(
                            psmini[0:1, 0:128], den[:, 3:4],
                            consts["identf"][:, :]))
                        p.op("vector", lambda e: e.tensor_copy(
                            denr[:, :], psmini[0:1, 0:128]))
                        p.op("tensor", lambda e, base=base: e.matmul(
                            psmini[base:base + 64, 0:128],
                            consts["one_row"][0:1, 0:64], denr[:, :],
                            start=True, stop=True))
                        p.op("vector", lambda e, base=base: e.tensor_copy(
                            bca[base:base + 64, :],
                            psmini[base:base + 64, 0:128]))
                        for s_ in range(ntile):
                            sloc = (kj0l // 128) + s_ if w > 0 else s_
                            p.op("tensor", lambda e, s_=s_, sloc=sloc, base=base, ntile=ntile: e.matmul(
                                psav[base:base + 64, :],
                                vtok[:, sloc * 64:(sloc + 1) * 64],
                                pt[:, s_ * 128:(s_ + 1) * 128],
                                start=(s_ == 0), stop=(s_ == ntile - 1)))
                        p.op("vector", lambda e, ft=ft, base=base, r=r: e.tensor_tensor(
                            oTw[ft][base:base + 64, r * 128:(r + 1) * 128],
                            psav[base:base + 64, :], bca[base:base + 64, :],
                            ALU.mult))
                if w == 0 and b == 0:
                    for k in range(4):
                        p.dma(wo_sb[k][:, :],
                              wo[l * 512 + k * 128:l * 512 + (k + 1) * 128, :])
                for sc in range(4):
                    gi = b * (S // 128) + w * 4 + sc
                    p.dma(xres[:, :], src[gi * 128:(gi + 1) * 128, :])
                    for n in range(2):
                        for k in range(4):
                            p.op("tensor", lambda e, n=n, k=k, sc=sc: e.matmul(
                                ps1[:, :], oTw[k][:, sc * 128:(sc + 1) * 128],
                                wo_sb[k][:, n * 512:(n + 1) * 512],
                                start=(k == 0), stop=(k == 3)))
                        p.op("vector", lambda e, n=n: e.tensor_tensor(
                            xres[:, n * 512:(n + 1) * 512], ps1[:, :],
                            xres[:, n * 512:(n + 1) * 512], ALU.add))
                    p.dma(xf_dram[gi * 128:(gi + 1) * 128, :], xres[:, :])
        if dbg is not None and l == 0:
            for nm, tile in (("q0", qkvT[0]), ("k0", qkvT[4]), ("v0", qkvT[8])):
                if nm in dbg:
                    p.dma(dbg[nm][:, :], tile[:, :])
            if "xattn0" in dbg:
                for i in range(NT):
                    p.dma(dbg["xattn0"][i * 128:(i + 1) * 128, :],
                          xf_dram[i * 128:(i + 1) * 128, :])
        p.dma(affr[0:1, 0:E], lng_d[l:l + 1, :])
        p.dma(affr[0:1, E:2 * E], lnb_d[l:l + 1, :])
        p.dma(affr[0:1, 2 * E:3 * E], b2_d[l:l + 1, :])
        for n in range(2):
            p.op("tensor", lambda e, n=n: e.matmul(
                ps1[:, :], consts["one_row"][:, :],
                affr[0:1, n * 512:(n + 1) * 512], start=True, stop=True))
            p.op("vector", lambda e, n=n: e.tensor_copy(
                gbc[:, n * 512:(n + 1) * 512], ps1[:, :]))
            p.op("tensor", lambda e, n=n: e.matmul(
                ps1[:, :], consts["one_row"][:, :],
                affr[0:1, E + n * 512:E + (n + 1) * 512], start=True, stop=True))
            p.op("vector", lambda e, n=n: e.tensor_copy(
                gbc[:, E + n * 512:E + (n + 1) * 512], ps1[:, :]))
        for g in range(T // 512):
            for sc in range(4):
                gi = g * 4 + sc
                p.dma(h_t[:, 0:512], xf_dram[gi * 128:(gi + 1) * 128, 0:512])
                if gi % (S // 128) == 0:
                    p.op("vector", lambda e: e.memset(h_t[0:1, 512:1024], 0.0))
                    p.dma(h_t[1:128, 512:1024],
                          xf_dram[gi * 128:(gi + 1) * 128 - 1, 512:1024])
                else:
                    p.dma(h_t[:, 512:1024],
                          xf_dram[gi * 128 - 1:(gi + 1) * 128 - 1, 512:1024])
                emit_layernorm_tile(p, nc, bufs, h_t, h_t, 1e-5, pref="f_")
                p.op("vector", lambda e: e.tensor_tensor(
                    h_t[:, :], h_t[:, :], gbc[:, 0:E], ALU.mult))
                p.op("vector", lambda e: e.tensor_tensor(
                    h_t[:, :], h_t[:, :], gbc[:, E:2 * E], ALU.add))
                p.op("vector", lambda e: e.tensor_copy(hb[:, :], h_t[:, :]))
                for f in range(8):
                    p.op("tensor", lambda e, f=f: e.transpose(
                        pst[:, :], hb[:, f * 128:(f + 1) * 128], idb[:, :]))
                    p.op("vector", lambda e, f=f, sc=sc: e.tensor_copy(
                        hnT[f][:, sc * 128:(sc + 1) * 128], pst[:, :]))
            for m in range(NZ):
                mrows = min(128, HID - m * 128)
                for part, psx in ((0, ps1), (1, ps2)):
                    col0 = l * 2 * HID + part * HID + m * 128
                    for kt in range(8):
                        p.dma(w1sb[:, kt * 128:kt * 128 + mrows],
                              w1[kt * 128:(kt + 1) * 128, col0:col0 + mrows])
                        p.op("tensor", lambda e, kt=kt, mrows=mrows, psx=psx: e.matmul(
                            psx[0:mrows, :], w1sb[:, kt * 128:kt * 128 + mrows],
                            hnT[kt][:, :], start=(kt == 0), stop=(kt == 7)))
                p.op("scalar", lambda e, m=m, mrows=mrows, l=l: e.activation(
                    gg[0:mrows, :], ps2[0:mrows, :], AF.Gelu,
                    bias=b1g[l][0:mrows, m:m + 1]))
                p.op("vector", lambda e, m=m, mrows=mrows, l=l: e.scalar_tensor_tensor(
                    zt[m][0:mrows, :], ps1[0:mrows, :], b1a[l][0:mrows, m:m + 1],
                    gg[0:mrows, :], ALU.add, ALU.mult))
            for sc in range(4):
                gi = g * 4 + sc
                p.dma(xres[:, :], xf_dram[gi * 128:(gi + 1) * 128, :])
                for n in range(2):
                    for m in range(NZ):
                        mrows = min(128, HID - m * 128)
                        p.dma(w2sb[0:mrows, n * 512:(n + 1) * 512],
                              w2[l * HID + m * 128:l * HID + m * 128 + mrows,
                                 n * 512:(n + 1) * 512])
                        p.op("tensor", lambda e, m=m, mrows=mrows, sc=sc, n=n: e.matmul(
                            ps1[:, :], zt[m][0:mrows, sc * 128:(sc + 1) * 128],
                            w2sb[0:mrows, n * 512:(n + 1) * 512],
                            start=(m == 0), stop=False, skip_group_check=True))
                    p.op("tensor", lambda e, n=n: e.matmul(
                        ps1[:, :], consts["one_row"][:, :],
                        affr[0:1, 2 * E + n * 512:2 * E + (n + 1) * 512],
                        start=False, stop=True, skip_group_check=True))
                    p.op("vector", lambda e, n=n: e.tensor_tensor(
                        xres[:, n * 512:(n + 1) * 512], ps1[:, :],
                        xres[:, n * 512:(n + 1) * 512], ALU.add))
                p.dma(xf_dram[gi * 128:(gi + 1) * 128, :], xres[:, :])
        if dbg is not None and l == 0 and "xf0" in dbg:
            for i in range(NT):
                p.dma(dbg["xf0"][i * 128:(i + 1) * 128, :],
                      xf_dram[i * 128:(i + 1) * 128, :])


def emit_l0_ffn(p, nc, bufs, consts, xn_dram, wdqT, bias_d, f0_dram):
    idb = consts["identb"]
    wt = [bufs.sb(f"lf_w{k}", (128, E), BF16) for k in range(8)]
    for kt in range(8):
        p.dma(wt[kt][:, :], wdqT[kt * 128:(kt + 1) * 128, :])
    brow = bufs.sb("lf_brow", (1, E), F32)
    p.dma(brow[:, :], bias_d[0:E].rearrange("(one n) -> one n", one=1))
    a = bufs.sb("lf_a", (128, E), F32)
    ab = bufs.sb("lf_ab", (128, E), BF16)
    xdT = bufs.sb("lf_xdT", (128, E), BF16)
    res = bufs.sb("lf_res", (128, E), F32)
    pst = bufs.ps("lf_pst", (128, 128), BF16)
    ps = [bufs.ps(f"lf_ps{n}", (128, 512), F32) for n in range(2)]
    for i in range(NT):
        p.dma(a[:, :], xn_dram[i * 128:(i + 1) * 128, :])
        emit_quant_tile(p, nc, bufs, a, a, pref="lf_q_")
        p.op("vector", lambda e: e.tensor_copy(ab[:, :], a[:, :]))
        for f in range(8):
            p.op("tensor", lambda e, f=f: e.transpose(
                pst[:, :], ab[:, f * 128:(f + 1) * 128], idb[:, :]))
            p.op("vector", lambda e, f=f: e.tensor_copy(
                xdT[:, f * 128:(f + 1) * 128], pst[:, :]))
        for n in range(2):
            for kt in range(8):
                p.op("tensor", lambda e, n=n, kt=kt: e.matmul(
                    ps[n][:, :], xdT[:, kt * 128:(kt + 1) * 128],
                    wt[kt][:, n * 512:(n + 1) * 512],
                    start=(kt == 0), stop=False, skip_group_check=True))
            p.op("tensor", lambda e, n=n: e.matmul(
                ps[n][:, :], consts["one_row"][:, :],
                brow[:, n * 512:(n + 1) * 512],
                start=False, stop=True, skip_group_check=True))
            p.op("vector", lambda e, n=n: e.tensor_copy(
                res[:, n * 512:(n + 1) * 512], ps[n][:, :]))
        p.dma(f0_dram[i * 128:(i + 1) * 128, :], res[:, :])


def emit_prep(p, nc, bufs, consts, ins, outs):
    idb = consts["identb"]
    for nm in ("qw", "kw", "vw", "outw"):
        emit_cast_bf16(p, nc, bufs, ins[nm], outs[nm + "_bf"], E, E)
    emit_cast_bf16(p, nc, bufs, ins["wqkv"], outs["wqkv_bf"], 2 * E, 1536)
    emit_cast_bf16(p, nc, bufs, ins["wo"], outs["wo_bf"], E, E)
    emit_cast_bf16(p, nc, bufs, ins["w1"], outs["w1_bf"], E, 2 * 5460)
    emit_cast_bf16(p, nc, bufs, ins["w2"], outs["w2_bf"], 5460, E)
    emit_transpose_bf16(p, nc, bufs, ins["ffnw"], outs["wdqT"], E, E, idb,
                        quant=True, pref="pq_")
    emit_transpose_bf16(p, nc, bufs, ins["W_in"], outs["WinT"], 2048, E, idb,
                        pref="p1_")
    emit_transpose_bf16(p, nc, bufs, ins["W_x"], outs["WxT"], 320, 2048, idb,
                        pref="p2_")
    emit_transpose_bf16(p, nc, bufs, ins["W_dt"], outs["WdtT"], 128, 64, idb,
                        pref="p3_")
    emit_transpose_bf16(p, nc, bufs, ins["W_out"], outs["WoutT"], E, 2048, idb,
                        pref="p4_")


from contextlib import ExitStack


def _dram(nc, name, shape, dt, kind):
    return nc.dram_tensor(name, list(shape), dt, kind=kind).ap()


def _run_launch(build_fn, in_specs, out_specs, in_arrays_per_core, cneed,
                cvals):
    nc = bass.Bass("TRN2", target_bir_lowering=False)
    ins = {k: _dram(nc, k, sh, dt, "ExternalInput")
           for k, (sh, dt) in in_specs.items()}
    outs = {k: _dram(nc, k, sh, dt, "ExternalOutput")
            for k, (sh, dt) in out_specs.items()}
    p = P(nc)
    with ExitStack() as stk:
        bufs = Bufs(nc, stk)
        consts = load_constants(p, nc, bufs, cvals, cneed)
        build_fn(p, nc, bufs, consts, ins, outs)
        p.emit()
    in_maps = []
    for c in range(NCORES):
        m = dict(in_arrays_per_core[c])
        for k in cneed:
            m["c_" + k] = cvals[k]
        in_maps.append(m)
    r = run_bass_kernel_spmd(nc, in_maps, core_ids=list(range(NCORES)))
    return r.results


def kernel(**inputs):
    cvals = make_constants()
    x_full = np.ascontiguousarray(
        inputs["L0_attn_x"].reshape(32 * S, E).astype(np.float32))
    shards = [np.ascontiguousarray(x_full[c * T:(c + 1) * T])
              for c in range(NCORES)]

    f32c = np.ascontiguousarray
    wq = f32c(inputs["L0_attn_q_w"]); wk = f32c(inputs["L0_attn_k_w"])
    wv = f32c(inputs["L0_attn_v_w"]); wout = f32c(inputs["L0_attn_out_w"])
    ffnw = f32c(inputs["L0_ffn_weight"])
    W_in = f32c(inputs["L1_attn_W_in"])
    conv_w = f32c(inputs["L1_attn_conv_w"].reshape(2048, 4))
    W_x = f32c(inputs["L1_attn_W_x"])
    W_dt = f32c(inputs["L1_attn_W_dt"])
    wqkv = f32c(inputs["L1_ffn_wqkv"].reshape(2 * E, 1536))
    wo = f32c(inputs["L1_ffn_wo"].reshape(E, E))
    w1 = f32c(np.transpose(inputs["L1_ffn_w1"], (1, 0, 2)).reshape(E, 2 * 5460))
    w2 = f32c(inputs["L1_ffn_w2"].reshape(2 * 2730, E))
    W_out = f32c(inputs["L1_attn_W_out"])

    BF16s = lambda sh: (sh, BF16)
    F32s = lambda sh: (sh, F32)

    prep_in = {"qw": F32s([E, E]), "kw": F32s([E, E]), "vw": F32s([E, E]),
               "outw": F32s([E, E]), "ffnw": F32s([E, E]),
               "W_in": F32s([2048, E]), "W_x": F32s([320, 2048]),
               "W_dt": F32s([128, 64]), "W_out": F32s([E, 2048]),
               "wqkv": F32s([2 * E, 1536]), "wo": F32s([E, E]),
               "w1": F32s([E, 2 * 5460]), "w2": F32s([5460, E])}
    prep_out = {"qw_bf": BF16s([E, E]), "kw_bf": BF16s([E, E]),
                "vw_bf": BF16s([E, E]), "outw_bf": BF16s([E, E]),
                "wdqT": BF16s([E, E]), "WinT": BF16s([E, 2048]),
                "WxT": BF16s([2048, 320]), "WdtT": BF16s([64, 128]),
                "WoutT": BF16s([2048, E]), "wqkv_bf": BF16s([2 * E, 1536]),
                "wo_bf": BF16s([E, E]), "w1_bf": BF16s([E, 2 * 5460]),
                "w2_bf": BF16s([5460, E])}
    prep_arrs = {"qw": wq, "kw": wk, "vw": wv, "outw": wout, "ffnw": ffnw,
                 "W_in": W_in, "W_x": W_x, "W_dt": W_dt, "W_out": W_out,
                 "wqkv": wqkv, "wo": wo, "w1": w1, "w2": w2}
    rp = _run_launch(
        lambda p, nc, bufs, consts, ins, outs: emit_prep(p, nc, bufs, consts,
                                                         ins, outs),
        prep_in, prep_out, [prep_arrs] * NCORES, {"identb", "one_row"}, cvals)

    r2 = _run_launch(
        lambda p, nc, bufs, consts, ins, outs: emit_ln0(
            p, nc, bufs, consts, ins["x"], outs["xn"], outs["xnT"]),
        {"x": F32s([T, E])},
        {"xn": F32s([T, E]), "xnT": BF16s([E, T])},
        [{"x": shards[c]} for c in range(NCORES)], {"identb"}, cvals)

    r3 = _run_launch(
        lambda p, nc, bufs, consts, ins, outs: emit_l0_attn(
            p, nc, bufs, consts, ins["xnT"], ins["qw"], ins["kw"], ins["vw"],
            ins["qb"], ins["kb"], ins["vb"], ins["proj"], outs["attnT"]),
        {"xnT": BF16s([E, T]), "qw": BF16s([E, E]), "kw": BF16s([E, E]),
         "vw": BF16s([E, E]), "qb": F32s([E]), "kb": F32s([E]),
         "vb": F32s([E]), "proj": F32s([256, 64])},
        {"attnT": BF16s([E, T])},
        [{"xnT": r2[c]["xnT"], "qw": rp[c]["qw_bf"], "kw": rp[c]["kw_bf"],
          "vw": rp[c]["vw_bf"], "qb": inputs["L0_attn_q_b"],
          "kb": inputs["L0_attn_k_b"], "vb": inputs["L0_attn_v_b"],
          "proj": f32c(inputs["L0_attn_proj"])} for c in range(NCORES)],
        {"identb", "identf", "identb64", "triu_mult", "cols", "negone_row"},
        cvals)

    r4 = _run_launch(
        lambda p, nc, bufs, consts, ins, outs: emit_l0_post(
            p, nc, bufs, consts, ins["attnT"], ins["outw"], ins["outb"],
            ins["xn"], ins["gamma"], ins["beta"], outs["a0"]),
        {"attnT": BF16s([E, T]), "outw": BF16s([E, E]), "outb": F32s([E]),
         "xn": F32s([T, E]), "gamma": F32s([E]), "beta": F32s([E])},
        {"a0": F32s([T, E])},
        [{"attnT": r3[c]["attnT"], "outw": rp[c]["outw_bf"],
          "outb": inputs["L0_attn_out_b"], "xn": r2[c]["xn"],
          "gamma": inputs["L0_attn_gamma"], "beta": inputs["L0_attn_beta"]}
         for c in range(NCORES)], {"one_row"}, cvals)

    r5 = _run_launch(
        lambda p, nc, bufs, consts, ins, outs: emit_l0_ffn(
            p, nc, bufs, consts, ins["xn"], ins["wdqT"], ins["fb"],
            outs["f0"]),
        {"xn": F32s([T, E]), "wdqT": BF16s([E, E]), "fb": F32s([E])},
        {"f0": F32s([T, E])},
        [{"xn": r2[c]["xn"], "wdqT": rp[c]["wdqT"],
          "fb": inputs["L0_ffn_bias"]} for c in range(NCORES)],
        {"identb", "one_row"}, cvals)

    r6 = _run_launch(
        lambda p, nc, bufs, consts, ins, outs: emit_combine_ln(
            p, nc, bufs, consts, [ins["x"], ins["a0"], ins["f0"]],
            outs["x1"], outs["xn1"], outs["xn1T"]),
        {"x": F32s([T, E]), "a0": F32s([T, E]), "f0": F32s([T, E])},
        {"x1": F32s([T, E]), "xn1": F32s([T, E]), "xn1T": BF16s([E, T])},
        [{"x": shards[c], "a0": r4[c]["a0"], "f0": r5[c]["f0"]}
         for c in range(NCORES)], {"identb"}, cvals)

    r7 = _run_launch(
        lambda p, nc, bufs, consts, ins, outs: emit_l1_attn(
            p, nc, bufs, consts, ins["xn1T"], ins["WinT"], ins["convw"],
            ins["convb"], ins["WxT"], ins["WdtT"], ins["bdt"], ins["Alog"],
            ins["wg"], outs["zT"], outs["yw"]),
        {"xn1T": BF16s([E, T]), "WinT": BF16s([E, 2048]),
         "convw": F32s([2048, 4]), "convb": F32s([2048]),
         "WxT": BF16s([2048, 320]), "WdtT": BF16s([64, 128]),
         "bdt": F32s([128]), "Alog": F32s([128]), "wg": F32s([1])},
        {"zT": BF16s([2048, T]), "yw": F32s([BL, S])},
        [{"xn1T": r6[c]["xn1T"], "WinT": rp[c]["WinT"], "convw": conv_w,
          "convb": inputs["L1_attn_conv_b"], "WxT": rp[c]["WxT"],
          "WdtT": rp[c]["WdtT"], "bdt": inputs["L1_attn_b_dt"],
          "Alog": inputs["L1_attn_A_log"], "wg": inputs["L1_attn_w_gate"]}
         for c in range(NCORES)], {"cols", "identf"}, cvals)

    r8 = _run_launch(
        lambda p, nc, bufs, consts, ins, outs: emit_l1_attn_out(
            p, nc, bufs, consts, ins["zT"], ins["yw"], ins["WoutT"],
            outs["a1"]),
        {"zT": BF16s([2048, T]), "yw": F32s([BL, S]),
         "WoutT": BF16s([2048, E])},
        {"a1": F32s([T, E])},
        [{"zT": r7[c]["zT"], "yw": r7[c]["yw"], "WoutT": rp[c]["WoutT"]}
         for c in range(NCORES)], {"identf"}, cvals)

    r9 = _run_launch(
        lambda p, nc, bufs, consts, ins, outs: emit_l1_ffn(
            p, nc, bufs, consts, ins["xn1"], ins["wqkv"], ins["qs"],
            ins["ks"], ins["wo"], ins["lng"], ins["lnb"], ins["w1"],
            ins["b1"], ins["w2"], ins["b2"], outs["xf"]),
        {"xn1": F32s([T, E]), "wqkv": BF16s([2 * E, 1536]),
         "qs": F32s([2, 64]), "ks": F32s([2, 64]), "wo": BF16s([E, E]),
         "lng": F32s([2, E]), "lnb": F32s([2, E]),
         "w1": BF16s([E, 2 * 5460]), "b1": F32s([2 * 5460]),
         "w2": BF16s([5460, E]), "b2": F32s([2, E])},
        {"xf": F32s([T, E])},
        [{"xn1": r6[c]["xn1"], "wqkv": rp[c]["wqkv_bf"],
          "qs": f32c(inputs["L1_ffn_q_scale"]),
          "ks": f32c(inputs["L1_ffn_k_scale"]), "wo": rp[c]["wo_bf"],
          "lng": f32c(inputs["L1_ffn_ln_g"]), "lnb": f32c(inputs["L1_ffn_ln_b"]),
          "w1": rp[c]["w1_bf"], "b1": f32c(inputs["L1_ffn_b1"].reshape(-1)),
          "w2": rp[c]["w2_bf"], "b2": f32c(inputs["L1_ffn_b2"])}
         for c in range(NCORES)],
        {"identb", "identf", "identb64", "maskLb", "maskRb",
         "rot_Aq", "rot_Bq", "rot_Ak", "rot_Bk", "cols", "one_row"}, cvals)

    r10 = _run_launch(
        lambda p, nc, bufs, consts, ins, outs: emit_sum_final(
            p, nc, bufs, [ins["x1"], ins["a1"], ins["xf"]], outs["out"]),
        {"x1": F32s([T, E]), "a1": F32s([T, E]), "xf": F32s([T, E])},
        {"out": F32s([T, E])},
        [{"x1": r6[c]["x1"], "a1": r8[c]["a1"], "xf": r9[c]["xf"]}
         for c in range(NCORES)], set(), cvals)

    out = np.concatenate([r10[c]["out"] for c in range(NCORES)], axis=0)
    return out.reshape(32, S, E).astype(np.float32)


# version 03
# speedup vs baseline: 1.1244x; 1.1244x over previous
import numpy as np
import ml_dtypes
import concourse.bass as bass
import concourse.mybir as mybir
from concourse.bass_utils import run_bass_kernel_spmd

F32 = mybir.dt.float32
BF16 = mybir.dt.bfloat16
FP8 = mybir.dt.float8e4
AX = mybir.AxisListType
ALU = mybir.AluOpType
AF = mybir.ActivationFunctionType

NCORES = 8
BL = 4
S = 2048
T = BL * S
E = 1024
NT = T // 128

BIG_NEG = np.float32(-3.4028235e38)
BF = ml_dtypes.bfloat16


class P:
    ENGINES = ("sync", "scalar", "vector", "tensor", "gpsimd")

    def __init__(self, nc):
        self.nc = nc
        self.ops = []

    def op(self, engine, fn, inc=1):
        self.ops.append((engine, fn, inc))

    def dma(self, out, in_, engine="sync", **kw):
        self.ops.append((engine, lambda e: e.dma_start(out=out, in_=in_, **kw), 16))

    def emit(self):
        nc = self.nc
        starts, c = [], 0
        for (_, _, inc) in self.ops:
            starts.append(c)
            c += inc
        total = c
        by_eng = {}
        for i, o in enumerate(self.ops):
            by_eng.setdefault(o[0], []).append((i, o))
        with nc.semaphore("tick") as tick, nc.Block() as block:
            def mk(ename):
                def body(eng):
                    for i, (_, fn, inc) in by_eng.get(ename, []):
                        if starts[i] > 0:
                            eng.wait_ge(tick, starts[i])
                        fn(eng).then_inc(tick, inc)
                    if ename == "sync":
                        eng.wait_ge(tick, total)
                body.__name__ = "body_" + ename
                return body
            for ename in self.ENGINES:
                if ename in by_eng or ename == "sync":
                    getattr(block, ename)(mk(ename))
        return total


class Bufs:

    def __init__(self, nc, stack):
        self.nc, self.stack, self.cache = nc, stack, {}

    def sb(self, name, shape, dtype=F32):
        if name not in self.cache:
            self.cache[name] = self.stack.enter_context(
                self.nc.sbuf_tensor(name, list(shape), dtype))
        return self.cache[name]

    def ps(self, name, shape, dtype=F32):
        if name not in self.cache:
            esz = 4 if dtype == F32 else 2
            free = max(shape[1], 2048 // esz)
            self.cache[name] = self.stack.enter_context(
                self.nc.psum_tensor(name, [shape[0], free], dtype))
        return self.cache[name][0:shape[0], 0:shape[1]]


def make_constants():
    c = {}
    c["identb"] = np.eye(128).astype(BF)
    c["identf"] = np.eye(128, dtype=np.float32)
    e64 = np.eye(64)
    c["identb64"] = np.vstack([e64, e64]).astype(BF)
    row, col = np.mgrid[0:128, 0:128]
    c["triu_mult"] = (row <= col).astype(np.float32)
    NEGB = -1e38
    c["maskLb"] = np.where(col >= row, 0.0, NEGB).astype(BF)
    c["maskRb"] = np.where(col <= row, 0.0, NEGB).astype(BF)
    DH, WINDOW = 64, 512
    half = np.arange(0, DH, 2, dtype=np.float64)
    inv_freq = 1.0 / 10000.0 ** (half / DH)
    t = np.arange(S, dtype=np.float64)
    freqs = np.concatenate([t[:, None] * inv_freq[None, :]] * 2, -1)
    base = (half + 0.4 * DH) / (1.4 * DH)
    power = (t - S // 2) / float(WINDOW // 2)
    scale = np.concatenate([base[None, :] ** power[:, None]] * 2, -1)
    cos, sin = np.cos(freqs), np.sin(freqs)
    Sq, Sk = (sin * scale).T, (sin / scale).T
    Aq, Ak = (cos * scale).T, (cos / scale).T
    Bq = np.concatenate([-Sq[0:32], Sq[32:64]], 0)
    Bk = np.concatenate([-Sk[0:32], Sk[32:64]], 0)
    c["rot_Aq"] = np.vstack([Aq, Aq]).astype(np.float32)
    c["rot_Ak"] = np.vstack([Ak, Ak]).astype(np.float32)
    c["rot_Bq"] = np.vstack([Bq, Bq]).astype(np.float32)
    c["rot_Bk"] = np.vstack([Bk, Bk]).astype(np.float32)
    cols = np.zeros((128, 8), dtype=np.float32)
    cols[:, 0] = 1.0
    cols[0:64, 1] = 1.0
    cols[64:128, 2] = 1.0
    cols[0:64, 3] = 1.0 / 128.0
    cols[64:128, 4] = 1.0 / 128.0
    c["cols"] = cols
    c["one_row"] = np.ones((1, 128), dtype=np.float32)
    c["negone_row"] = np.full((1, 128), -1.0, dtype=np.float32)
    return c


CONST_SPECS = {
    "identb": BF16, "identf": F32, "identb64": BF16,
    "triu_mult": F32, "maskLb": BF16, "maskRb": BF16,
    "rot_Aq": F32, "rot_Bq": F32, "rot_Ak": F32, "rot_Bk": F32,
    "cols": F32, "one_row": F32, "negone_row": F32,
}


def load_constants(p, nc, bufs, cvals, names=None):
    tiles = {}
    for name, dt in CONST_SPECS.items():
        if names is not None and name not in names:
            continue
        arr = cvals[name]
        d = nc.dram_tensor("c_" + name, list(arr.shape), dt, kind="ExternalInput").ap()
        tsb = bufs.sb("sb_" + name, arr.shape, dt)
        p.dma(tsb[:, :], d[:, :])
        tiles[name] = tsb
    return tiles


def load_col(p, nc, bufs, dram_vec, lo, n, name):
    t = bufs.sb(name, (128, 1), F32)
    src = dram_vec[lo:lo + n].rearrange("(p one) -> p one", one=1)
    p.dma(t[0:n, :], src, allow_slow_non_contiguous=True)
    return t


def emit_layernorm_tile(p, nc, bufs, xt, xn, eps, n=E, pref=""):
    st = bufs.sb(pref + "ln_stat", (128, 8), F32)
    sq = bufs.sb(pref + "ln_sq", (128, n), F32)
    p.op("vector", lambda e: e.reduce_sum(st[:, 0:1], xt[:, 0:n], AX.X))
    p.op("vector", lambda e: e.tensor_scalar_mul(st[:, 1:2], st[:, 0:1], -1.0 / n))
    p.op("vector", lambda e: e.tensor_scalar_add(xn[:, 0:n], xt[:, 0:n], st[:, 1:2]))
    p.op("scalar", lambda e: e.activation(sq[:, 0:n], xn[:, 0:n], AF.Square,
                                          accum_out=st[:, 2:3]))
    p.op("scalar", lambda e: e.activation(st[:, 3:4], st[:, 2:3], AF.Sqrt,
                                          bias=np.float32(eps), scale=1.0 / n))
    p.op("vector", lambda e: e.reciprocal(st[:, 4:5], st[:, 3:4]))
    p.op("vector", lambda e: e.tensor_scalar_mul(xn[:, 0:n], xn[:, 0:n], st[:, 4:5]))


def emit_transpose_store(p, nc, bufs, srcb, dst_dram, row0, col0, nf, idb, pref=""):
    pst = bufs.ps(pref + "tr_ps", (128, 128), BF16)
    tb = bufs.sb(pref + "tr_out", (128, 128), BF16)
    for f in range(nf):
        p.op("tensor", lambda e, f=f: e.transpose(pst[:, :], srcb[:, f * 128:(f + 1) * 128], idb[:, :]))
        p.op("vector", lambda e: e.tensor_copy(tb[:, :], pst[:, :]))
        p.dma(dst_dram[row0 + f * 128:row0 + (f + 1) * 128, col0:col0 + 128], tb[:, :])


def emit_ln0(p, nc, bufs, consts, x_dram, xn_dram, xnT_dram, eps=1e-5):
    xt = bufs.sb("ln0_x", (128, E), F32)
    xb = bufs.sb("ln0_xb", (128, E), BF16)
    for i in range(NT):
        p.dma(xt[:, :], x_dram[i * 128:(i + 1) * 128, :])
        emit_layernorm_tile(p, nc, bufs, xt, xt, eps)
        if xn_dram is not None:
            p.dma(xn_dram[i * 128:(i + 1) * 128, :], xt[:, :])
        if xnT_dram is not None:
            p.op("vector", lambda e: e.tensor_copy(xb[:, :], xt[:, :]))
            emit_transpose_store(p, nc, bufs, xb, xnT_dram, 0, i * 128, 8,
                                 consts["identb"])


def np_layernorm(x, eps=1e-5):
    m = x.mean(-1, keepdims=True)
    v = x.var(-1, keepdims=True)
    return (x - m) / np.sqrt(v + eps)


M1 = 1572864.0


def emit_quant_tile(p, nc, bufs, xt, xd, n=E, pref="q_"):
    nb = n // 32
    am = bufs.sb(pref + "am", (128, E // 32), F32)
    y = bufs.sb(pref + "y", (128, E), F32)
    g = bufs.sb(pref + "g", (128, E), F32)
    f8 = bufs.sb(pref + "f8", (128, E), FP8)
    r1 = bufs.sb(pref + "r1", (128, E), F32)
    xr = xt[:, 0:n].rearrange("p (k c) -> p k c", c=32)
    p.op("vector", lambda e: e.tensor_reduce(am[:, 0:nb], xr, AX.X, ALU.max,
                                             apply_absolute_value=True))
    p.op("vector", lambda e: e.tensor_scalar(am[:, 0:nb], am[:, 0:nb],
                                             1.0 / 7.5, 1e-30, ALU.mult, ALU.max))
    rs = bufs.sb(pref + "rs", (128, E // 32), F32)
    p.op("vector", lambda e: e.reciprocal(rs[:, 0:nb], am[:, 0:nb]))
    bc = lambda t: t[:, 0:nb].unsqueeze(2).broadcast_to([128, nb, 32])
    yr = y[:, 0:n].rearrange("p (k c) -> p k c", c=32)
    p.op("vector", lambda e: e.tensor_tensor(yr, xr, bc(rs), ALU.mult))
    p.op("vector", lambda e: e.tensor_copy(f8[:, 0:n], y[:, 0:n]))
    p.op("vector", lambda e: e.tensor_copy(g[:, 0:n], f8[:, 0:n]))
    p.op("vector", lambda e: e.tensor_scalar_add(r1[:, 0:n], y[:, 0:n], M1))
    p.op("vector", lambda e: e.tensor_scalar_add(r1[:, 0:n], r1[:, 0:n], -M1))
    p.op("vector", lambda e: e.tensor_scalar(r1[:, 0:n], r1[:, 0:n], 1.0, -1.0,
                                             ALU.min, ALU.max))
    p.op("vector", lambda e: e.tensor_scalar(y[:, 0:n], g[:, 0:n], 1.0, -1.0,
                                             ALU.min, ALU.max))
    p.op("vector", lambda e: e.tensor_sub(g[:, 0:n], g[:, 0:n], y[:, 0:n]))
    p.op("vector", lambda e: e.tensor_add(g[:, 0:n], g[:, 0:n], r1[:, 0:n]))
    xdr = xd[:, 0:n].rearrange("p (k c) -> p k c", c=32)
    gr = g[:, 0:n].rearrange("p (k c) -> p k c", c=32)
    p.op("vector", lambda e: e.tensor_tensor(xdr, gr, bc(am), ALU.mult))


def np_quant(x):
    vals = set()
    for bits in range(64):
        s = bits >> 5 & 1
        ee = bits >> 3 & 3
        m = bits & 7
        v = m / 8.0 if ee == 0 else (1.0 + m / 8.0) * 2.0 ** (ee - 1)
        vals.add(-v if s else v)
    grid = np.array(sorted(vals), dtype=np.float32)
    rows, K = x.shape
    b = x.reshape(rows, K // 32, 32)
    scales = np.maximum(np.max(np.abs(b), -1) / 7.5, 1e-30).astype(np.float32)
    y = (b / scales[..., None]).astype(np.float32)
    idx = np.clip(np.searchsorted(grid, y), 0, grid.shape[0] - 1)
    lo = np.maximum(idx - 1, 0)
    vhi, vlo = grid[idx], grid[lo]
    q = np.where(np.abs(y - vlo) <= np.abs(y - vhi), vlo, vhi)
    return (q * scales[..., None]).reshape(rows, K).astype(np.float32)


def emit_cast_bf16(p, nc, bufs, src_dram, dst_dram, R, C, pref="cb_"):
    a = bufs.sb(pref + "a", (128, 2048), F32)
    b = bufs.sb(pref + "b", (128, 2048), BF16)
    for r0 in range(0, R, 128):
        nr = min(128, R - r0)
        for c0 in range(0, C, 2048):
            ncol = min(2048, C - c0)
            p.dma(a[0:nr, 0:ncol], src_dram[r0:r0 + nr, c0:c0 + ncol])
            p.op("vector", lambda e, nr=nr, ncol=ncol: e.tensor_copy(
                b[0:nr, 0:ncol], a[0:nr, 0:ncol]))
            p.dma(dst_dram[r0:r0 + nr, c0:c0 + ncol], b[0:nr, 0:ncol])


def emit_transpose_bf16(p, nc, bufs, src_dram, dst_dram, R, C, idb,
                        quant=False, scale=None, pref="tb_"):
    a = bufs.sb(pref + "a", (128, max(C, 32)), F32)
    b = bufs.sb(pref + "b", (128, max(C, 32)), BF16)
    pst = bufs.ps(pref + "ps", (128, 128), BF16)
    o = bufs.sb(pref + "o", (128, 128), BF16)
    assert C <= 8192
    for r0 in range(0, R, 128):
        nr = min(128, R - r0)
        p.dma(a[0:nr, 0:C], src_dram[r0:r0 + nr, 0:C])
        if quant:
            assert nr == 128
            emit_quant_tile(p, nc, bufs, a, a, n=C, pref=pref + "q_")
        if scale is not None:
            p.op("vector", lambda e, nr=nr: e.tensor_scalar_mul(
                a[0:nr, 0:C], a[0:nr, 0:C], float(scale)))
        p.op("vector", lambda e, nr=nr: e.tensor_copy(b[0:nr, 0:C], a[0:nr, 0:C]))
        for c0 in range(0, C, 128):
            ncol = min(128, C - c0)
            p.op("tensor", lambda e, nr=nr, c0=c0, ncol=ncol: e.transpose(
                pst[0:ncol, 0:nr], b[0:nr, c0:c0 + ncol], idb[0:nr, 0:nr]))
            p.op("vector", lambda e, nr=nr, ncol=ncol: e.tensor_copy(
                o[0:ncol, 0:nr], pst[0:ncol, 0:nr]))
            p.dma(dst_dram[c0:c0 + ncol, r0:r0 + nr], o[0:ncol, 0:nr])


def load_bias_cols(p, nc, bufs, dram_vec, n, name):
    nf = n // 128
    t = bufs.sb(name, (128, nf), F32)
    src = dram_vec[0:n].rearrange("(f p) -> p f", p=128)
    p.dma(t[:, 0:nf], src, allow_slow_non_contiguous=True)
    return t


def emit_l0_attn(p, nc, bufs, consts, xnT, qw, kw, vw, qb_d, kb_d, vb_d,
                 proj_d, attnT):
    idb = consts["identb"]
    H, D, Mf = 16, 64, 256
    NCH = S // 128

    qb = load_bias_cols(p, nc, bufs, qb_d, E, "l0_qb")
    kb = load_bias_cols(p, nc, bufs, kb_d, E, "l0_kb")
    vb = load_bias_cols(p, nc, bufs, vb_d, E, "l0_vb")

    projT = bufs.sb("l0_projT", (128, 256), BF16)
    pj = bufs.sb("l0_pj", (128, 64), F32)
    pps = bufs.ps("l0_psmisc", (128, 512), F32)[0:64, 0:128]
    for half in range(2):
        p.dma(pj[:, :], proj_d[half * 128:(half + 1) * 128, :])
        p.op("tensor", lambda e: e.transpose(pps, pj[:, :],
                                             consts["identf"][:, :]))
        p.op("scalar", lambda e, half=half: e.activation(
            projT[0:64, half * 128:(half + 1) * 128], pps, AF.Copy,
            scale=0.125))
        p.op("scalar", lambda e, half=half: e.activation(
            projT[64:128, half * 128:(half + 1) * 128], pps, AF.Copy,
            scale=0.125))

    xnb = [bufs.sb(f"l0_xn{k}", (128, S), BF16) for k in range(8)]
    wsb = bufs.sb("l0_w", (128, 3 * E), BF16)
    qkvT = [bufs.sb(f"l0_qkvT{j}", (128, S), BF16) for j in range(6)]
    corrK = [bufs.sb(f"l0_corrK{h}", (1, S), F32) for h in range(4)]
    sqt = bufs.sb("l0_sq", (128, 512), F32)
    qp = [bufs.sb(f"l0_qp{j}", (128, S), BF16) for j in range(2)]
    kp = [bufs.sb(f"l0_kp{j}", (128, S), BF16) for j in range(2)]
    vtok = bufs.sb("l0_vtok", (128, NCH * 64), BF16)
    kptok = bufs.sb("l0_kptok", (128, 256), BF16)
    smT = bufs.sb("l0_smT", (128, 128), BF16)
    kvsb = bufs.sb("l0_kvsb", (128, 2 * 65), BF16)
    numsb = bufs.sb("l0_numsb", (128, 65), F32)
    att = bufs.sb("l0_att", (128, 64), BF16)
    attT = bufs.sb("l0_attT", (64, 128), BF16)
    onesb = bufs.sb("l0_onesb", (128, 1), BF16)
    triub = bufs.sb("l0_triub", (128, 128), F32)
    p.op("vector", lambda e: e.memset(onesb[:, :], 1.0))
    p.op("vector", lambda e: e.tensor_copy(triub[:, :], consts["triu_mult"][:, :]))

    ps_mm = bufs.ps("l0_psmm", (128, 512), F32)
    ps_map = bufs.ps("l0_psmap", (128, 512), F32)
    ps_misc = bufs.ps("l0_psmisc", (128, 512), F32)
    ps_corr = ps_misc[0:1, 0:512]
    ps_sqk = ps_misc[:, 0:128]
    ps_nd = bufs.ps("l0_psnd", (128, 65), F32)
    ps_kv = [bufs.ps(f"l0_pskv{j}", (128, 65), F32) for j in range(2)]
    ps_trbf = bufs.ps("l0_pstrbf", (128, 128), BF16)
    ps_trb = ps_trbf[:, 0:128]
    ps_tr64 = ps_trbf[:, 0:64]

    for b in range(BL):
        for k in range(8):
            p.dma(xnb[k][:, :], xnT[k * 128:(k + 1) * 128, b * S:(b + 1) * S])
        for hg in range(4):
            for ft in range(2):
                frow = hg * 256 + ft * 128
                for wi, (wd, bias) in enumerate(((qw, qb), (kw, kb), (vw, vb))):
                    for kt in range(8):
                        p.dma(wsb[:, wi * E + kt * 128:wi * E + (kt + 1) * 128],
                              wd[kt * 128:(kt + 1) * 128, frow:frow + 128])
                for wi, (wd, bias) in enumerate(((qw, qb), (kw, kb), (vw, vb))):
                    bcol = bias[:, frow // 128:frow // 128 + 1]
                    for c in range(4):
                        for kt in range(8):
                            p.op("tensor", lambda e, wi=wi, kt=kt, c=c: e.matmul(
                                ps_mm[:, :],
                                wsb[:, wi * E + kt * 128:wi * E + (kt + 1) * 128],
                                xnb[kt][:, c * 512:(c + 1) * 512],
                                start=(kt == 0), stop=(kt == 7)))
                        dst = qkvT[wi * 2 + ft]
                        p.op("vector", lambda e, dst=dst, c=c, bcol=bcol: e.tensor_scalar_add(
                            dst[:, c * 512:(c + 1) * 512], ps_mm[:, :], bcol))
                        if wi == 1:
                            p.op("scalar", lambda e, bcol=bcol: e.activation(
                                sqt[:, :], ps_mm[:, :], AF.Square, bias=bcol))
                            for hh in range(2):
                                hloc = ft * 2 + hh
                                p.op("tensor", lambda e, hh=hh: e.matmul(
                                    ps_corr[:, :], consts["cols"][:, 3 + hh:4 + hh],
                                    sqt[:, :], start=True, stop=True))
                                p.op("vector", lambda e, hloc=hloc, c=c: e.tensor_copy(
                                    corrK[hloc][:, c * 512:(c + 1) * 512],
                                    ps_corr[:, :]))
            for h in range(4):
                ft, base = h // 2, (h % 2) * 64
                hglob = hg * 4 + h
                qh = qkvT[ft]
                kh = qkvT[2 + ft]
                vh = qkvT[4 + ft]
                for dst, srcT, corr in ((qp, qh, None), (kp, kh, corrK[h])):
                    for mt in range(2):
                        for c in range(4):
                            p.op("tensor", lambda e, mt=mt, c=c, srcT=srcT, base=base: e.matmul(
                                ps_map[:, :],
                                projT[base:base + 64, mt * 128:(mt + 1) * 128],
                                srcT[base:base + 64, c * 512:(c + 1) * 512],
                                start=True, stop=(corr is None)))
                            if corr is not None:
                                p.op("tensor", lambda e, c=c, corr=corr: e.matmul(
                                    ps_map[:, :], consts["negone_row"][:, :],
                                    corr[:, c * 512:(c + 1) * 512],
                                    start=False, stop=True))
                            p.op("scalar", lambda e, dst=dst, mt=mt, c=c: e.activation(
                                dst[mt][:, c * 512:(c + 1) * 512], ps_map[:, :],
                                AF.Exp))
                for c in range(NCH):
                    p.op("tensor", lambda e, c=c, vh=vh, base=base: e.transpose(
                        ps_tr64[:, :], vh[base:base + 64, c * 128:(c + 1) * 128],
                        consts["identb64"][base:base + 64, 0:64]))
                    p.op("vector", lambda e, c=c: e.tensor_copy(
                        vtok[:, c * 64:(c + 1) * 64], ps_tr64[:, :]))
                for c in range(NCH):
                    cs = slice(c * 128, (c + 1) * 128)
                    for mt in range(2):
                        p.op("tensor", lambda e, mt=mt, cs=cs: e.matmul(
                            ps_sqk[:, :], kp[mt][:, cs], qp[mt][:, cs],
                            start=(mt == 0), stop=(mt == 1)))
                    p.op("vector", lambda e: e.tensor_tensor(
                        smT[:, :], ps_sqk[:, :], triub[:, :], ALU.mult))
                    p.op("tensor", lambda e, c=c: e.matmul(
                        ps_nd[:, 0:64], smT[:, :], vtok[:, c * 64:(c + 1) * 64],
                        start=True, stop=False, skip_group_check=True))
                    p.op("tensor", lambda e: e.matmul(
                        ps_nd[:, 64:65], smT[:, :], onesb[:, :],
                        start=False, stop=False, skip_group_check=True))
                    for mt in range(2):
                        if c > 0:
                            p.op("tensor", lambda e, mt=mt, cs=cs: e.matmul(
                                ps_nd[:, 0:65], qp[mt][:, cs],
                                kvsb[:, mt * 65:(mt + 1) * 65],
                                start=False, stop=(mt == 1),
                                skip_group_check=True))
                    for mt in range(2):
                        p.op("tensor", lambda e, mt=mt, cs=cs: e.transpose(
                            ps_trb[:, :], kp[mt][:, cs], idb[:, :]))
                        p.op("vector", lambda e, mt=mt: e.tensor_copy(
                            kptok[:, mt * 128:(mt + 1) * 128], ps_trb[:, :]))
                    for mt in range(2):
                        p.op("tensor", lambda e, mt=mt, c=c: e.matmul(
                            ps_kv[mt][:, 0:64], kptok[:, mt * 128:(mt + 1) * 128],
                            vtok[:, c * 64:(c + 1) * 64],
                            start=(c == 0), stop=False, skip_group_check=True))
                        p.op("tensor", lambda e, mt=mt, c=c: e.matmul(
                            ps_kv[mt][:, 64:65], kptok[:, mt * 128:(mt + 1) * 128],
                            onesb[:, :],
                            start=False, stop=(c == NCH - 1),
                            skip_group_check=True))
                        p.op("vector", lambda e, mt=mt: e.tensor_copy(
                            kvsb[:, mt * 65:(mt + 1) * 65], ps_kv[mt][:, :]))
                    p.op("vector", lambda e: e.reciprocal(numsb[:, 64:65],
                                                          ps_nd[:, 64:65]))
                    p.op("vector", lambda e: e.tensor_scalar(
                        att[:, :], ps_nd[:, 0:64], numsb[:, 64:65], None,
                        ALU.mult))
                    p.op("tensor", lambda e: e.transpose(ps_trb[0:64, 0:128],
                                                         att[:, :], idb[:, :]))
                    p.op("vector", lambda e: e.tensor_copy(attT[:, :],
                                                           ps_trb[0:64, 0:128]))
                    p.dma(attnT[hglob * 64:(hglob + 1) * 64,
                                b * S + c * 128:b * S + (c + 1) * 128],
                          attT[:, :])




def emit_l0_post(p, nc, bufs, consts, attnT, outw, outb_d, xn_dram,
                 gamma_d, beta_d, a0_dram):
    wt = [bufs.sb(f"lp_w{k}", (128, E), BF16) for k in range(8)]
    for kt in range(8):
        p.dma(wt[kt][:, :], outw[kt * 128:(kt + 1) * 128, :])
    brow = bufs.sb("lp_brow", (1, E), F32)
    p.dma(brow[:, :], outb_d[0:E].rearrange("(one n) -> one n", one=1))
    grow = bufs.sb("lp_grow", (1, E), F32)
    berow = bufs.sb("lp_berow", (1, E), F32)
    p.dma(grow[:, :], gamma_d[0:E].rearrange("(one n) -> one n", one=1))
    p.dma(berow[:, :], beta_d[0:E].rearrange("(one n) -> one n", one=1))
    ps = [bufs.ps(f"lp_ps{k}", (128, 512), F32) for k in range(2)]
    gb = bufs.sb("lp_gb", (128, E), F32)
    bb = bufs.sb("lp_bb", (128, E), F32)
    for n in range(2):
        p.op("tensor", lambda e, n=n: e.matmul(
            ps[0][:, :], consts["one_row"][:, :], grow[:, n * 512:(n + 1) * 512],
            start=True, stop=True))
        p.op("vector", lambda e, n=n: e.tensor_copy(gb[:, n * 512:(n + 1) * 512],
                                                    ps[0][:, :]))
        p.op("tensor", lambda e, n=n: e.matmul(
            ps[0][:, :], consts["one_row"][:, :], berow[:, n * 512:(n + 1) * 512],
            start=True, stop=True))
        p.op("vector", lambda e, n=n: e.tensor_copy(bb[:, n * 512:(n + 1) * 512],
                                                    ps[0][:, :]))
    at = bufs.sb("lp_at", (128, E), BF16)
    res = bufs.sb("lp_res", (128, E), F32)
    for i in range(NT):
        for kt in range(8):
            p.dma(at[:, kt * 128:(kt + 1) * 128],
                  attnT[kt * 128:(kt + 1) * 128, i * 128:(i + 1) * 128])
        for n in range(2):
            for kt in range(8):
                p.op("tensor", lambda e, n=n, kt=kt: e.matmul(
                    ps[n][:, :], at[:, kt * 128:(kt + 1) * 128],
                    wt[kt][:, n * 512:(n + 1) * 512],
                    start=(kt == 0), stop=False, skip_group_check=True))
            p.op("tensor", lambda e, n=n: e.matmul(
                ps[n][:, :], consts["one_row"][:, :],
                brow[:, n * 512:(n + 1) * 512],
                start=False, stop=True, skip_group_check=True))
        p.dma(res[:, :], xn_dram[i * 128:(i + 1) * 128, :])
        for n in range(2):
            p.op("vector", lambda e, n=n: e.tensor_tensor(
                res[:, n * 512:(n + 1) * 512], ps[n][:, :],
                res[:, n * 512:(n + 1) * 512], ALU.add))
        emit_layernorm_tile(p, nc, bufs, res, res, 1e-6, pref="lp_")
        p.op("vector", lambda e: e.tensor_tensor(res[:, :], res[:, :], gb[:, :],
                                                 ALU.mult))
        p.op("vector", lambda e: e.tensor_tensor(res[:, :], res[:, :], bb[:, :],
                                                 ALU.add))
        p.dma(a0_dram[i * 128:(i + 1) * 128, :], res[:, :])


def np_l0_post(attn, xn, outw, outb, gamma, beta):
    main = attn.astype(np.float64) @ outw.astype(np.float64) + outb
    res = main + xn
    mu = res.mean(-1, keepdims=True)
    var = ((res - mu) ** 2).mean(-1, keepdims=True)
    return (gamma * (res - mu) / np.sqrt(var + 1e-6) + beta).astype(np.float32)


def emit_combine_ln(p, nc, bufs, consts, srcs, x_out, xn_out, xnT_out,
                    eps=1e-5):
    a = bufs.sb("cb_x", (128, E), F32)
    t = bufs.sb("cb_t", (128, E), F32)
    xb = bufs.sb("cb_xb", (128, E), BF16)
    for i in range(NT):
        p.dma(a[:, :], srcs[0][i * 128:(i + 1) * 128, :])
        for s in srcs[1:]:
            p.dma(t[:, :], s[i * 128:(i + 1) * 128, :])
            p.op("vector", lambda e: e.tensor_add(a[:, :], a[:, :], t[:, :]))
        if x_out is not None:
            p.dma(x_out[i * 128:(i + 1) * 128, :], a[:, :])
        emit_layernorm_tile(p, nc, bufs, a, a, eps, pref="cb_")
        if xn_out is not None:
            p.dma(xn_out[i * 128:(i + 1) * 128, :], a[:, :])
        if xnT_out is not None:
            p.op("vector", lambda e: e.tensor_copy(xb[:, :], a[:, :]))
            emit_transpose_store(p, nc, bufs, xb, xnT_out, 0, i * 128, 8,
                                 consts["identb"], pref="cb_")


def emit_sum_final(p, nc, bufs, srcs, out_dram):
    a = bufs.sb("fi_x", (128, E), F32)
    t = bufs.sb("fi_t", (128, E), F32)
    for i in range(NT):
        p.dma(a[:, :], srcs[0][i * 128:(i + 1) * 128, :])
        for s in srcs[1:]:
            p.dma(t[:, :], s[i * 128:(i + 1) * 128, :])
            p.op("vector", lambda e: e.tensor_add(a[:, :], a[:, :], t[:, :]))
        p.dma(out_dram[i * 128:(i + 1) * 128, :], a[:, :])


def emit_l1_attn(p, nc, bufs, consts, xnT, WinT, convw_d, convb_d, WxT,
                 WdtT, bdt_d, Alog_d, wgate_d, zT_dram, yw_dram, dbg=None):
    DI = 2048
    cb = load_bias_cols(p, nc, bufs, convb_d, DI, "m_cb")
    bdt = load_col(p, nc, bufs, bdt_d, 0, 128, "m_bdt")
    alog = load_col(p, nc, bufs, Alog_d, 0, 128, "m_alog")
    wg = bufs.sb("m_wg", (1, 1), F32)
    p.dma(wg[:, :], wgate_d[0:1].rearrange("(a b) -> a b", a=1))
    acol = bufs.sb("m_acol", (128, 1), F32)
    p.op("scalar", lambda e: e.activation(acol[:, :], alog[:, :], AF.Exp))
    p.op("vector", lambda e: e.tensor_scalar_mul(acol[:, :], acol[:, :], -1.0))

    xnb = [bufs.sb(f"m_xn{k}", (128, S), BF16) for k in range(8)]
    wsb = bufs.sb("m_w", (128, E), BF16)
    cw = bufs.sb("m_cw", (128, 4), F32)
    cbuf = bufs.sb("m_cbuf", (128, 4 + S), F32)
    acc = [bufs.sb(f"m_acc{k}", (128, S), F32) for k in range(2)]
    xcT_dram = nc.dram_tensor("m_xcT", [2048, S], BF16, kind="Internal").ap()
    xcs = bufs.sb("m_xcs", (128, S), BF16)
    xck = bufs.sb("m_xck", (128, 512), BF16)
    zbf = bufs.sb("m_zbf", (128, 512), BF16)
    wx = bufs.sb("m_wx", (128, 2048), BF16)
    dtT = bufs.sb("m_dtT", (64, S), BF16)
    BmT = bufs.sb("m_BmT", (128, S), F32)
    CmT = bufs.sb("m_CmT", (128, S), F32)
    delta = bufs.sb("m_delta", (128, S), F32)
    u1 = bufs.sb("m_u1", (128, S), F32)
    u2 = bufs.sb("m_u2", (128, S), F32)
    hT = bufs.sb("m_hT", (128, S), F32)
    ywrow = bufs.sb("m_ywrow", (1, S), F32)
    wdt = bufs.sb("m_wdt", (64, 128), BF16)
    p.dma(wdt[:, :], WdtT[:, :])
    ps = bufs.ps("m_ps", (128, 512), F32)
    ps2 = bufs.ps("m_ps2", (128, 512), F32)
    psy = bufs.ps("m_psy", (1, 512), F32)

    for b in range(BL):
        for k in range(8):
            p.dma(xnb[k][:, :], xnT[k * 128:(k + 1) * 128, b * S:(b + 1) * S])
        for dt in range(16):
            for kt in range(8):
                p.dma(wsb[:, kt * 128:(kt + 1) * 128],
                      WinT[kt * 128:(kt + 1) * 128, dt * 128:(dt + 1) * 128])
            p.dma(cw[:, :], convw_d[dt * 128:(dt + 1) * 128, :])
            p.op("vector", lambda e: e.memset(cbuf[:, 0:4], 0.0))
            for c in range(4):
                for kt in range(8):
                    p.op("tensor", lambda e, kt=kt, c=c: e.matmul(
                        ps[:, :], wsb[:, kt * 128:(kt + 1) * 128],
                        xnb[kt][:, c * 512:(c + 1) * 512],
                        start=(kt == 0), stop=(kt == 7)))
                p.op("scalar", lambda e: e.activation(zbf[:, :], ps[:, :], AF.Silu))
                p.dma(zT_dram[dt * 128:(dt + 1) * 128,
                              b * S + c * 512:b * S + (c + 1) * 512], zbf[:, :])
                p.op("vector", lambda e, c=c: e.tensor_copy(
                    cbuf[:, 3 + c * 512:3 + (c + 1) * 512], ps[:, :]))
            p.op("vector", lambda e, dt=dt: e.tensor_scalar(
                acc[0][:, :], cbuf[:, 0:S], cw[:, 0:1],
                cb[:, dt:dt + 1], ALU.mult, ALU.add))
            for j in (1, 2, 3):
                src, dst = acc[(j + 1) % 2], acc[j % 2]
                p.op("vector", lambda e, j=j, src=src, dst=dst: e.scalar_tensor_tensor(
                    dst[:, :], cbuf[:, j:j + S], cw[:, j:j + 1], src[:, :],
                    ALU.mult, ALU.add))
            p.op("vector", lambda e: e.tensor_copy(xcs[:, :], acc[1][:, :]))
            p.dma(xcT_dram[dt * 128:(dt + 1) * 128, :], xcs[:, :])
        for (mlo, msz, dst, dtyp) in ((0, 64, dtT, BF16), (64, 128, BmT, F32),
                                      (192, 128, CmT, F32)):
            for kt in range(16):
                p.dma(wx[0:128, kt * 128:kt * 128 + msz],
                      WxT[kt * 128:(kt + 1) * 128, mlo:mlo + msz])
            for c in range(4):
                for kt in range(16):
                    p.dma(xck[:, :], xcT_dram[kt * 128:(kt + 1) * 128,
                                              c * 512:(c + 1) * 512])
                    p.op("tensor", lambda e, kt=kt, msz=msz: e.matmul(
                        ps[0:msz, :], wx[:, kt * 128:kt * 128 + msz],
                        xck[:, :], start=(kt == 0), stop=(kt == 15)))
                p.op("vector", lambda e, dst=dst, c=c, msz=msz: e.tensor_copy(
                    dst[0:msz, c * 512:(c + 1) * 512], ps[0:msz, :]))
        for c in range(4):
            p.op("tensor", lambda e, c=c: e.matmul(
                ps2[:, :], wdt[:, :], dtT[:, c * 512:(c + 1) * 512],
                start=True, stop=True))
            p.op("vector", lambda e, c=c: e.tensor_scalar(
                u1[:, c * 512:(c + 1) * 512], ps2[:, :], bdt[:, 0:1], 0.0,
                ALU.add, ALU.max))
            p.op("scalar", lambda e, c=c: e.activation(
                u2[:, c * 512:(c + 1) * 512], ps2[:, :], AF.Abs, bias=bdt[:, 0:1]))
        p.op("scalar", lambda e: e.activation(u2[:, :], u2[:, :], AF.Exp, scale=-1.0))
        p.op("scalar", lambda e: e.activation(u2[:, :], u2[:, :], AF.Ln, bias=1.0))
        p.op("vector", lambda e: e.tensor_add(delta[:, :], u1[:, :], u2[:, :]))
        p.op("scalar", lambda e: e.activation(u1[:, :], delta[:, :], AF.Exp,
                                              scale=acol[:, 0:1]))
        p.op("vector", lambda e: e.tensor_mul(u2[:, :], delta[:, :], BmT[:, :]))
        p.op("vector", lambda e: e.tensor_tensor_scan(hT[:, :], u1[:, :],
                                                      u2[:, :], 0.0,
                                                      ALU.mult, ALU.add))
        p.op("vector", lambda e: e.tensor_mul(hT[:, :], hT[:, :], CmT[:, :]))
        for c in range(4):
            p.op("tensor", lambda e, c=c: e.matmul(
                psy[:, :], consts["cols"][:, 0:1], hT[:, c * 512:(c + 1) * 512],
                start=True, stop=True))
            p.op("vector", lambda e, c=c: e.tensor_scalar(
                ywrow[:, c * 512:(c + 1) * 512], psy[:, :], wg[0:1, 0:1], None,
                ALU.mult))
        p.dma(yw_dram[b:b + 1, :], ywrow[:, :])
        if dbg is not None and b == 0:
            for nm, tile in (("xc0", xcs), ("dtT", dtT), ("BmT", BmT),
                             ("CmT", CmT), ("delta", delta), ("Abar", u1),
                             ("hT", hT)):
                if nm in dbg:
                    p.dma(dbg[nm][:, :], tile[:, :] if nm != "dtT"
                          else tile[0:64, :])


def np_l1_attn(xn, W_in, conv_w, conv_b, W_x, W_dt, b_dt, A_log, w_gate):
    DI = 2048
    z = np.zeros((T, DI), dtype=np.float32)
    yw = np.zeros((BL, S), dtype=np.float32)
    for b in range(BL):
        x = xn[b * S:(b + 1) * S].astype(np.float64)
        xi = x @ W_in.T.astype(np.float64)
        xc = np.zeros_like(xi)
        for j in range(4):
            lo = 3 - j
            xc[lo:, :] += xi[:S - lo, :] * conv_w[:, j]
        xc += conv_b
        ssm = xc @ W_x.T.astype(np.float64)
        dt, Bm, Cm = ssm[:, 0:64], ssm[:, 64:192], ssm[:, 192:320]
        da = dt @ W_dt.T.astype(np.float64) + b_dt
        delta = np.where(da > 30, da, np.log1p(np.exp(np.minimum(da, 30.0))))
        A = -np.exp(A_log.astype(np.float64))
        Ab = np.exp(delta * A)
        Bb = delta * Bm
        h = np.zeros((S, 128))
        hp = np.zeros(128)
        for t in range(S):
            hp = Ab[t] * hp + Bb[t]
            h[t] = hp
        y = (Cm * h).sum(-1)
        yw[b] = (y * w_gate[0]).astype(np.float32)
        z[b * S:(b + 1) * S] = (xi / (1 + np.exp(-xi))).astype(np.float32)
    return z, yw


def emit_l1_attn_out(p, nc, bufs, consts, zT_dram, yw_dram, WoutT, a1_dram):
    DI = 2048
    ywt = bufs.sb("ao_ywt", (64, 128), F32)
    ywc = bufs.sb("ao_ywc", (128, 64), F32)
    psy = bufs.ps("ao_psy", (128, 128), F32)
    p.dma(ywt[:, :], yw_dram.rearrange("b (ch p) -> (b ch) p", p=128))
    p.op("tensor", lambda e: e.transpose(psy[0:128, 0:64], ywt[:, :],
                                         consts["identf"][0:64, 0:64]))
    p.op("vector", lambda e: e.tensor_copy(ywc[:, :], psy[0:128, 0:64]))
    wt = [bufs.sb(f"ao_w{k}", (128, E), BF16) for k in range(16)]
    for kt in range(16):
        p.dma(wt[kt][:, :], WoutT[kt * 128:(kt + 1) * 128, :])
    zt = bufs.sb("ao_zt", (128, DI), BF16)
    res = bufs.sb("ao_res", (128, E), F32)
    ps = [bufs.ps(f"ao_ps{n}", (128, 512), F32) for n in range(2)]
    for i in range(NT):
        for kt in range(16):
            p.dma(zt[:, kt * 128:(kt + 1) * 128],
                  zT_dram[kt * 128:(kt + 1) * 128, i * 128:(i + 1) * 128])
        for n in range(2):
            for kt in range(16):
                p.op("tensor", lambda e, n=n, kt=kt: e.matmul(
                    ps[n][:, :], zt[:, kt * 128:(kt + 1) * 128],
                    wt[kt][:, n * 512:(n + 1) * 512],
                    start=(kt == 0), stop=(kt == 15)))
            p.op("vector", lambda e, n=n, i=i: e.tensor_scalar(
                res[:, n * 512:(n + 1) * 512], ps[n][:, :],
                ywc[:, i:i + 1], None, ALU.mult))
        p.dma(a1_dram[i * 128:(i + 1) * 128, :], res[:, :])



def load_bias_cols_partial(p, nc, bufs, dram_vec, lo, n, name):
    nf = (n + 127) // 128
    t = bufs.sb(name, (128, nf), F32)
    full = (n // 128) * 128
    src = dram_vec[lo:lo + full].rearrange("(f p) -> p f", p=128)
    p.dma(t[:, 0:n // 128], src, allow_slow_non_contiguous=True)
    if n % 128:
        src2 = dram_vec[lo + full:lo + n].rearrange("(p one) -> p one", one=1)
        p.dma(t[0:n % 128, nf - 1:nf], src2, allow_slow_non_contiguous=True)
    return t


def emit_l1_ffn(p, nc, bufs, consts, xn1_dram, wqkv, qs_d, ks_d, wo,
                lng_d, lnb_d, w1, b1_d, w2, b2_d, xf_dram, dbg=None):
    HEADS, DH, INNER, HID = 8, 64, 512, 2730
    W = 512
    NW = S // W
    NZ = (HID + 127) // 128
    idb = consts["identb"]
    hT_dram = nc.dram_tensor("ffn_hT", [E, T], BF16, kind="Internal").ap()

    h_t = bufs.sb("f_h", (128, E), F32)
    hb = bufs.sb("f_hb", (128, E), BF16)
    qsr = bufs.sb("f_qsr", (1, 2 * DH), F32)
    ksr = bufs.sb("f_ksr", (1, 2 * DH), F32)
    p.dma(qsr[:, :], qs_d.rearrange("l d -> (l d)").rearrange("(a n) -> a n", a=1))
    p.dma(ksr[:, :], ks_d.rearrange("l d -> (l d)").rearrange("(a n) -> a n", a=1))
    p.op("vector", lambda e: e.tensor_scalar_mul(qsr[:, :], qsr[:, :], 8.0))
    affr = bufs.sb("f_affr", (1, 3 * E), F32)
    b1a = [load_bias_cols_partial(p, nc, bufs, b1_d, l * 2 * HID, HID, f"f_b1a{l}")
           for l in range(2)]
    b1g = [load_bias_cols_partial(p, nc, bufs, b1_d, l * 2 * HID + HID, HID,
                                  f"f_b1g{l}") for l in range(2)]

    wq_sb = bufs.sb("f_wq", (128, E), BF16)
    hTc = [bufs.sb(f"f_hTc{k}", (128, 512), BF16) for k in range(8)]
    qkvT = [bufs.sb(f"f_qkvT{j}", (128, S), BF16) for j in range(12)]
    qf = bufs.sb("f_qf", (128, 512), F32)
    sq = bufs.sb("f_sq", (128, 512), F32)
    n2r = [bufs.sb(f"f_n2r{h}", (1, 512), F32) for h in range(2)]
    vtok = bufs.sb("f_vtok", (128, 8 * 64), BF16)
    pa = bufs.sb("f_pa", (128, 512), BF16)
    pb = bufs.sb("f_pb", (128, 128), BF16)
    pt = bufs.sb("f_pt", (128, 5 * 128), BF16)
    oTw = [bufs.sb(f"f_oTw{j}", (128, 512), BF16) for j in range(4)]
    den = bufs.sb("f_den", (128, 4), F32)
    denr = bufs.sb("f_denr", (1, 128), F32)
    bca = bufs.sb("f_bca", (128, 128), F32)
    wo_sb = [bufs.sb(f"f_wo{k}", (128, E), BF16) for k in range(4)]
    xres = bufs.sb("f_xres", (128, E), F32)
    zt = [bufs.sb(f"f_z{m}", (128, 512), BF16) for m in range(NZ)]
    hnT = [bufs.sb(f"f_hnT{k}", (128, 512), BF16) for k in range(8)]
    w1sb = bufs.sb("f_w1sb", (128, 1024), BF16)
    w2sb = bufs.sb("f_w2sb", (128, 1024), BF16)
    gg = bufs.sb("f_gg", (128, 512), F32)
    gbc = bufs.sb("f_gbc", (128, 2 * E), F32)

    dbgS_sb = bufs.sb("f_dbgS", (128, 128), F32) if dbg is not None else None
    ps1 = bufs.ps("f_ps1", (128, 512), F32)
    ps2 = bufs.ps("f_ps2", (128, 512), F32)
    psS = bufs.ps("f_psS", (128, 512), F32)
    psB = bufs.ps("f_psB", (128, 128), F32)
    pst = bufs.ps("f_pst", (128, 128), BF16)
    psmini = bufs.ps("f_psmini", (128, 512), F32)
    psav = bufs.ps("f_psav", (128, 128), F32)

    for l in range(2):
        src = xn1_dram if l == 0 else xf_dram
        for i in range(NT):
            p.dma(h_t[:, 0:512], src[i * 128:(i + 1) * 128, 0:512])
            if i % (S // 128) == 0:
                p.op("vector", lambda e: e.memset(h_t[0:1, 512:1024], 0.0))
                p.dma(h_t[1:128, 512:1024],
                      src[i * 128:(i + 1) * 128 - 1, 512:1024])
            else:
                p.dma(h_t[:, 512:1024],
                      src[i * 128 - 1:(i + 1) * 128 - 1, 512:1024])
            p.op("vector", lambda e: e.tensor_copy(hb[:, :], h_t[:, :]))
            emit_transpose_store(p, nc, bufs, hb, hT_dram, 0, i * 128, 8, idb,
                                 pref="f_")
        for b in range(BL):
            for c in range(4):
                for k in range(8):
                    p.dma(hTc[k][:, :], hT_dram[k * 128:(k + 1) * 128,
                                                b * S + c * 512:b * S + (c + 1) * 512])
                for fo in range(12):
                    for kt in range(8):
                        p.dma(wq_sb[:, kt * 128:(kt + 1) * 128],
                              wqkv[l * E + kt * 128:l * E + (kt + 1) * 128,
                                   fo * 128:(fo + 1) * 128])
                    for kt in range(8):
                        p.op("tensor", lambda e, kt=kt: e.matmul(
                            ps1[:, :], wq_sb[:, kt * 128:(kt + 1) * 128],
                            hTc[kt][:, :], start=(kt == 0), stop=(kt == 7)))
                    if fo >= 8:
                        p.op("vector", lambda e, fo=fo, c=c: e.tensor_copy(
                            qkvT[fo][:, c * 512:(c + 1) * 512], ps1[:, :]))
                        continue
                    p.op("vector", lambda e: e.tensor_copy(qf[:, :], ps1[:, :]))
                    isq = fo < 4
                    gr = qsr if isq else ksr
                    rA = consts["rot_Aq"] if isq else consts["rot_Ak"]
                    rB = consts["rot_Bq"] if isq else consts["rot_Bk"]
                    p.op("scalar", lambda e: e.activation(sq[:, :], qf[:, :],
                                                          AF.Square))
                    for hh in range(2):
                        p.op("tensor", lambda e, hh=hh: e.matmul(
                            psmini[0:1, :], consts["cols"][:, 1 + hh:2 + hh],
                            sq[:, :], start=True, stop=True))
                        p.op("vector", lambda e, hh=hh: e.tensor_copy(
                            n2r[hh][:, :], psmini[0:1, :]))
                        p.op("scalar", lambda e, hh=hh: e.activation(
                            n2r[hh][:, :], n2r[hh][:, :], AF.Sqrt))
                        p.op("vector", lambda e, hh=hh: e.tensor_scalar_max(
                            n2r[hh][:, :], n2r[hh][:, :], 1e-12))
                        p.op("vector", lambda e, hh=hh: e.reciprocal(
                            n2r[hh][:, :], n2r[hh][:, :]))
                        p.op("tensor", lambda e, hh=hh, gr=gr, l=l: e.matmul(
                            ps2[hh * 64:hh * 64 + 64, :],
                            gr[0:1, l * 64:(l + 1) * 64], n2r[hh][:, :],
                            start=True, stop=True))
                    cs = slice(c * 512, (c + 1) * 512)
                    p.op("vector", lambda e: e.tensor_tensor(
                        ps1[:, :], qf[:, :], ps2[:, :], ALU.mult))
                    p.op("vector", lambda e, cs=cs, rA=rA: e.tensor_tensor(
                        sq[:, :], ps1[:, :], rA[:, cs], ALU.mult))
                    for qr in range(4):
                        sw = (qr + 1) if qr % 2 == 0 else (qr - 1)
                        p.op("vector", lambda e, qr=qr, sw=sw, cs=cs, rB=rB: e.tensor_tensor(
                            qf[qr * 32:(qr + 1) * 32, :],
                            ps1[sw * 32:(sw + 1) * 32, :],
                            rB[qr * 32:(qr + 1) * 32, cs], ALU.mult))
                    p.op("vector", lambda e, fo=fo, cs=cs: e.tensor_tensor(
                        qkvT[fo][:, cs], sq[:, :], qf[:, :], ALU.add))
            for w in range(NW):
                jt0 = max(0, (w - 1) * 4)
                njt = 4 if w == 0 else 8
                for j in range(4):
                    p.op("vector", lambda e, j=j: e.memset(oTw[j][:, :], 0.0))
                for h in range(HEADS):
                    ft, base = h // 2, (h % 2) * 64
                    qT = qkvT[ft]
                    kT = qkvT[4 + ft]
                    vT = qkvT[8 + ft]
                    for s_ in range(njt):
                        jt = jt0 + s_
                        p.op("tensor", lambda e, jt=jt, vT=vT, base=base: e.transpose(
                            pst[0:128, 0:64],
                            vT[base:base + 64, jt * 128:(jt + 1) * 128],
                            consts["identb64"][base:base + 64, 0:64]))
                        p.op("vector", lambda e, s_=s_: e.tensor_copy(
                            vtok[:, s_ * 64:(s_ + 1) * 64], pst[0:128, 0:64]))
                    for r in range(4):
                        q0 = w * W + r * 128
                        if w == 0:
                            kj0l = 0
                            nk = (r + 1) * 128
                        else:
                            kj0l = r * 128
                            nk = 640
                        kjg = jt0 * 128 + kj0l
                        nk1 = min(nk, 512)
                        p.op("tensor", lambda e, base=base, q0=q0, kjg=kjg, nk1=nk1, qT=qT, kT=kT: e.matmul(
                            psS[:, 0:nk1], qT[base:base + 64, q0:q0 + 128],
                            kT[base:base + 64, kjg:kjg + nk1],
                            start=True, stop=False, skip_group_check=True))
                        if nk > 512:
                            p.op("tensor", lambda e, base=base, q0=q0, kjg=kjg, qT=qT, kT=kT: e.matmul(
                                psB[:, :], qT[base:base + 64, q0:q0 + 128],
                                kT[base:base + 64, kjg + 512:kjg + 640],
                                start=True, stop=False, skip_group_check=True))
                        if w > 0:
                            p.op("tensor", lambda e: e.matmul(
                                psS[:, 0:128], idb[:, :], consts["maskLb"][:, :],
                                start=False, stop=True, skip_group_check=True))
                            p.op("tensor", lambda e: e.matmul(
                                psB[:, :], idb[:, :], consts["maskRb"][:, :],
                                start=False, stop=True, skip_group_check=True))
                        else:
                            p.op("tensor", lambda e, r=r: e.matmul(
                                psS[:, r * 128:(r + 1) * 128], idb[:, :],
                                consts["maskRb"][:, :],
                                start=False, stop=True, skip_group_check=True))
                        if (dbg is not None and "S00" in dbg and l == 0
                                and b == 0 and w == 0 and h == 0 and r == 0):
                            p.op("vector", lambda e: e.tensor_copy(
                                dbgS_sb[:, :], psS[:, 0:128]))
                            p.dma(dbg["S00"][:, :], dbgS_sb[:, :])
                        p.op("vector", lambda e, nk1=nk1: e.reduce_max(
                            den[:, 0:1], psS[:, 0:nk1], AX.X))
                        if nk > 512:
                            p.op("vector", lambda e: e.reduce_max(
                                den[:, 1:2], psB[:, :], AX.X))
                            p.op("vector", lambda e: e.tensor_max(
                                den[:, 0:1], den[:, 0:1], den[:, 1:2]))
                        p.op("vector", lambda e: e.tensor_scalar_mul(
                            den[:, 2:3], den[:, 0:1], -1.0))
                        p.op("scalar", lambda e, nk1=nk1: e.activation(
                            pa[:, 0:nk1], psS[:, 0:nk1], AF.Exp,
                            bias=den[:, 2:3], accum_out=den[:, 0:1]))
                        if nk > 512:
                            p.op("scalar", lambda e: e.activation(
                                pb[:, :], psB[:, :], AF.Exp,
                                bias=den[:, 2:3], accum_out=den[:, 1:2]))
                            p.op("vector", lambda e: e.tensor_add(
                                den[:, 0:1], den[:, 0:1], den[:, 1:2]))
                        p.op("vector", lambda e: e.reciprocal(
                            den[:, 3:4], den[:, 0:1]))
                        ntile = nk // 128
                        for s_ in range(ntile):
                            srcP = pa[:, (s_ % 4) * 128:(s_ % 4) * 128 + 128] \
                                if s_ < 4 else pb[:, 0:128]
                            p.op("tensor", lambda e, srcP=srcP: e.transpose(
                                pst[:, :], srcP, idb[:, :]))
                            p.op("vector", lambda e, s_=s_: e.tensor_copy(
                                pt[:, s_ * 128:(s_ + 1) * 128], pst[:, :]))
                        p.op("tensor", lambda e: e.transpose(
                            psmini[0:1, 0:128], den[:, 3:4],
                            consts["identf"][:, :]))
                        p.op("vector", lambda e: e.tensor_copy(
                            denr[:, :], psmini[0:1, 0:128]))
                        p.op("tensor", lambda e, base=base: e.matmul(
                            psmini[base:base + 64, 0:128],
                            consts["one_row"][0:1, 0:64], denr[:, :],
                            start=True, stop=True))
                        p.op("vector", lambda e, base=base: e.tensor_copy(
                            bca[base:base + 64, :],
                            psmini[base:base + 64, 0:128]))
                        for s_ in range(ntile):
                            sloc = (kj0l // 128) + s_ if w > 0 else s_
                            p.op("tensor", lambda e, s_=s_, sloc=sloc, base=base, ntile=ntile: e.matmul(
                                psav[base:base + 64, :],
                                vtok[:, sloc * 64:(sloc + 1) * 64],
                                pt[:, s_ * 128:(s_ + 1) * 128],
                                start=(s_ == 0), stop=(s_ == ntile - 1)))
                        p.op("vector", lambda e, ft=ft, base=base, r=r: e.tensor_tensor(
                            oTw[ft][base:base + 64, r * 128:(r + 1) * 128],
                            psav[base:base + 64, :], bca[base:base + 64, :],
                            ALU.mult))
                if w == 0 and b == 0:
                    for k in range(4):
                        p.dma(wo_sb[k][:, :],
                              wo[l * 512 + k * 128:l * 512 + (k + 1) * 128, :])
                for sc in range(4):
                    gi = b * (S // 128) + w * 4 + sc
                    p.dma(xres[:, :], src[gi * 128:(gi + 1) * 128, :])
                    for n in range(2):
                        for k in range(4):
                            p.op("tensor", lambda e, n=n, k=k, sc=sc: e.matmul(
                                ps1[:, :], oTw[k][:, sc * 128:(sc + 1) * 128],
                                wo_sb[k][:, n * 512:(n + 1) * 512],
                                start=(k == 0), stop=(k == 3)))
                        p.op("vector", lambda e, n=n: e.tensor_tensor(
                            xres[:, n * 512:(n + 1) * 512], ps1[:, :],
                            xres[:, n * 512:(n + 1) * 512], ALU.add))
                    p.dma(xf_dram[gi * 128:(gi + 1) * 128, :], xres[:, :])
        if dbg is not None and l == 0:
            for nm, tile in (("q0", qkvT[0]), ("k0", qkvT[4]), ("v0", qkvT[8])):
                if nm in dbg:
                    p.dma(dbg[nm][:, :], tile[:, :])
            if "xattn0" in dbg:
                for i in range(NT):
                    p.dma(dbg["xattn0"][i * 128:(i + 1) * 128, :],
                          xf_dram[i * 128:(i + 1) * 128, :])
        p.dma(affr[0:1, 0:E], lng_d[l:l + 1, :])
        p.dma(affr[0:1, E:2 * E], lnb_d[l:l + 1, :])
        p.dma(affr[0:1, 2 * E:3 * E], b2_d[l:l + 1, :])
        for n in range(2):
            p.op("tensor", lambda e, n=n: e.matmul(
                ps1[:, :], consts["one_row"][:, :],
                affr[0:1, n * 512:(n + 1) * 512], start=True, stop=True))
            p.op("vector", lambda e, n=n: e.tensor_copy(
                gbc[:, n * 512:(n + 1) * 512], ps1[:, :]))
            p.op("tensor", lambda e, n=n: e.matmul(
                ps1[:, :], consts["one_row"][:, :],
                affr[0:1, E + n * 512:E + (n + 1) * 512], start=True, stop=True))
            p.op("vector", lambda e, n=n: e.tensor_copy(
                gbc[:, E + n * 512:E + (n + 1) * 512], ps1[:, :]))
        for g in range(T // 512):
            for sc in range(4):
                gi = g * 4 + sc
                p.dma(h_t[:, 0:512], xf_dram[gi * 128:(gi + 1) * 128, 0:512])
                if gi % (S // 128) == 0:
                    p.op("vector", lambda e: e.memset(h_t[0:1, 512:1024], 0.0))
                    p.dma(h_t[1:128, 512:1024],
                          xf_dram[gi * 128:(gi + 1) * 128 - 1, 512:1024])
                else:
                    p.dma(h_t[:, 512:1024],
                          xf_dram[gi * 128 - 1:(gi + 1) * 128 - 1, 512:1024])
                emit_layernorm_tile(p, nc, bufs, h_t, h_t, 1e-5, pref="f_")
                p.op("vector", lambda e: e.tensor_tensor(
                    h_t[:, :], h_t[:, :], gbc[:, 0:E], ALU.mult))
                p.op("vector", lambda e: e.tensor_tensor(
                    h_t[:, :], h_t[:, :], gbc[:, E:2 * E], ALU.add))
                p.op("vector", lambda e: e.tensor_copy(hb[:, :], h_t[:, :]))
                for f in range(8):
                    p.op("tensor", lambda e, f=f: e.transpose(
                        pst[:, :], hb[:, f * 128:(f + 1) * 128], idb[:, :]))
                    p.op("vector", lambda e, f=f, sc=sc: e.tensor_copy(
                        hnT[f][:, sc * 128:(sc + 1) * 128], pst[:, :]))
            for m in range(NZ):
                mrows = min(128, HID - m * 128)
                for part, psx in ((0, ps1), (1, ps2)):
                    col0 = l * 2 * HID + part * HID + m * 128
                    for kt in range(8):
                        p.dma(w1sb[:, kt * 128:kt * 128 + mrows],
                              w1[kt * 128:(kt + 1) * 128, col0:col0 + mrows])
                        p.op("tensor", lambda e, kt=kt, mrows=mrows, psx=psx: e.matmul(
                            psx[0:mrows, :], w1sb[:, kt * 128:kt * 128 + mrows],
                            hnT[kt][:, :], start=(kt == 0), stop=(kt == 7)))
                p.op("scalar", lambda e, m=m, mrows=mrows, l=l: e.activation(
                    gg[0:mrows, :], ps2[0:mrows, :], AF.Gelu,
                    bias=b1g[l][0:mrows, m:m + 1]))
                p.op("vector", lambda e, m=m, mrows=mrows, l=l: e.scalar_tensor_tensor(
                    zt[m][0:mrows, :], ps1[0:mrows, :], b1a[l][0:mrows, m:m + 1],
                    gg[0:mrows, :], ALU.add, ALU.mult))
            for sc in range(4):
                gi = g * 4 + sc
                p.dma(xres[:, :], xf_dram[gi * 128:(gi + 1) * 128, :])
                for n in range(2):
                    for m in range(NZ):
                        mrows = min(128, HID - m * 128)
                        p.dma(w2sb[0:mrows, n * 512:(n + 1) * 512],
                              w2[l * HID + m * 128:l * HID + m * 128 + mrows,
                                 n * 512:(n + 1) * 512])
                        p.op("tensor", lambda e, m=m, mrows=mrows, sc=sc, n=n: e.matmul(
                            ps1[:, :], zt[m][0:mrows, sc * 128:(sc + 1) * 128],
                            w2sb[0:mrows, n * 512:(n + 1) * 512],
                            start=(m == 0), stop=False, skip_group_check=True))
                    p.op("tensor", lambda e, n=n: e.matmul(
                        ps1[:, :], consts["one_row"][:, :],
                        affr[0:1, 2 * E + n * 512:2 * E + (n + 1) * 512],
                        start=False, stop=True, skip_group_check=True))
                    p.op("vector", lambda e, n=n: e.tensor_tensor(
                        xres[:, n * 512:(n + 1) * 512], ps1[:, :],
                        xres[:, n * 512:(n + 1) * 512], ALU.add))
                p.dma(xf_dram[gi * 128:(gi + 1) * 128, :], xres[:, :])
        if dbg is not None and l == 0 and "xf0" in dbg:
            for i in range(NT):
                p.dma(dbg["xf0"][i * 128:(i + 1) * 128, :],
                      xf_dram[i * 128:(i + 1) * 128, :])


def emit_l0_ffn(p, nc, bufs, consts, xn_dram, wdqT, bias_d, f0_dram):
    idb = consts["identb"]
    wt = [bufs.sb(f"lf_w{k}", (128, E), BF16) for k in range(8)]
    for kt in range(8):
        p.dma(wt[kt][:, :], wdqT[kt * 128:(kt + 1) * 128, :])
    brow = bufs.sb("lf_brow", (1, E), F32)
    p.dma(brow[:, :], bias_d[0:E].rearrange("(one n) -> one n", one=1))
    a = bufs.sb("lf_a", (128, E), F32)
    ab = bufs.sb("lf_ab", (128, E), BF16)
    xdT = bufs.sb("lf_xdT", (128, E), BF16)
    res = bufs.sb("lf_res", (128, E), F32)
    pst = bufs.ps("lf_pst", (128, 128), BF16)
    ps = [bufs.ps(f"lf_ps{n}", (128, 512), F32) for n in range(2)]
    for i in range(NT):
        p.dma(a[:, :], xn_dram[i * 128:(i + 1) * 128, :])
        emit_quant_tile(p, nc, bufs, a, a, pref="lf_q_")
        p.op("vector", lambda e: e.tensor_copy(ab[:, :], a[:, :]))
        for f in range(8):
            p.op("tensor", lambda e, f=f: e.transpose(
                pst[:, :], ab[:, f * 128:(f + 1) * 128], idb[:, :]))
            p.op("vector", lambda e, f=f: e.tensor_copy(
                xdT[:, f * 128:(f + 1) * 128], pst[:, :]))
        for n in range(2):
            for kt in range(8):
                p.op("tensor", lambda e, n=n, kt=kt: e.matmul(
                    ps[n][:, :], xdT[:, kt * 128:(kt + 1) * 128],
                    wt[kt][:, n * 512:(n + 1) * 512],
                    start=(kt == 0), stop=False, skip_group_check=True))
            p.op("tensor", lambda e, n=n: e.matmul(
                ps[n][:, :], consts["one_row"][:, :],
                brow[:, n * 512:(n + 1) * 512],
                start=False, stop=True, skip_group_check=True))
            p.op("vector", lambda e, n=n: e.tensor_copy(
                res[:, n * 512:(n + 1) * 512], ps[n][:, :]))
        p.dma(f0_dram[i * 128:(i + 1) * 128, :], res[:, :])


def emit_prep(p, nc, bufs, consts, ins, outs):
    idb = consts["identb"]
    for nm in ("qw", "kw", "vw", "outw"):
        emit_cast_bf16(p, nc, bufs, ins[nm], outs[nm + "_bf"], E, E)
    emit_cast_bf16(p, nc, bufs, ins["wqkv"], outs["wqkv_bf"], 2 * E, 1536)
    emit_cast_bf16(p, nc, bufs, ins["wo"], outs["wo_bf"], E, E)
    emit_cast_bf16(p, nc, bufs, ins["w1"], outs["w1_bf"], E, 2 * 5460)
    emit_cast_bf16(p, nc, bufs, ins["w2"], outs["w2_bf"], 5460, E)
    emit_transpose_bf16(p, nc, bufs, ins["ffnw"], outs["wdqT"], E, E, idb,
                        quant=True, pref="pq_")
    emit_transpose_bf16(p, nc, bufs, ins["W_in"], outs["WinT"], 2048, E, idb,
                        pref="p1_")
    emit_transpose_bf16(p, nc, bufs, ins["W_x"], outs["WxT"], 320, 2048, idb,
                        pref="p2_")
    emit_transpose_bf16(p, nc, bufs, ins["W_dt"], outs["WdtT"], 128, 64, idb,
                        pref="p3_")
    emit_transpose_bf16(p, nc, bufs, ins["W_out"], outs["WoutT"], E, 2048, idb,
                        pref="p4_")


from contextlib import ExitStack


def _dram(nc, name, shape, dt, kind):
    return nc.dram_tensor(name, list(shape), dt, kind=kind).ap()


def _run_launch(build_fn, in_specs, out_specs, in_arrays_per_core, cneed,
                cvals):
    nc = bass.Bass("TRN2", target_bir_lowering=False)
    ins = {k: _dram(nc, k, sh, dt, "ExternalInput")
           for k, (sh, dt) in in_specs.items()}
    outs = {k: _dram(nc, k, sh, dt, "ExternalOutput")
            for k, (sh, dt) in out_specs.items()}
    p = P(nc)
    with ExitStack() as stk:
        bufs = Bufs(nc, stk)
        consts = load_constants(p, nc, bufs, cvals, cneed)
        build_fn(p, nc, bufs, consts, ins, outs)
        p.emit()
    in_maps = []
    for c in range(NCORES):
        m = dict(in_arrays_per_core[c])
        for k in cneed:
            m["c_" + k] = cvals[k]
        in_maps.append(m)
    r = run_bass_kernel_spmd(nc, in_maps, core_ids=list(range(NCORES)))
    return r.results


def kernel(**inputs):
    cvals = make_constants()
    x_full = np.ascontiguousarray(
        inputs["L0_attn_x"].reshape(32 * S, E).astype(np.float32))
    shards = [np.ascontiguousarray(x_full[c * T:(c + 1) * T])
              for c in range(NCORES)]

    f32c = np.ascontiguousarray
    wq = f32c(inputs["L0_attn_q_w"]); wk = f32c(inputs["L0_attn_k_w"])
    wv = f32c(inputs["L0_attn_v_w"]); wout = f32c(inputs["L0_attn_out_w"])
    ffnw = f32c(inputs["L0_ffn_weight"])
    W_in = f32c(inputs["L1_attn_W_in"])
    conv_w = f32c(inputs["L1_attn_conv_w"].reshape(2048, 4))
    W_x = f32c(inputs["L1_attn_W_x"])
    W_dt = f32c(inputs["L1_attn_W_dt"])
    wqkv = f32c(inputs["L1_ffn_wqkv"].reshape(2 * E, 1536))
    wo = f32c(inputs["L1_ffn_wo"].reshape(E, E))
    w1 = f32c(np.transpose(inputs["L1_ffn_w1"], (1, 0, 2)).reshape(E, 2 * 5460))
    w2 = f32c(inputs["L1_ffn_w2"].reshape(2 * 2730, E))
    W_out = f32c(inputs["L1_attn_W_out"])

    BF16s = lambda sh: (sh, BF16)
    F32s = lambda sh: (sh, F32)

    prep_in = {"qw": F32s([E, E]), "kw": F32s([E, E]), "vw": F32s([E, E]),
               "outw": F32s([E, E]), "ffnw": F32s([E, E]),
               "W_in": F32s([2048, E]), "W_x": F32s([320, 2048]),
               "W_dt": F32s([128, 64]), "W_out": F32s([E, 2048]),
               "wqkv": F32s([2 * E, 1536]), "wo": F32s([E, E]),
               "w1": F32s([E, 2 * 5460]), "w2": F32s([5460, E])}
    prep_out = {"qw_bf": BF16s([E, E]), "kw_bf": BF16s([E, E]),
                "vw_bf": BF16s([E, E]), "outw_bf": BF16s([E, E]),
                "wdqT": BF16s([E, E]), "WinT": BF16s([E, 2048]),
                "WxT": BF16s([2048, 320]), "WdtT": BF16s([64, 128]),
                "WoutT": BF16s([2048, E]), "wqkv_bf": BF16s([2 * E, 1536]),
                "wo_bf": BF16s([E, E]), "w1_bf": BF16s([E, 2 * 5460]),
                "w2_bf": BF16s([5460, E])}
    prep_arrs = {"qw": wq, "kw": wk, "vw": wv, "outw": wout, "ffnw": ffnw,
                 "W_in": W_in, "W_x": W_x, "W_dt": W_dt, "W_out": W_out,
                 "wqkv": wqkv, "wo": wo, "w1": w1, "w2": w2}
    rp = _run_launch(
        lambda p, nc, bufs, consts, ins, outs: emit_prep(p, nc, bufs, consts,
                                                         ins, outs),
        prep_in, prep_out, [prep_arrs] * NCORES, {"identb", "one_row"}, cvals)

    r2 = _run_launch(
        lambda p, nc, bufs, consts, ins, outs: emit_ln0(
            p, nc, bufs, consts, ins["x"], outs["xn"], outs["xnT"]),
        {"x": F32s([T, E])},
        {"xn": F32s([T, E]), "xnT": BF16s([E, T])},
        [{"x": shards[c]} for c in range(NCORES)], {"identb"}, cvals)

    r3 = _run_launch(
        lambda p, nc, bufs, consts, ins, outs: emit_l0_attn(
            p, nc, bufs, consts, ins["xnT"], ins["qw"], ins["kw"], ins["vw"],
            ins["qb"], ins["kb"], ins["vb"], ins["proj"], outs["attnT"]),
        {"xnT": BF16s([E, T]), "qw": BF16s([E, E]), "kw": BF16s([E, E]),
         "vw": BF16s([E, E]), "qb": F32s([E]), "kb": F32s([E]),
         "vb": F32s([E]), "proj": F32s([256, 64])},
        {"attnT": BF16s([E, T])},
        [{"xnT": r2[c]["xnT"], "qw": rp[c]["qw_bf"], "kw": rp[c]["kw_bf"],
          "vw": rp[c]["vw_bf"], "qb": inputs["L0_attn_q_b"],
          "kb": inputs["L0_attn_k_b"], "vb": inputs["L0_attn_v_b"],
          "proj": f32c(inputs["L0_attn_proj"])} for c in range(NCORES)],
        {"identb", "identf", "identb64", "triu_mult", "cols", "negone_row"},
        cvals)

    r4 = _run_launch(
        lambda p, nc, bufs, consts, ins, outs: emit_l0_post(
            p, nc, bufs, consts, ins["attnT"], ins["outw"], ins["outb"],
            ins["xn"], ins["gamma"], ins["beta"], outs["a0"]),
        {"attnT": BF16s([E, T]), "outw": BF16s([E, E]), "outb": F32s([E]),
         "xn": F32s([T, E]), "gamma": F32s([E]), "beta": F32s([E])},
        {"a0": F32s([T, E])},
        [{"attnT": r3[c]["attnT"], "outw": rp[c]["outw_bf"],
          "outb": inputs["L0_attn_out_b"], "xn": r2[c]["xn"],
          "gamma": inputs["L0_attn_gamma"], "beta": inputs["L0_attn_beta"]}
         for c in range(NCORES)], {"one_row"}, cvals)

    r5 = _run_launch(
        lambda p, nc, bufs, consts, ins, outs: emit_l0_ffn(
            p, nc, bufs, consts, ins["xn"], ins["wdqT"], ins["fb"],
            outs["f0"]),
        {"xn": F32s([T, E]), "wdqT": BF16s([E, E]), "fb": F32s([E])},
        {"f0": F32s([T, E])},
        [{"xn": r2[c]["xn"], "wdqT": rp[c]["wdqT"],
          "fb": inputs["L0_ffn_bias"]} for c in range(NCORES)],
        {"identb", "one_row"}, cvals)

    r6 = _run_launch(
        lambda p, nc, bufs, consts, ins, outs: emit_combine_ln(
            p, nc, bufs, consts, [ins["x"], ins["a0"], ins["f0"]],
            outs["x1"], outs["xn1"], outs["xn1T"]),
        {"x": F32s([T, E]), "a0": F32s([T, E]), "f0": F32s([T, E])},
        {"x1": F32s([T, E]), "xn1": F32s([T, E]), "xn1T": BF16s([E, T])},
        [{"x": shards[c], "a0": r4[c]["a0"], "f0": r5[c]["f0"]}
         for c in range(NCORES)], {"identb"}, cvals)

    r7 = _run_launch(
        lambda p, nc, bufs, consts, ins, outs: emit_l1_attn(
            p, nc, bufs, consts, ins["xn1T"], ins["WinT"], ins["convw"],
            ins["convb"], ins["WxT"], ins["WdtT"], ins["bdt"], ins["Alog"],
            ins["wg"], outs["zT"], outs["yw"]),
        {"xn1T": BF16s([E, T]), "WinT": BF16s([E, 2048]),
         "convw": F32s([2048, 4]), "convb": F32s([2048]),
         "WxT": BF16s([2048, 320]), "WdtT": BF16s([64, 128]),
         "bdt": F32s([128]), "Alog": F32s([128]), "wg": F32s([1])},
        {"zT": BF16s([2048, T]), "yw": F32s([BL, S])},
        [{"xn1T": r6[c]["xn1T"], "WinT": rp[c]["WinT"], "convw": conv_w,
          "convb": inputs["L1_attn_conv_b"], "WxT": rp[c]["WxT"],
          "WdtT": rp[c]["WdtT"], "bdt": inputs["L1_attn_b_dt"],
          "Alog": inputs["L1_attn_A_log"], "wg": inputs["L1_attn_w_gate"]}
         for c in range(NCORES)], {"cols", "identf"}, cvals)

    r8 = _run_launch(
        lambda p, nc, bufs, consts, ins, outs: emit_l1_attn_out(
            p, nc, bufs, consts, ins["zT"], ins["yw"], ins["WoutT"],
            outs["a1"]),
        {"zT": BF16s([2048, T]), "yw": F32s([BL, S]),
         "WoutT": BF16s([2048, E])},
        {"a1": F32s([T, E])},
        [{"zT": r7[c]["zT"], "yw": r7[c]["yw"], "WoutT": rp[c]["WoutT"]}
         for c in range(NCORES)], {"identf"}, cvals)

    r9 = _run_launch(
        lambda p, nc, bufs, consts, ins, outs: emit_l1_ffn(
            p, nc, bufs, consts, ins["xn1"], ins["wqkv"], ins["qs"],
            ins["ks"], ins["wo"], ins["lng"], ins["lnb"], ins["w1"],
            ins["b1"], ins["w2"], ins["b2"], outs["xf"]),
        {"xn1": F32s([T, E]), "wqkv": BF16s([2 * E, 1536]),
         "qs": F32s([2, 64]), "ks": F32s([2, 64]), "wo": BF16s([E, E]),
         "lng": F32s([2, E]), "lnb": F32s([2, E]),
         "w1": BF16s([E, 2 * 5460]), "b1": F32s([2 * 5460]),
         "w2": BF16s([5460, E]), "b2": F32s([2, E])},
        {"xf": F32s([T, E])},
        [{"xn1": r6[c]["xn1"], "wqkv": rp[c]["wqkv_bf"],
          "qs": f32c(inputs["L1_ffn_q_scale"]),
          "ks": f32c(inputs["L1_ffn_k_scale"]), "wo": rp[c]["wo_bf"],
          "lng": f32c(inputs["L1_ffn_ln_g"]), "lnb": f32c(inputs["L1_ffn_ln_b"]),
          "w1": rp[c]["w1_bf"], "b1": f32c(inputs["L1_ffn_b1"].reshape(-1)),
          "w2": rp[c]["w2_bf"], "b2": f32c(inputs["L1_ffn_b2"])}
         for c in range(NCORES)],
        {"identb", "identf", "identb64", "maskLb", "maskRb",
         "rot_Aq", "rot_Bq", "rot_Ak", "rot_Bk", "cols", "one_row"}, cvals)

    r10 = _run_launch(
        lambda p, nc, bufs, consts, ins, outs: emit_sum_final(
            p, nc, bufs, [ins["x1"], ins["a1"], ins["xf"]], outs["out"]),
        {"x1": F32s([T, E]), "a1": F32s([T, E]), "xf": F32s([T, E])},
        {"out": F32s([T, E])},
        [{"x1": r6[c]["x1"], "a1": r8[c]["a1"], "xf": r9[c]["xf"]}
         for c in range(NCORES)], set(), cvals)

    out = np.concatenate([r10[c]["out"] for c in range(NCORES)], axis=0)
    return out.reshape(32, S, E).astype(np.float32)


class _RecInst:
    def then_inc(self, *a, **k):
        return self


class _Recorder:

    def __init__(self):
        self.call = None

    def __getattr__(self, name):
        def f(*args, **kwargs):
            assert self.call is None, "op fn must emit exactly one instruction"
            self.call = (name, args, kwargs)
            return _RecInst()
        return f


def _ap_name(a):
    try:
        return a.tensor.name
    except Exception:
        return None


def _ap_is_psum(a):
    try:
        return "PSUM" in str(a.space).upper() or "PSUM" in str(
            getattr(a.tensor, "space", "")).upper()
    except Exception:
        return False


class P2:
    ENGINES = ("sync", "scalar", "vector", "tensor", "gpsimd")
    DMAQS = ("sync", "gpsimd")

    def __init__(self, nc, same_engine_edges=True):
        self.nc = nc
        self.ops = []
        self.last_w = {}
        self.readers = {}
        self.counts = {e: 0 for e in self.ENGINES}
        self.wcount = {}
        self.qof = {}
        self.see = same_engine_edges
        self.rr = 0
        self.barrier_pending = None
        self.barriers = []

    def _dep_events(self, reads, writes):
        deps = []
        for nm in reads:
            ev = self.last_w.get(nm)
            if ev:
                deps.append(ev)
        for nm in writes:
            ev = self.last_w.get(nm)
            if ev:
                deps.append(ev)
            deps.extend(self.readers.get(nm, ()))
        return deps

    def _record(self, engine, call, is_dma, queue=None):
        name, args, kwargs = call
        writes, reads = [], []
        out_kw = kwargs.get("out")
        if out_kw is not None and _ap_name(out_kw):
            writes.append(_ap_name(out_kw))
        acc = kwargs.get("accum_out")
        if acc is not None and _ap_name(acc):
            writes.append(_ap_name(acc))
        psum_reads = []
        for i, a in enumerate(args):
            nm = _ap_name(a)
            if nm is None:
                continue
            if i == 0 and out_kw is None:
                writes.append(nm)
            else:
                reads.append(nm)
                if _ap_is_psum(a):
                    psum_reads.append(nm)
        for k, v in kwargs.items():
            if k in ("out", "accum_out"):
                continue
            nm = _ap_name(v)
            if nm is not None:
                reads.append(nm)
                if _ap_is_psum(v):
                    psum_reads.append(nm)
        deps = self._dep_events(reads, writes)
        for nm in psum_reads:
            for r in self.readers.get(nm, ()):
                if r[0] == "c" and r[1] != engine:
                    deps.append(r)
        if self.barrier_pending is not None:
            deps = deps + [("b", self.barrier_pending, 0)]
        if is_dma:
            wn = writes[0]
            self.wcount[wn] = self.wcount.get(wn, 0) + 1
            self.qof[wn] = queue
            ev = ("t", wn, self.wcount[wn])
        else:
            self.counts[engine] += 1
            ev = ("c", engine, self.counts[engine])
        for nm in reads:
            self.readers.setdefault(nm, []).append(ev)
        for nm in writes:
            self.last_w[nm] = ev
            self.readers[nm] = []
        self.ops.append(dict(engine=engine, call=call, is_dma=is_dma,
                             queue=queue, deps=deps, ev=ev))

    def op(self, engine, fn, inc=1):
        rec = _Recorder()
        fn(rec)
        self._record(engine, rec.call, False)

    def dma(self, out, in_, engine=None, **kw):
        nm = _ap_name(out) or ""
        q = self.DMAQS[(hash(nm) & 0xffff) % len(self.DMAQS)]
        self._record(q, ("dma_start", (), dict(out=out, in_=in_, **kw)),
                     True, queue=q)

    def barrier(self):
        bid = len(self.barriers)
        snap = ([("c", e, self.counts[e]) for e in self.ENGINES
                 if self.counts[e] > 0]
                + [("t", nm, c) for nm, c in self.wcount.items() if c > 0])
        self.barriers.append(snap)
        self.barrier_pending = bid

    def emit(self):
        nc = self.nc
        assign = {}
        pool = {q: [] for q in self.DMAQS}
        sem_carried = {}
        qof_sem = {}
        n_sems = [0]
        covered = {}
        walk_count = {}
        barrier_tr = {}
        last_barrier_seen = [-1]

        def tsem(nm):
            if nm not in assign:
                q = self.qof[nm]
                if pool[q]:
                    si = pool[q].pop()
                else:
                    si = n_sems[0]
                    n_sems[0] += 1
                    sem_carried[si] = 0
                    qof_sem[si] = q
                assign[nm] = (si, sem_carried[si], covered.get(nm, 0))
            return assign[nm]

        def recycle_all():
            for nm, (si, base, sc) in list(assign.items()):
                sem_carried[si] = base + 16 * (walk_count.get(nm, 0) - sc)
                pool[qof_sem[si]].append(si)
            assign.clear()

        plan = []
        for oi, o in enumerate(self.ops):
            waits = {}

            def add_wait(key, val):
                if waits.get(key, 0) < val:
                    waits[key] = val

            bdep = None
            for d in o["deps"]:
                if d[0] == "b":
                    bdep = d[1]
            if bdep is not None and bdep not in barrier_tr:
                tr = []
                for (k2, s2, v2) in self.barriers[bdep]:
                    if k2 == "c":
                        tr.append((("c", s2), v2))
                    elif s2 in assign:
                        si, base, sc = assign[s2]
                        tr.append((("t", si), base + 16 * (v2 - sc)))
                barrier_tr[bdep] = tr
                for (k2, s2, v2) in self.barriers[bdep]:
                    if k2 == "t":
                        covered[s2] = max(covered.get(s2, 0), v2)
                last_barrier_seen[0] = bdep
                recycle_all()
            if bdep is not None:
                for (key, val) in barrier_tr[bdep]:
                    add_wait(key, val)
            for d in o["deps"]:
                if d[0] == "b":
                    continue
                if d[0] == "c":
                    if d[1] == o["engine"] and (d[1] == "tensor"
                                                or not self.see):
                        continue
                    add_wait(("c", d[1]), d[2])
                else:
                    nm, k = d[1], d[2]
                    if k <= covered.get(nm, 0):
                        continue
                    si, base, sc = tsem(nm)
                    add_wait(("t", si), base + 16 * (k - sc))
            if o["is_dma"]:
                wn = o["ev"][1]
                si, base, sc = tsem(wn)
                walk_count[wn] = walk_count.get(wn, 0) + 1
                inc = ("t", si)
            else:
                inc = ("c", o["engine"])
            plan.append((sorted(waits.items()), inc))
        final_c = {e: self.counts[e] for e in self.ENGINES if self.counts[e]}
        final_t = {}
        for nm, (si, base, sc) in assign.items():
            final_t[si] = max(final_t.get(si, 0),
                              base + 16 * (walk_count.get(nm, 0) - sc))
        for si, v in sem_carried.items():
            if v > 0:
                final_t[si] = max(final_t.get(si, 0), v)
        by_eng = {}
        for oi, o in enumerate(self.ops):
            by_eng.setdefault(o["engine"], []).append(oi)
        with ExitStack() as stk:
            csem = {e: stk.enter_context(nc.semaphore("s_" + e))
                    for e in self.ENGINES}
            tsems = [stk.enter_context(nc.semaphore(f"st_{i}"))
                     for i in range(n_sems[0])]
            with nc.Block() as block:
                def mk(ename):
                    def body(eng):
                        waited = {}
                        for oi in by_eng.get(ename, []):
                            o = self.ops[oi]
                            w8, inc = plan[oi]
                            for (key, val) in w8:
                                if waited.get(key, 0) >= val:
                                    continue
                                sem = (csem[key[1]] if key[0] == "c"
                                       else tsems[key[1]])
                                eng.wait_ge(sem, val)
                                waited[key] = val
                            name, args, kwargs = o["call"]
                            inst = getattr(eng, name)(*args, **kwargs)
                            if inc[0] == "t":
                                inst.then_inc(tsems[inc[1]], 16)
                            else:
                                inst.then_inc(csem[ename], 1)
                        if ename == "sync":
                            for e2, v in final_c.items():
                                eng.wait_ge(csem[e2], v)
                            for si, v in final_t.items():
                                eng.wait_ge(tsems[si], v)
                    body.__name__ = "body_" + ename
                    return body
                for ename in self.ENGINES:
                    if ename in by_eng or ename == "sync":
                        getattr(block, ename)(mk(ename))
        return len(self.ops)


USE_SCHED = True
def kernel_fused(**inputs):
    cvals = make_constants()
    x_full = np.ascontiguousarray(
        inputs["L0_attn_x"].reshape(32 * S, E).astype(np.float32))
    shards = [np.ascontiguousarray(x_full[c * T:(c + 1) * T])
              for c in range(NCORES)]
    f32c = np.ascontiguousarray

    nc = bass.Bass("TRN2", target_bir_lowering=False)
    EI, INT, EO = "ExternalInput", "Internal", "ExternalOutput"
    di = {}
    def ein(name, shape, dt):
        di[name] = _dram(nc, name, shape, dt, EI)
        return di[name]
    x_d = ein("x", [T, E], F32)
    qw_d = ein("qw", [E, E], F32); kw_d = ein("kw", [E, E], F32)
    vw_d = ein("vw", [E, E], F32); outw_d = ein("outw", [E, E], F32)
    qb_d = ein("qb", [E], F32); kb_d = ein("kb", [E], F32)
    vb_d = ein("vb", [E], F32); outb_d = ein("outb", [E], F32)
    proj_d = ein("proj", [256, 64], F32)
    gamma_d = ein("gamma", [E], F32); beta_d = ein("beta", [E], F32)
    ffnw_d = ein("ffnw", [E, E], F32); fb_d = ein("fb", [E], F32)
    Win_d = ein("W_in", [2048, E], F32)
    convw_d = ein("convw", [2048, 4], F32); convb_d = ein("convb", [2048], F32)
    Wx_d = ein("W_x", [320, 2048], F32); Wdt_d = ein("W_dt", [128, 64], F32)
    bdt_d = ein("bdt", [128], F32); Alog_d = ein("Alog", [128], F32)
    wg_d = ein("wg", [1], F32); Wout_d = ein("W_out", [E, 2048], F32)
    wqkv_d = ein("wqkv", [2 * E, 1536], F32)
    qs_d = ein("qs", [2, 64], F32); ks_d = ein("ks", [2, 64], F32)
    wo_d = ein("wo", [E, E], F32)
    lng_d = ein("lng", [2, E], F32); lnb_d = ein("lnb", [2, E], F32)
    w1_d = ein("w1", [E, 2 * 5460], F32); b1_d = ein("b1", [2 * 5460], F32)
    w2_d = ein("w2", [5460, E], F32); b2_d = ein("b2", [2, E], F32)
    out_d = _dram(nc, "out", [T, E], F32, EO)

    sc = {}
    for name, shape, dt in (
        ("qw_bf", [E, E], BF16), ("kw_bf", [E, E], BF16),
        ("vw_bf", [E, E], BF16), ("outw_bf", [E, E], BF16),
        ("wdqT", [E, E], BF16), ("WinT", [E, 2048], BF16),
        ("WxT", [2048, 320], BF16), ("WdtT", [64, 128], BF16),
        ("WoutT", [2048, E], BF16), ("wqkv_bf", [2 * E, 1536], BF16),
        ("wo_bf", [E, E], BF16), ("w1_bf", [E, 2 * 5460], BF16),
        ("w2_bf", [5460, E], BF16),
        ("xn", [T, E], F32), ("xnT", [E, T], BF16),
        ("attnT", [E, T], BF16), ("a0", [T, E], F32), ("f0", [T, E], F32),
        ("x1", [T, E], F32), ("xn1", [T, E], F32), ("xn1T", [E, T], BF16),
        ("zT", [2048, T], BF16), ("yw", [BL, S], F32), ("a1", [T, E], F32),
        ("xf", [T, E], F32),
    ):
        sc[name] = _dram(nc, name, shape, dt, INT)

    p = P2(nc) if USE_SCHED else P(nc)
    with ExitStack() as stk0:
        cb = Bufs(nc, stk0)
        consts = load_constants(p, nc, cb, cvals)

        def stage(fn):
            with ExitStack() as stks:
                bufs = Bufs(nc, stks)
                fn(bufs)
            if USE_SCHED:
                p.barrier()

        stage(lambda bufs: emit_prep(p, nc, bufs, consts,
              {"qw": qw_d, "kw": kw_d, "vw": vw_d, "outw": outw_d,
               "ffnw": ffnw_d, "W_in": Win_d, "W_x": Wx_d, "W_dt": Wdt_d,
               "W_out": Wout_d, "wqkv": wqkv_d, "wo": wo_d, "w1": w1_d,
               "w2": w2_d}, sc))
        stage(lambda bufs: emit_ln0(p, nc, bufs, consts, x_d, sc["xn"],
                                    sc["xnT"]))
        stage(lambda bufs: emit_l0_attn(p, nc, bufs, consts, sc["xnT"],
              sc["qw_bf"], sc["kw_bf"], sc["vw_bf"], qb_d, kb_d, vb_d,
              proj_d, sc["attnT"]))
        stage(lambda bufs: emit_l0_post(p, nc, bufs, consts, sc["attnT"],
              sc["outw_bf"], outb_d, sc["xn"], gamma_d, beta_d, sc["a0"]))
        stage(lambda bufs: emit_l0_ffn(p, nc, bufs, consts, sc["xn"],
              sc["wdqT"], fb_d, sc["f0"]))
        stage(lambda bufs: emit_combine_ln(p, nc, bufs, consts,
              [x_d, sc["a0"], sc["f0"]], sc["x1"], sc["xn1"], sc["xn1T"]))
        stage(lambda bufs: emit_l1_attn(p, nc, bufs, consts, sc["xn1T"],
              sc["WinT"], convw_d, convb_d, sc["WxT"], sc["WdtT"], bdt_d,
              Alog_d, wg_d, sc["zT"], sc["yw"]))
        stage(lambda bufs: emit_l1_attn_out(p, nc, bufs, consts, sc["zT"],
              sc["yw"], sc["WoutT"], sc["a1"]))
        stage(lambda bufs: emit_l1_ffn(p, nc, bufs, consts, sc["xn1"],
              sc["wqkv_bf"], qs_d, ks_d, sc["wo_bf"], lng_d, lnb_d,
              sc["w1_bf"], b1_d, sc["w2_bf"], b2_d, sc["xf"]))
        stage(lambda bufs: emit_sum_final(p, nc, bufs,
              [sc["x1"], sc["a1"], sc["xf"]], out_d))
        p.emit()

    shared = {
        "qw": f32c(inputs["L0_attn_q_w"]), "kw": f32c(inputs["L0_attn_k_w"]),
        "vw": f32c(inputs["L0_attn_v_w"]), "outw": f32c(inputs["L0_attn_out_w"]),
        "qb": f32c(inputs["L0_attn_q_b"]), "kb": f32c(inputs["L0_attn_k_b"]),
        "vb": f32c(inputs["L0_attn_v_b"]), "outb": f32c(inputs["L0_attn_out_b"]),
        "proj": f32c(inputs["L0_attn_proj"]),
        "gamma": f32c(inputs["L0_attn_gamma"]), "beta": f32c(inputs["L0_attn_beta"]),
        "ffnw": f32c(inputs["L0_ffn_weight"]), "fb": f32c(inputs["L0_ffn_bias"]),
        "W_in": f32c(inputs["L1_attn_W_in"]),
        "convw": f32c(inputs["L1_attn_conv_w"].reshape(2048, 4)),
        "convb": f32c(inputs["L1_attn_conv_b"]),
        "W_x": f32c(inputs["L1_attn_W_x"]), "W_dt": f32c(inputs["L1_attn_W_dt"]),
        "bdt": f32c(inputs["L1_attn_b_dt"]), "Alog": f32c(inputs["L1_attn_A_log"]),
        "wg": f32c(inputs["L1_attn_w_gate"]), "W_out": f32c(inputs["L1_attn_W_out"]),
        "wqkv": f32c(inputs["L1_ffn_wqkv"].reshape(2 * E, 1536)),
        "qs": f32c(inputs["L1_ffn_q_scale"]), "ks": f32c(inputs["L1_ffn_k_scale"]),
        "wo": f32c(inputs["L1_ffn_wo"].reshape(E, E)),
        "lng": f32c(inputs["L1_ffn_ln_g"]), "lnb": f32c(inputs["L1_ffn_ln_b"]),
        "w1": f32c(np.transpose(inputs["L1_ffn_w1"], (1, 0, 2)).reshape(E, 2 * 5460)),
        "b1": f32c(inputs["L1_ffn_b1"].reshape(-1)),
        "w2": f32c(inputs["L1_ffn_w2"].reshape(2 * 2730, E)),
        "b2": f32c(inputs["L1_ffn_b2"]),
    }
    for k2 in CONST_SPECS:
        shared["c_" + k2] = cvals[k2]
    in_maps = [dict(shared, x=shards[c]) for c in range(NCORES)]
    r = run_bass_kernel_spmd(nc, in_maps, core_ids=list(range(NCORES)))
    out = np.concatenate([r.results[c]["out"] for c in range(NCORES)], axis=0)
    return out.reshape(32, S, E).astype(np.float32)




def p2_truncate(p, K):
    p.ops = p.ops[:K]
    p.counts = {e: 0 for e in p.ENGINES}
    p.wcount = {}
    for o in p.ops:
        k, s, v = o["ev"]
        if k == "c":
            p.counts[s] = v
        else:
            p.wcount[s] = v


kernel = kernel_fused


# version 04
# speedup vs baseline: 1.4457x; 1.2857x over previous
import numpy as np
import ml_dtypes
import concourse.bass as bass
import concourse.mybir as mybir
from concourse.bass_utils import run_bass_kernel_spmd

F32 = mybir.dt.float32
BF16 = mybir.dt.bfloat16
FP8 = mybir.dt.float8e4
AX = mybir.AxisListType
ALU = mybir.AluOpType
AF = mybir.ActivationFunctionType

NCORES = 8
BL = 4
S = 2048
T = BL * S
E = 1024
NT = T // 128

BIG_NEG = np.float32(-3.4028235e38)
BF = ml_dtypes.bfloat16


class P:
    ENGINES = ("sync", "scalar", "vector", "tensor", "gpsimd")

    def __init__(self, nc):
        self.nc = nc
        self.ops = []

    def op(self, engine, fn, inc=1):
        self.ops.append((engine, fn, inc))

    def dma(self, out, in_, engine="sync", **kw):
        self.ops.append((engine, lambda e: e.dma_start(out=out, in_=in_, **kw), 16))

    def emit(self):
        nc = self.nc
        starts, c = [], 0
        for (_, _, inc) in self.ops:
            starts.append(c)
            c += inc
        total = c
        by_eng = {}
        for i, o in enumerate(self.ops):
            by_eng.setdefault(o[0], []).append((i, o))
        with nc.semaphore("tick") as tick, nc.Block() as block:
            def mk(ename):
                def body(eng):
                    for i, (_, fn, inc) in by_eng.get(ename, []):
                        if starts[i] > 0:
                            eng.wait_ge(tick, starts[i])
                        fn(eng).then_inc(tick, inc)
                    if ename == "sync":
                        eng.wait_ge(tick, total)
                body.__name__ = "body_" + ename
                return body
            for ename in self.ENGINES:
                if ename in by_eng or ename == "sync":
                    getattr(block, ename)(mk(ename))
        return total


class Bufs:

    def __init__(self, nc, stack):
        self.nc, self.stack, self.cache = nc, stack, {}

    def sb(self, name, shape, dtype=F32):
        if name not in self.cache:
            self.cache[name] = self.stack.enter_context(
                self.nc.sbuf_tensor(name, list(shape), dtype))
        return self.cache[name]

    def ps(self, name, shape, dtype=F32):
        if name not in self.cache:
            esz = 4 if dtype == F32 else 2
            free = max(shape[1], 2048 // esz)
            self.cache[name] = self.stack.enter_context(
                self.nc.psum_tensor(name, [shape[0], free], dtype))
        return self.cache[name][0:shape[0], 0:shape[1]]


def make_constants():
    c = {}
    c["identb"] = np.eye(128).astype(BF)
    c["identf"] = np.eye(128, dtype=np.float32)
    e64 = np.eye(64)
    c["identb64"] = np.vstack([e64, e64]).astype(BF)
    row, col = np.mgrid[0:128, 0:128]
    c["triu_mult"] = (row <= col).astype(np.float32)
    NEGB = -1e38
    c["maskLb"] = np.where(col >= row, 0.0, NEGB).astype(BF)
    c["maskRb"] = np.where(col <= row, 0.0, NEGB).astype(BF)
    DH, WINDOW = 64, 512
    half = np.arange(0, DH, 2, dtype=np.float64)
    inv_freq = 1.0 / 10000.0 ** (half / DH)
    t = np.arange(S, dtype=np.float64)
    freqs = np.concatenate([t[:, None] * inv_freq[None, :]] * 2, -1)
    base = (half + 0.4 * DH) / (1.4 * DH)
    power = (t - S // 2) / float(WINDOW // 2)
    scale = np.concatenate([base[None, :] ** power[:, None]] * 2, -1)
    cos, sin = np.cos(freqs), np.sin(freqs)
    Sq, Sk = (sin * scale).T, (sin / scale).T
    Aq, Ak = (cos * scale).T, (cos / scale).T
    Bq = np.concatenate([-Sq[0:32], Sq[32:64]], 0)
    Bk = np.concatenate([-Sk[0:32], Sk[32:64]], 0)
    c["rot_Aq"] = np.vstack([Aq, Aq]).astype(np.float32)
    c["rot_Ak"] = np.vstack([Ak, Ak]).astype(np.float32)
    c["rot_Bq"] = np.vstack([Bq, Bq]).astype(np.float32)
    c["rot_Bk"] = np.vstack([Bk, Bk]).astype(np.float32)
    cols = np.zeros((128, 8), dtype=np.float32)
    cols[:, 0] = 1.0
    cols[0:64, 1] = 1.0
    cols[64:128, 2] = 1.0
    cols[0:64, 3] = 1.0 / 128.0
    cols[64:128, 4] = 1.0 / 128.0
    c["cols"] = cols
    c["one_row"] = np.ones((1, 128), dtype=np.float32)
    c["negone_row"] = np.full((1, 128), -1.0, dtype=np.float32)
    return c


CONST_SPECS = {
    "identb": BF16, "identf": F32, "identb64": BF16,
    "triu_mult": F32, "maskLb": BF16, "maskRb": BF16,
    "rot_Aq": F32, "rot_Bq": F32, "rot_Ak": F32, "rot_Bk": F32,
    "cols": F32, "one_row": F32, "negone_row": F32,
}


def load_constants(p, nc, bufs, cvals, names=None):
    tiles = {}
    for name, dt in CONST_SPECS.items():
        if names is not None and name not in names:
            continue
        arr = cvals[name]
        d = nc.dram_tensor("c_" + name, list(arr.shape), dt, kind="ExternalInput").ap()
        tsb = bufs.sb("sb_" + name, arr.shape, dt)
        p.dma(tsb[:, :], d[:, :])
        tiles[name] = tsb
    return tiles


def load_col(p, nc, bufs, dram_vec, lo, n, name):
    t = bufs.sb(name, (128, 1), F32)
    src = dram_vec[lo:lo + n].rearrange("(p one) -> p one", one=1)
    p.dma(t[0:n, :], src, allow_slow_non_contiguous=True)
    return t


def emit_layernorm_tile(p, nc, bufs, xt, xn, eps, n=E, pref="", sq=None):
    st = bufs.sb(pref + "ln_stat", (128, 8), F32)
    if sq is None:
        sq = bufs.sb(pref + "ln_sq", (128, n), F32)
    p.op("vector", lambda e: e.reduce_sum(st[:, 0:1], xt[:, 0:n], AX.X))
    p.op("vector", lambda e: e.tensor_scalar_mul(st[:, 1:2], st[:, 0:1], -1.0 / n))
    p.op("vector", lambda e: e.tensor_scalar_add(xn[:, 0:n], xt[:, 0:n], st[:, 1:2]))
    p.op("scalar", lambda e: e.activation(sq[:, 0:n], xn[:, 0:n], AF.Square,
                                          accum_out=st[:, 2:3]))
    p.op("scalar", lambda e: e.activation(st[:, 3:4], st[:, 2:3], AF.Sqrt,
                                          bias=np.float32(eps), scale=1.0 / n))
    p.op("vector", lambda e: e.reciprocal(st[:, 4:5], st[:, 3:4]))
    p.op("vector", lambda e: e.tensor_scalar_mul(xn[:, 0:n], xn[:, 0:n], st[:, 4:5]))


def emit_transpose_store(p, nc, bufs, srcb, dst_dram, row0, col0, nf, idb, pref=""):
    pst_r = [bufs.ps(pref + "tr_ps%d" % j, (128, 128), BF16) for j in range(2)]
    tb_r = [bufs.sb(pref + "tr_out%d" % j, (128, 128), BF16) for j in range(2)]
    for f in range(nf):
        pst, tb = pst_r[f % 2], tb_r[f % 2]
        p.op("tensor", lambda e, f=f, pst=pst: e.transpose(pst[:, :], srcb[:, f * 128:(f + 1) * 128], idb[:, :]))
        p.op("vector", lambda e, tb=tb, pst=pst: e.tensor_copy(tb[:, :], pst[:, :]))
        p.dma(dst_dram[row0 + f * 128:row0 + (f + 1) * 128, col0:col0 + 128], tb[:, :])


def emit_ln0(p, nc, bufs, consts, x_dram, xn_dram, xnT_dram, eps=1e-5):
    xt_r = [bufs.sb(f"ln0_x{j}", (128, E), F32) for j in range(2)]
    xb_r = [bufs.sb(f"ln0_xb{j}", (128, E), BF16) for j in range(2)]
    for i in range(NT):
        j = i % 2
        xt, xb = xt_r[j], xb_r[j]
        p.dma(xt[:, :], x_dram[i * 128:(i + 1) * 128, :])
        emit_layernorm_tile(p, nc, bufs, xt, xt, eps, pref="ln0_%d" % j)
        if xn_dram is not None:
            p.dma(xn_dram[i * 128:(i + 1) * 128, :], xt[:, :])
        if xnT_dram is not None:
            p.op("vector", lambda e, xb=xb, xt=xt: e.tensor_copy(xb[:, :], xt[:, :]))
            emit_transpose_store(p, nc, bufs, xb, xnT_dram, 0, i * 128, 8,
                                 consts["identb"], pref="ln0_%d" % j)


def np_layernorm(x, eps=1e-5):
    m = x.mean(-1, keepdims=True)
    v = x.var(-1, keepdims=True)
    return (x - m) / np.sqrt(v + eps)


M1 = 1572864.0


def emit_quant_tile(p, nc, bufs, xt, xd, n=E, pref="q_"):
    nb = n // 32
    am = bufs.sb(pref + "am", (128, E // 32), F32)
    y = bufs.sb(pref + "y", (128, E), F32)
    g = bufs.sb(pref + "g", (128, E), F32)
    f8 = bufs.sb(pref + "f8", (128, E), FP8)
    r1 = bufs.sb(pref + "r1", (128, E), F32)
    xr = xt[:, 0:n].rearrange("p (k c) -> p k c", c=32)
    p.op("vector", lambda e: e.tensor_reduce(am[:, 0:nb], xr, AX.X, ALU.max,
                                             apply_absolute_value=True))
    p.op("vector", lambda e: e.tensor_scalar(am[:, 0:nb], am[:, 0:nb],
                                             1.0 / 7.5, 1e-30, ALU.mult, ALU.max))
    rs = bufs.sb(pref + "rs", (128, E // 32), F32)
    p.op("vector", lambda e: e.reciprocal(rs[:, 0:nb], am[:, 0:nb]))
    bc = lambda t: t[:, 0:nb].unsqueeze(2).broadcast_to([128, nb, 32])
    yr = y[:, 0:n].rearrange("p (k c) -> p k c", c=32)
    p.op("vector", lambda e: e.tensor_tensor(yr, xr, bc(rs), ALU.mult))
    p.op("vector", lambda e: e.tensor_copy(f8[:, 0:n], y[:, 0:n]))
    p.op("vector", lambda e: e.tensor_copy(g[:, 0:n], f8[:, 0:n]))
    p.op("vector", lambda e: e.tensor_scalar_add(r1[:, 0:n], y[:, 0:n], M1))
    p.op("vector", lambda e: e.tensor_scalar_add(r1[:, 0:n], r1[:, 0:n], -M1))
    p.op("vector", lambda e: e.tensor_scalar(r1[:, 0:n], r1[:, 0:n], 1.0, -1.0,
                                             ALU.min, ALU.max))
    p.op("vector", lambda e: e.tensor_scalar(y[:, 0:n], g[:, 0:n], 1.0, -1.0,
                                             ALU.min, ALU.max))
    p.op("vector", lambda e: e.tensor_sub(g[:, 0:n], g[:, 0:n], y[:, 0:n]))
    p.op("vector", lambda e: e.tensor_add(g[:, 0:n], g[:, 0:n], r1[:, 0:n]))
    xdr = xd[:, 0:n].rearrange("p (k c) -> p k c", c=32)
    gr = g[:, 0:n].rearrange("p (k c) -> p k c", c=32)
    p.op("vector", lambda e: e.tensor_tensor(xdr, gr, bc(am), ALU.mult))


def np_quant(x):
    vals = set()
    for bits in range(64):
        s = bits >> 5 & 1
        ee = bits >> 3 & 3
        m = bits & 7
        v = m / 8.0 if ee == 0 else (1.0 + m / 8.0) * 2.0 ** (ee - 1)
        vals.add(-v if s else v)
    grid = np.array(sorted(vals), dtype=np.float32)
    rows, K = x.shape
    b = x.reshape(rows, K // 32, 32)
    scales = np.maximum(np.max(np.abs(b), -1) / 7.5, 1e-30).astype(np.float32)
    y = (b / scales[..., None]).astype(np.float32)
    idx = np.clip(np.searchsorted(grid, y), 0, grid.shape[0] - 1)
    lo = np.maximum(idx - 1, 0)
    vhi, vlo = grid[idx], grid[lo]
    q = np.where(np.abs(y - vlo) <= np.abs(y - vhi), vlo, vhi)
    return (q * scales[..., None]).reshape(rows, K).astype(np.float32)


def emit_cast_bf16(p, nc, bufs, src_dram, dst_dram, R, C, pref="cb_"):
    a = bufs.sb(pref + "a", (128, 2048), F32)
    b = bufs.sb(pref + "b", (128, 2048), BF16)
    for r0 in range(0, R, 128):
        nr = min(128, R - r0)
        for c0 in range(0, C, 2048):
            ncol = min(2048, C - c0)
            p.dma(a[0:nr, 0:ncol], src_dram[r0:r0 + nr, c0:c0 + ncol])
            p.op("vector", lambda e, nr=nr, ncol=ncol: e.tensor_copy(
                b[0:nr, 0:ncol], a[0:nr, 0:ncol]))
            p.dma(dst_dram[r0:r0 + nr, c0:c0 + ncol], b[0:nr, 0:ncol])


def emit_transpose_bf16(p, nc, bufs, src_dram, dst_dram, R, C, idb,
                        quant=False, scale=None, pref="tb_"):
    a = bufs.sb(pref + "a", (128, max(C, 32)), F32)
    b = bufs.sb(pref + "b", (128, max(C, 32)), BF16)
    pst = bufs.ps(pref + "ps", (128, 128), BF16)
    o = bufs.sb(pref + "o", (128, 128), BF16)
    assert C <= 8192
    for r0 in range(0, R, 128):
        nr = min(128, R - r0)
        p.dma(a[0:nr, 0:C], src_dram[r0:r0 + nr, 0:C])
        if quant:
            assert nr == 128
            emit_quant_tile(p, nc, bufs, a, a, n=C, pref=pref + "q_")
        if scale is not None:
            p.op("vector", lambda e, nr=nr: e.tensor_scalar_mul(
                a[0:nr, 0:C], a[0:nr, 0:C], float(scale)))
        p.op("vector", lambda e, nr=nr: e.tensor_copy(b[0:nr, 0:C], a[0:nr, 0:C]))
        for c0 in range(0, C, 128):
            ncol = min(128, C - c0)
            p.op("tensor", lambda e, nr=nr, c0=c0, ncol=ncol: e.transpose(
                pst[0:ncol, 0:nr], b[0:nr, c0:c0 + ncol], idb[0:nr, 0:nr]))
            p.op("vector", lambda e, nr=nr, ncol=ncol: e.tensor_copy(
                o[0:ncol, 0:nr], pst[0:ncol, 0:nr]))
            p.dma(dst_dram[c0:c0 + ncol, r0:r0 + nr], o[0:ncol, 0:nr])


def load_bias_cols(p, nc, bufs, dram_vec, n, name):
    nf = n // 128
    t = bufs.sb(name, (128, nf), F32)
    src = dram_vec[0:n].rearrange("(f p) -> p f", p=128)
    p.dma(t[:, 0:nf], src, allow_slow_non_contiguous=True)
    return t


def emit_l0_attn(p, nc, bufs, consts, xnT, qw, kw, vw, qb_d, kb_d, vb_d,
                 proj_d, attnT):
    idb = consts["identb"]
    H, D, Mf = 16, 64, 256
    NCH = S // 128

    qb = load_bias_cols(p, nc, bufs, qb_d, E, "l0_qb")
    kb = load_bias_cols(p, nc, bufs, kb_d, E, "l0_kb")
    vb = load_bias_cols(p, nc, bufs, vb_d, E, "l0_vb")

    projT = bufs.sb("l0_projT", (128, 256), BF16)
    pj = bufs.sb("l0_pj", (128, 64), F32)
    pps = bufs.ps("l0_psmisc", (128, 512), F32)[0:64, 0:128]
    for half in range(2):
        p.dma(pj[:, :], proj_d[half * 128:(half + 1) * 128, :])
        p.op("tensor", lambda e: e.transpose(pps, pj[:, :],
                                             consts["identf"][:, :]))
        p.op("scalar", lambda e, half=half: e.activation(
            projT[0:64, half * 128:(half + 1) * 128], pps, AF.Copy,
            scale=0.125))
        p.op("scalar", lambda e, half=half: e.activation(
            projT[64:128, half * 128:(half + 1) * 128], pps, AF.Copy,
            scale=0.125))

    xnb = [bufs.sb(f"l0_xn{k}", (128, S), BF16) for k in range(8)]
    wsb_r = [bufs.sb(f"l0_w{j}", (128, 3 * E), BF16) for j in range(2)]
    qkvT = [bufs.sb(f"l0_qkvT{j}", (128, S), BF16) for j in range(6)]
    corrK = [bufs.sb(f"l0_corrK{h}", (1, S), F32) for h in range(4)]
    sqt = bufs.sb("l0_sq", (128, 512), F32)
    qp = [bufs.sb(f"l0_qp{j}", (128, S), BF16) for j in range(2)]
    kp = [bufs.sb(f"l0_kp{j}", (128, S), BF16) for j in range(2)]
    vtok = bufs.sb("l0_vtok", (128, NCH * 64), BF16)
    kptok = bufs.sb("l0_kptok", (128, 256), BF16)
    smT = bufs.sb("l0_smT", (128, 128), BF16)
    kvsb = bufs.sb("l0_kvsb", (128, 2 * 65), BF16)
    numsb = bufs.sb("l0_numsb", (128, 65), F32)
    att = bufs.sb("l0_att", (128, 64), BF16)
    attT = bufs.sb("l0_attT", (64, 128), BF16)
    onesb = bufs.sb("l0_onesb", (128, 1), BF16)
    triub = bufs.sb("l0_triub", (128, 128), F32)
    p.op("vector", lambda e: e.memset(onesb[:, :], 1.0))
    p.op("vector", lambda e: e.tensor_copy(triub[:, :], consts["triu_mult"][:, :]))

    ps_mm = [bufs.ps(f"l0_psmm{j}", (128, 512), F32) for j in range(2)]
    ps_map = [bufs.ps("l0_psmap", (128, 512), F32), ps_mm[1]]
    ps_misc = bufs.ps("l0_psmisc", (128, 512), F32)
    ps_corr = ps_misc[0:1, 0:512]
    ps_sqk = ps_misc[:, 0:128]
    ps_nd = bufs.ps("l0_psnd", (128, 65), F32)
    ps_kv = [bufs.ps(f"l0_pskv{j}", (128, 65), F32) for j in range(2)]
    ps_trbf = bufs.ps("l0_pstrbf", (128, 128), BF16)
    ps_trb = ps_trbf[:, 0:128]
    ps_tr64 = ps_trbf[:, 0:64]

    for b in range(BL):
        for k in range(8):
            p.dma(xnb[k][:, :], xnT[k * 128:(k + 1) * 128, b * S:(b + 1) * S])
        for hg in range(4):
            for ft in range(2):
                frow = hg * 256 + ft * 128
                wsb = wsb_r[ft % 2]
                for wi, (wd, bias) in enumerate(((qw, qb), (kw, kb), (vw, vb))):
                    for kt in range(8):
                        p.dma(wsb[:, wi * E + kt * 128:wi * E + (kt + 1) * 128],
                              wd[kt * 128:(kt + 1) * 128, frow:frow + 128])
                for wi, (wd, bias) in enumerate(((qw, qb), (kw, kb), (vw, vb))):
                    bcol = bias[:, frow // 128:frow // 128 + 1]
                    for c in range(4):
                        psm_ = ps_mm[(wi * 4 + c) % 2]
                        for kt in range(8):
                            p.op("tensor", lambda e, wi=wi, kt=kt, c=c, wsb=wsb, psm_=psm_: e.matmul(
                                psm_[:, :],
                                wsb[:, wi * E + kt * 128:wi * E + (kt + 1) * 128],
                                xnb[kt][:, c * 512:(c + 1) * 512],
                                start=(kt == 0), stop=(kt == 7)))
                        dst = qkvT[wi * 2 + ft]
                        p.op("vector", lambda e, dst=dst, c=c, bcol=bcol, psm_=psm_: e.tensor_scalar_add(
                            dst[:, c * 512:(c + 1) * 512], psm_[:, :], bcol))
                        if wi == 1:
                            p.op("scalar", lambda e, bcol=bcol, psm_=psm_: e.activation(
                                sqt[:, :], psm_[:, :], AF.Square, bias=bcol))
                            for hh in range(2):
                                hloc = ft * 2 + hh
                                p.op("tensor", lambda e, hh=hh: e.matmul(
                                    ps_corr[:, :], consts["cols"][:, 3 + hh:4 + hh],
                                    sqt[:, :], start=True, stop=True))
                                p.op("vector", lambda e, hloc=hloc, c=c: e.tensor_copy(
                                    corrK[hloc][:, c * 512:(c + 1) * 512],
                                    ps_corr[:, :]))
            for h in range(4):
                ft, base = h // 2, (h % 2) * 64
                hglob = hg * 4 + h
                qh = qkvT[ft]
                kh = qkvT[2 + ft]
                vh = qkvT[4 + ft]
                for dst, srcT, corr in ((qp, qh, None), (kp, kh, corrK[h])):
                    for mt in range(2):
                        for c in range(4):
                            psm_ = ps_map[(mt * 4 + c) % 2]
                            p.op("tensor", lambda e, mt=mt, c=c, srcT=srcT, base=base, psm_=psm_: e.matmul(
                                psm_[:, :],
                                projT[base:base + 64, mt * 128:(mt + 1) * 128],
                                srcT[base:base + 64, c * 512:(c + 1) * 512],
                                start=True, stop=(corr is None)))
                            if corr is not None:
                                p.op("tensor", lambda e, c=c, corr=corr, psm_=psm_: e.matmul(
                                    psm_[:, :], consts["negone_row"][:, :],
                                    corr[:, c * 512:(c + 1) * 512],
                                    start=False, stop=True))
                            p.op("scalar", lambda e, dst=dst, mt=mt, c=c, psm_=psm_: e.activation(
                                dst[mt][:, c * 512:(c + 1) * 512], psm_[:, :],
                                AF.Exp))
                for c in range(NCH):
                    p.op("tensor", lambda e, c=c, vh=vh, base=base: e.transpose(
                        ps_tr64[:, :], vh[base:base + 64, c * 128:(c + 1) * 128],
                        consts["identb64"][base:base + 64, 0:64]))
                    p.op("vector", lambda e, c=c: e.tensor_copy(
                        vtok[:, c * 64:(c + 1) * 64], ps_tr64[:, :]))
                for c in range(NCH):
                    cs = slice(c * 128, (c + 1) * 128)
                    for mt in range(2):
                        p.op("tensor", lambda e, mt=mt, cs=cs: e.matmul(
                            ps_sqk[:, :], kp[mt][:, cs], qp[mt][:, cs],
                            start=(mt == 0), stop=(mt == 1)))
                    p.op("vector", lambda e: e.tensor_tensor(
                        smT[:, :], ps_sqk[:, :], triub[:, :], ALU.mult))
                    p.op("tensor", lambda e, c=c: e.matmul(
                        ps_nd[:, 0:64], smT[:, :], vtok[:, c * 64:(c + 1) * 64],
                        start=True, stop=False, skip_group_check=True))
                    p.op("tensor", lambda e: e.matmul(
                        ps_nd[:, 64:65], smT[:, :], onesb[:, :],
                        start=False, stop=False, skip_group_check=True))
                    for mt in range(2):
                        if c > 0:
                            p.op("tensor", lambda e, mt=mt, cs=cs: e.matmul(
                                ps_nd[:, 0:65], qp[mt][:, cs],
                                kvsb[:, mt * 65:(mt + 1) * 65],
                                start=False, stop=(mt == 1),
                                skip_group_check=True))
                    for mt in range(2):
                        p.op("tensor", lambda e, mt=mt, cs=cs: e.transpose(
                            ps_trb[:, :], kp[mt][:, cs], idb[:, :]))
                        p.op("vector", lambda e, mt=mt: e.tensor_copy(
                            kptok[:, mt * 128:(mt + 1) * 128], ps_trb[:, :]))
                    for mt in range(2):
                        p.op("tensor", lambda e, mt=mt, c=c: e.matmul(
                            ps_kv[mt][:, 0:64], kptok[:, mt * 128:(mt + 1) * 128],
                            vtok[:, c * 64:(c + 1) * 64],
                            start=(c == 0), stop=False, skip_group_check=True))
                        p.op("tensor", lambda e, mt=mt, c=c: e.matmul(
                            ps_kv[mt][:, 64:65], kptok[:, mt * 128:(mt + 1) * 128],
                            onesb[:, :],
                            start=False, stop=(c == NCH - 1),
                            skip_group_check=True))
                        p.op("vector", lambda e, mt=mt: e.tensor_copy(
                            kvsb[:, mt * 65:(mt + 1) * 65], ps_kv[mt][:, :]))
                    p.op("vector", lambda e: e.reciprocal(numsb[:, 64:65],
                                                          ps_nd[:, 64:65]))
                    p.op("vector", lambda e: e.tensor_scalar(
                        att[:, :], ps_nd[:, 0:64], numsb[:, 64:65], None,
                        ALU.mult))
                    p.op("tensor", lambda e: e.transpose(ps_trb[0:64, 0:128],
                                                         att[:, :], idb[:, :]))
                    p.op("vector", lambda e: e.tensor_copy(attT[:, :],
                                                           ps_trb[0:64, 0:128]))
                    p.dma(attnT[hglob * 64:(hglob + 1) * 64,
                                b * S + c * 128:b * S + (c + 1) * 128],
                          attT[:, :])




def emit_l0_post(p, nc, bufs, consts, attnT, outw, outb_d, xn_dram,
                 gamma_d, beta_d, a0_dram):
    wt = [bufs.sb(f"lp_w{k}", (128, E), BF16) for k in range(8)]
    for kt in range(8):
        p.dma(wt[kt][:, :], outw[kt * 128:(kt + 1) * 128, :])
    brow = bufs.sb("lp_brow", (1, E), F32)
    p.dma(brow[:, :], outb_d[0:E].rearrange("(one n) -> one n", one=1))
    grow = bufs.sb("lp_grow", (1, E), F32)
    berow = bufs.sb("lp_berow", (1, E), F32)
    p.dma(grow[:, :], gamma_d[0:E].rearrange("(one n) -> one n", one=1))
    p.dma(berow[:, :], beta_d[0:E].rearrange("(one n) -> one n", one=1))
    ps_all = [bufs.ps(f"lp_ps{k}", (128, 512), F32) for k in range(4)]
    gb = bufs.sb("lp_gb", (128, E), F32)
    bb = bufs.sb("lp_bb", (128, E), F32)
    ps = ps_all[0:2]
    for n in range(2):
        p.op("tensor", lambda e, n=n: e.matmul(
            ps[0][:, :], consts["one_row"][:, :], grow[:, n * 512:(n + 1) * 512],
            start=True, stop=True))
        p.op("vector", lambda e, n=n: e.tensor_copy(gb[:, n * 512:(n + 1) * 512],
                                                    ps[0][:, :]))
        p.op("tensor", lambda e, n=n: e.matmul(
            ps[0][:, :], consts["one_row"][:, :], berow[:, n * 512:(n + 1) * 512],
            start=True, stop=True))
        p.op("vector", lambda e, n=n: e.tensor_copy(bb[:, n * 512:(n + 1) * 512],
                                                    ps[0][:, :]))
    at_r = [bufs.sb(f"lp_at{j}", (128, E), BF16) for j in range(2)]
    res_r = [bufs.sb(f"lp_res{j}", (128, E), F32) for j in range(2)]
    for i in range(NT):
        at = at_r[i % 2]
        res = res_r[i % 2]
        ps = ps_all[(i % 2) * 2:(i % 2) * 2 + 2]
        for kt in range(8):
            p.dma(at[:, kt * 128:(kt + 1) * 128],
                  attnT[kt * 128:(kt + 1) * 128, i * 128:(i + 1) * 128])
        for n in range(2):
            for kt in range(8):
                p.op("tensor", lambda e, n=n, kt=kt, at=at: e.matmul(
                    ps[n][:, :], at[:, kt * 128:(kt + 1) * 128],
                    wt[kt][:, n * 512:(n + 1) * 512],
                    start=(kt == 0), stop=False, skip_group_check=True))
            p.op("tensor", lambda e, n=n: e.matmul(
                ps[n][:, :], consts["one_row"][:, :],
                brow[:, n * 512:(n + 1) * 512],
                start=False, stop=True, skip_group_check=True))
        p.dma(res[:, :], xn_dram[i * 128:(i + 1) * 128, :])
        for n in range(2):
            p.op("vector", lambda e, n=n, res=res: e.tensor_tensor(
                res[:, n * 512:(n + 1) * 512], ps[n][:, :],
                res[:, n * 512:(n + 1) * 512], ALU.add))
        emit_layernorm_tile(p, nc, bufs, res, res, 1e-6, pref="lp_" + str(i % 2))
        p.op("vector", lambda e, res=res: e.tensor_tensor(res[:, :], res[:, :], gb[:, :],
                                                 ALU.mult))
        p.op("vector", lambda e, res=res: e.tensor_tensor(res[:, :], res[:, :], bb[:, :],
                                                 ALU.add))
        p.dma(a0_dram[i * 128:(i + 1) * 128, :], res[:, :])


def np_l0_post(attn, xn, outw, outb, gamma, beta):
    main = attn.astype(np.float64) @ outw.astype(np.float64) + outb
    res = main + xn
    mu = res.mean(-1, keepdims=True)
    var = ((res - mu) ** 2).mean(-1, keepdims=True)
    return (gamma * (res - mu) / np.sqrt(var + 1e-6) + beta).astype(np.float32)


def emit_combine_ln(p, nc, bufs, consts, srcs, x_out, xn_out, xnT_out,
                    eps=1e-5):
    a_r = [bufs.sb(f"cb_x{j}", (128, E), F32) for j in range(2)]
    t_r = [bufs.sb(f"cb_t{j}{k}", (128, E), F32) for j in range(2) for k in range(2)]
    xb_r = [bufs.sb(f"cb_xb{j}", (128, E), BF16) for j in range(2)]
    for i in range(NT):
        j = i % 2
        a, xb = a_r[j], xb_r[j]
        p.dma(a[:, :], srcs[0][i * 128:(i + 1) * 128, :])
        for si, s in enumerate(srcs[1:]):
            t = t_r[j * 2 + si]
            p.dma(t[:, :], s[i * 128:(i + 1) * 128, :])
            p.op("vector", lambda e, a=a, t=t: e.tensor_add(a[:, :], a[:, :], t[:, :]))
        if x_out is not None:
            p.dma(x_out[i * 128:(i + 1) * 128, :], a[:, :])
        emit_layernorm_tile(p, nc, bufs, a, a, eps, pref="cb_%d" % j)
        if xn_out is not None:
            p.dma(xn_out[i * 128:(i + 1) * 128, :], a[:, :])
        if xnT_out is not None:
            p.op("vector", lambda e, xb=xb, a=a: e.tensor_copy(xb[:, :], a[:, :]))
            emit_transpose_store(p, nc, bufs, xb, xnT_out, 0, i * 128, 8,
                                 consts["identb"], pref="cb_%d" % j)


def emit_sum_final(p, nc, bufs, srcs, out_dram):
    a_r = [bufs.sb(f"fi_x{j}", (128, E), F32) for j in range(2)]
    t_r = [bufs.sb(f"fi_t{j}{k}", (128, E), F32) for j in range(2) for k in range(2)]
    for i in range(NT):
        j = i % 2
        a = a_r[j]
        p.dma(a[:, :], srcs[0][i * 128:(i + 1) * 128, :])
        for si, s in enumerate(srcs[1:]):
            t = t_r[j * 2 + si]
            p.dma(t[:, :], s[i * 128:(i + 1) * 128, :])
            p.op("vector", lambda e, a=a, t=t: e.tensor_add(a[:, :], a[:, :], t[:, :]))
        p.dma(out_dram[i * 128:(i + 1) * 128, :], a[:, :])


def emit_l1_attn(p, nc, bufs, consts, xnT, WinT, convw_d, convb_d, WxT,
                 WdtT, bdt_d, Alog_d, wgate_d, zT_dram, yw_dram, dbg=None):
    DI = 2048
    cb = load_bias_cols(p, nc, bufs, convb_d, DI, "m_cb")
    bdt = load_col(p, nc, bufs, bdt_d, 0, 128, "m_bdt")
    alog = load_col(p, nc, bufs, Alog_d, 0, 128, "m_alog")
    wg = bufs.sb("m_wg", (1, 1), F32)
    p.dma(wg[:, :], wgate_d[0:1].rearrange("(a b) -> a b", a=1))
    acol = bufs.sb("m_acol", (128, 1), F32)
    p.op("scalar", lambda e: e.activation(acol[:, :], alog[:, :], AF.Exp))
    p.op("vector", lambda e: e.tensor_scalar_mul(acol[:, :], acol[:, :], -1.0))

    xnb = [bufs.sb(f"m_xn{k}", (128, S), BF16) for k in range(8)]
    wsb_r = [bufs.sb(f"m_w{j}", (128, E), BF16) for j in range(2)]
    cw = bufs.sb("m_cw", (128, 4), F32)
    cbuf = bufs.sb("m_cbuf", (128, 4 + S), F32)
    acc = [bufs.sb(f"m_acc{k}", (128, S), F32) for k in range(2)]
    xcT_dram = nc.dram_tensor("m_xcT", [2048, S], BF16, kind="Internal").ap()
    xcs = bufs.sb("m_xcs", (128, S), BF16)
    xck = [bufs.sb(f"m_xck{j}", (128, 512), BF16) for j in range(3)]
    zbf = [bufs.sb(f"m_zbf{j}", (128, 512), BF16) for j in range(2)]
    wx = bufs.sb("m_wx", (128, 2048), BF16)
    dtT = bufs.sb("m_dtT", (64, S), BF16)
    BmT = bufs.sb("m_BmT", (128, S), F32)
    CmT = bufs.sb("m_CmT", (128, S), F32)
    delta = bufs.sb("m_delta", (128, S), F32)
    u1 = bufs.sb("m_u1", (128, S), F32)
    u2 = bufs.sb("m_u2", (128, S), F32)
    hT = bufs.sb("m_hT", (128, S), F32)
    ywrow = bufs.sb("m_ywrow", (1, S), F32)
    wdt = bufs.sb("m_wdt", (64, 128), BF16)
    p.dma(wdt[:, :], WdtT[:, :])
    ps = bufs.ps("m_ps", (128, 512), F32)
    ps2 = bufs.ps("m_ps2", (128, 512), F32)
    psy = bufs.ps("m_psy", (1, 512), F32)

    for b in range(BL):
        for k in range(8):
            p.dma(xnb[k][:, :], xnT[k * 128:(k + 1) * 128, b * S:(b + 1) * S])
        for dt in range(16):
            wsb = wsb_r[dt % 2]
            for kt in range(8):
                p.dma(wsb[:, kt * 128:(kt + 1) * 128],
                      WinT[kt * 128:(kt + 1) * 128, dt * 128:(dt + 1) * 128])
            p.dma(cw[:, :], convw_d[dt * 128:(dt + 1) * 128, :])
            p.op("vector", lambda e: e.memset(cbuf[:, 0:4], 0.0))
            for c in range(4):
                ps_ = [ps, ps2][c % 2]
                for kt in range(8):
                    p.op("tensor", lambda e, kt=kt, c=c, wsb=wsb, ps_=ps_: e.matmul(
                        ps_[:, :], wsb[:, kt * 128:(kt + 1) * 128],
                        xnb[kt][:, c * 512:(c + 1) * 512],
                        start=(kt == 0), stop=(kt == 7)))
                zb_ = zbf[c % 2]
                p.op("scalar", lambda e, ps_=ps_, zb_=zb_: e.activation(zb_[:, :], ps_[:, :], AF.Silu))
                p.dma(zT_dram[dt * 128:(dt + 1) * 128,
                              b * S + c * 512:b * S + (c + 1) * 512], zb_[:, :])
                p.op("vector", lambda e, c=c, ps_=ps_: e.tensor_copy(
                    cbuf[:, 3 + c * 512:3 + (c + 1) * 512], ps_[:, :]))
            p.op("vector", lambda e, dt=dt: e.tensor_scalar(
                acc[0][:, :], cbuf[:, 0:S], cw[:, 0:1],
                cb[:, dt:dt + 1], ALU.mult, ALU.add))
            for j in (1, 2, 3):
                src, dst = acc[(j + 1) % 2], acc[j % 2]
                p.op("vector", lambda e, j=j, src=src, dst=dst: e.scalar_tensor_tensor(
                    dst[:, :], cbuf[:, j:j + S], cw[:, j:j + 1], src[:, :],
                    ALU.mult, ALU.add))
            p.op("vector", lambda e: e.tensor_copy(xcs[:, :], acc[1][:, :]))
            p.dma(xcT_dram[dt * 128:(dt + 1) * 128, :], xcs[:, :])
        for (mlo, msz, dst, dtyp) in ((0, 64, dtT, BF16), (64, 128, BmT, F32),
                                      (192, 128, CmT, F32)):
            for kt in range(16):
                p.dma(wx[0:128, kt * 128:kt * 128 + msz],
                      WxT[kt * 128:(kt + 1) * 128, mlo:mlo + msz])
            for c in range(4):
                for kt in range(16):
                    xk_ = xck[kt % 3]
                    p.dma(xk_[:, :], xcT_dram[kt * 128:(kt + 1) * 128,
                                              c * 512:(c + 1) * 512])
                    p.op("tensor", lambda e, kt=kt, msz=msz, xk_=xk_: e.matmul(
                        ps[0:msz, :], wx[:, kt * 128:kt * 128 + msz],
                        xk_[:, :], start=(kt == 0), stop=(kt == 15)))
                p.op("vector", lambda e, dst=dst, c=c, msz=msz: e.tensor_copy(
                    dst[0:msz, c * 512:(c + 1) * 512], ps[0:msz, :]))
        for c in range(4):
            p.op("tensor", lambda e, c=c: e.matmul(
                ps2[:, :], wdt[:, :], dtT[:, c * 512:(c + 1) * 512],
                start=True, stop=True))
            p.op("vector", lambda e, c=c: e.tensor_scalar(
                u1[:, c * 512:(c + 1) * 512], ps2[:, :], bdt[:, 0:1], 0.0,
                ALU.add, ALU.max))
            p.op("scalar", lambda e, c=c: e.activation(
                u2[:, c * 512:(c + 1) * 512], ps2[:, :], AF.Abs, bias=bdt[:, 0:1]))
        p.op("scalar", lambda e: e.activation(u2[:, :], u2[:, :], AF.Exp, scale=-1.0))
        p.op("scalar", lambda e: e.activation(u2[:, :], u2[:, :], AF.Ln, bias=1.0))
        p.op("vector", lambda e: e.tensor_add(delta[:, :], u1[:, :], u2[:, :]))
        p.op("scalar", lambda e: e.activation(u1[:, :], delta[:, :], AF.Exp,
                                              scale=acol[:, 0:1]))
        p.op("vector", lambda e: e.tensor_mul(u2[:, :], delta[:, :], BmT[:, :]))
        p.op("vector", lambda e: e.tensor_tensor_scan(hT[:, :], u1[:, :],
                                                      u2[:, :], 0.0,
                                                      ALU.mult, ALU.add))
        p.op("vector", lambda e: e.tensor_mul(hT[:, :], hT[:, :], CmT[:, :]))
        for c in range(4):
            p.op("tensor", lambda e, c=c: e.matmul(
                psy[:, :], consts["cols"][:, 0:1], hT[:, c * 512:(c + 1) * 512],
                start=True, stop=True))
            p.op("vector", lambda e, c=c: e.tensor_scalar(
                ywrow[:, c * 512:(c + 1) * 512], psy[:, :], wg[0:1, 0:1], None,
                ALU.mult))
        p.dma(yw_dram[b:b + 1, :], ywrow[:, :])
        if dbg is not None and b == 0:
            for nm, tile in (("xc0", xcs), ("dtT", dtT), ("BmT", BmT),
                             ("CmT", CmT), ("delta", delta), ("Abar", u1),
                             ("hT", hT)):
                if nm in dbg:
                    p.dma(dbg[nm][:, :], tile[:, :] if nm != "dtT"
                          else tile[0:64, :])


def np_l1_attn(xn, W_in, conv_w, conv_b, W_x, W_dt, b_dt, A_log, w_gate):
    DI = 2048
    z = np.zeros((T, DI), dtype=np.float32)
    yw = np.zeros((BL, S), dtype=np.float32)
    for b in range(BL):
        x = xn[b * S:(b + 1) * S].astype(np.float64)
        xi = x @ W_in.T.astype(np.float64)
        xc = np.zeros_like(xi)
        for j in range(4):
            lo = 3 - j
            xc[lo:, :] += xi[:S - lo, :] * conv_w[:, j]
        xc += conv_b
        ssm = xc @ W_x.T.astype(np.float64)
        dt, Bm, Cm = ssm[:, 0:64], ssm[:, 64:192], ssm[:, 192:320]
        da = dt @ W_dt.T.astype(np.float64) + b_dt
        delta = np.where(da > 30, da, np.log1p(np.exp(np.minimum(da, 30.0))))
        A = -np.exp(A_log.astype(np.float64))
        Ab = np.exp(delta * A)
        Bb = delta * Bm
        h = np.zeros((S, 128))
        hp = np.zeros(128)
        for t in range(S):
            hp = Ab[t] * hp + Bb[t]
            h[t] = hp
        y = (Cm * h).sum(-1)
        yw[b] = (y * w_gate[0]).astype(np.float32)
        z[b * S:(b + 1) * S] = (xi / (1 + np.exp(-xi))).astype(np.float32)
    return z, yw


def emit_l1_attn_out(p, nc, bufs, consts, zT_dram, yw_dram, WoutT, a1_dram):
    DI = 2048
    ywt = bufs.sb("ao_ywt", (64, 128), F32)
    ywc = bufs.sb("ao_ywc", (128, 64), F32)
    psy = bufs.ps("ao_psy", (128, 128), F32)
    p.dma(ywt[:, :], yw_dram.rearrange("b (ch p) -> (b ch) p", p=128))
    p.op("tensor", lambda e: e.transpose(psy[0:128, 0:64], ywt[:, :],
                                         consts["identf"][0:64, 0:64]))
    p.op("vector", lambda e: e.tensor_copy(ywc[:, :], psy[0:128, 0:64]))
    wt = [bufs.sb(f"ao_w{k}", (128, E), BF16) for k in range(16)]
    for kt in range(16):
        p.dma(wt[kt][:, :], WoutT[kt * 128:(kt + 1) * 128, :])
    zt = bufs.sb("ao_zt", (128, DI), BF16)
    res = bufs.sb("ao_res", (128, E), F32)
    ps = [bufs.ps(f"ao_ps{n}", (128, 512), F32) for n in range(2)]
    for i in range(NT):
        for kt in range(16):
            p.dma(zt[:, kt * 128:(kt + 1) * 128],
                  zT_dram[kt * 128:(kt + 1) * 128, i * 128:(i + 1) * 128])
        for n in range(2):
            for kt in range(16):
                p.op("tensor", lambda e, n=n, kt=kt: e.matmul(
                    ps[n][:, :], zt[:, kt * 128:(kt + 1) * 128],
                    wt[kt][:, n * 512:(n + 1) * 512],
                    start=(kt == 0), stop=(kt == 15)))
            p.op("vector", lambda e, n=n, i=i: e.tensor_scalar(
                res[:, n * 512:(n + 1) * 512], ps[n][:, :],
                ywc[:, i:i + 1], None, ALU.mult))
        p.dma(a1_dram[i * 128:(i + 1) * 128, :], res[:, :])



def load_bias_cols_partial(p, nc, bufs, dram_vec, lo, n, name):
    nf = (n + 127) // 128
    t = bufs.sb(name, (128, nf), F32)
    full = (n // 128) * 128
    src = dram_vec[lo:lo + full].rearrange("(f p) -> p f", p=128)
    p.dma(t[:, 0:n // 128], src, allow_slow_non_contiguous=True)
    if n % 128:
        src2 = dram_vec[lo + full:lo + n].rearrange("(p one) -> p one", one=1)
        p.dma(t[0:n % 128, nf - 1:nf], src2, allow_slow_non_contiguous=True)
    return t


def emit_l1_ffn(p, nc, bufs, consts, xn1_dram, wqkv, qs_d, ks_d, wo,
                lng_d, lnb_d, w1, b1_d, w2, b2_d, xf_dram, dbg=None):
    HEADS, DH, INNER, HID = 8, 64, 512, 2730
    W = 512
    NW = S // W
    NZ = (HID + 127) // 128
    idb = consts["identb"]
    hT_dram = nc.dram_tensor("ffn_hT", [E, T], BF16, kind="Internal").ap()

    h_t = bufs.sb("f_h", (128, E), F32)
    hb = bufs.sb("f_hb", (128, E), BF16)
    qsr = bufs.sb("f_qsr", (1, 2 * DH), F32)
    ksr = bufs.sb("f_ksr", (1, 2 * DH), F32)
    p.dma(qsr[:, :], qs_d.rearrange("l d -> (l d)").rearrange("(a n) -> a n", a=1))
    p.dma(ksr[:, :], ks_d.rearrange("l d -> (l d)").rearrange("(a n) -> a n", a=1))
    p.op("vector", lambda e: e.tensor_scalar_mul(qsr[:, :], qsr[:, :], 8.0))
    affr = bufs.sb("f_affr", (1, 3 * E), F32)
    b1a = [load_bias_cols_partial(p, nc, bufs, b1_d, l * 2 * HID, HID, f"f_b1a{l}")
           for l in range(2)]
    b1g = [load_bias_cols_partial(p, nc, bufs, b1_d, l * 2 * HID + HID, HID,
                                  f"f_b1g{l}") for l in range(2)]

    wq_sb = [bufs.sb(f"f_wq{j}", (128, E), BF16) for j in range(2)]
    hTcs = [[bufs.sb(f"f_hTc{j}_{k}", (128, 512), BF16) for k in range(8)]
            for j in range(2)]
    qkvT = [bufs.sb(f"f_qkvT{j}", (128, S), BF16) for j in range(12)]
    qf = bufs.sb("f_qf", (128, 512), F32)
    sq = bufs.sb("f_sq", (128, 512), F32)
    n2r = [bufs.sb(f"f_n2r{h}", (1, 512), F32) for h in range(2)]
    vtok = bufs.sb("f_vtok", (128, 8 * 64), BF16)
    pa = bufs.sb("f_pa", (128, 512), BF16)
    pb = bufs.sb("f_pb", (128, 128), BF16)
    pt = bufs.sb("f_pt", (128, 5 * 128), BF16)
    oTw = [bufs.sb(f"f_oTw{j}", (128, 512), BF16) for j in range(4)]
    den = bufs.sb("f_den", (128, 4), F32)
    denr = bufs.sb("f_denr", (1, 128), F32)
    bca = bufs.sb("f_bca", (128, 128), F32)
    wo_sb = [bufs.sb(f"f_wo{k}", (128, E), BF16) for k in range(4)]
    xres = bufs.sb("f_xres", (128, E), F32)
    zt = [bufs.sb(f"f_z{m}", (128, 512), BF16) for m in range(NZ)]
    hnT = [bufs.sb(f"f_hnT{k}", (128, 512), BF16) for k in range(8)]
    w1sb = [bufs.sb(f"f_w1sb{j}", (128, 1024), BF16) for j in range(3)]
    w2sb = [bufs.sb(f"f_w2sb{j}", (128, 1024), BF16) for j in range(3)]
    gg = [bufs.sb(f"f_gg{j}", (128, 512), F32) for j in range(2)]
    gbc = bufs.sb("f_gbc", (128, 2 * E), F32)

    dbgS_sb = bufs.sb("f_dbgS", (128, 128), F32) if dbg is not None else None
    ps1 = bufs.ps("f_ps1", (128, 512), F32)
    ps2 = bufs.ps("f_ps2", (128, 512), F32)
    psS = bufs.ps("f_psS", (128, 512), F32)
    psB = bufs.ps("f_psB", (128, 128), F32)
    pst = bufs.ps("f_tr_ps0", (128, 128), BF16)
    psmini = bufs.ps("f_psmini", (128, 512), F32)
    psav = bufs.ps("f_psav", (128, 128), F32)

    for l in range(2):
        src = xn1_dram if l == 0 else xf_dram
        for i in range(NT):
            p.dma(h_t[:, 0:512], src[i * 128:(i + 1) * 128, 0:512])
            if i % (S // 128) == 0:
                p.op("vector", lambda e: e.memset(h_t[0:1, 512:1024], 0.0))
                p.dma(h_t[1:128, 512:1024],
                      src[i * 128:(i + 1) * 128 - 1, 512:1024])
            else:
                p.dma(h_t[:, 512:1024],
                      src[i * 128 - 1:(i + 1) * 128 - 1, 512:1024])
            p.op("vector", lambda e: e.tensor_copy(hb[:, :], h_t[:, :]))
            emit_transpose_store(p, nc, bufs, hb, hT_dram, 0, i * 128, 8, idb,
                                 pref="f_")
        for b in range(BL):
            for c in range(4):
                hTc = hTcs[c % 2]
                for k in range(8):
                    p.dma(hTc[k][:, :], hT_dram[k * 128:(k + 1) * 128,
                                                b * S + c * 512:b * S + (c + 1) * 512])
                for fo in range(12):
                    wq_ = wq_sb[fo % 2]
                    for kt in range(8):
                        p.dma(wq_[:, kt * 128:(kt + 1) * 128],
                              wqkv[l * E + kt * 128:l * E + (kt + 1) * 128,
                                   fo * 128:(fo + 1) * 128])
                    psq_ = [ps1, psS][fo % 2]
                    for kt in range(8):
                        p.op("tensor", lambda e, kt=kt, wq_=wq_, psq_=psq_: e.matmul(
                            psq_[:, :], wq_[:, kt * 128:(kt + 1) * 128],
                            hTc[kt][:, :], start=(kt == 0), stop=(kt == 7)))
                    if fo >= 8:
                        p.op("vector", lambda e, fo=fo, c=c, psq_=psq_: e.tensor_copy(
                            qkvT[fo][:, c * 512:(c + 1) * 512], psq_[:, :]))
                        continue
                    p.op("vector", lambda e, psq_=psq_: e.tensor_copy(qf[:, :], psq_[:, :]))
                    isq = fo < 4
                    gr = qsr if isq else ksr
                    rA = consts["rot_Aq"] if isq else consts["rot_Ak"]
                    rB = consts["rot_Bq"] if isq else consts["rot_Bk"]
                    p.op("scalar", lambda e: e.activation(sq[:, :], qf[:, :],
                                                          AF.Square))
                    for hh in range(2):
                        p.op("tensor", lambda e, hh=hh: e.matmul(
                            psmini[0:1, :], consts["cols"][:, 1 + hh:2 + hh],
                            sq[:, :], start=True, stop=True))
                        p.op("vector", lambda e, hh=hh: e.tensor_copy(
                            n2r[hh][:, :], psmini[0:1, :]))
                        p.op("scalar", lambda e, hh=hh: e.activation(
                            n2r[hh][:, :], n2r[hh][:, :], AF.Sqrt))
                        p.op("vector", lambda e, hh=hh: e.tensor_scalar_max(
                            n2r[hh][:, :], n2r[hh][:, :], 1e-12))
                        p.op("vector", lambda e, hh=hh: e.reciprocal(
                            n2r[hh][:, :], n2r[hh][:, :]))
                        p.op("tensor", lambda e, hh=hh, gr=gr, l=l: e.matmul(
                            ps2[hh * 64:hh * 64 + 64, :],
                            gr[0:1, l * 64:(l + 1) * 64], n2r[hh][:, :],
                            start=True, stop=True))
                    cs = slice(c * 512, (c + 1) * 512)
                    p.op("vector", lambda e: e.tensor_tensor(
                        ps1[:, :], qf[:, :], ps2[:, :], ALU.mult))
                    p.op("vector", lambda e, cs=cs, rA=rA: e.tensor_tensor(
                        sq[:, :], ps1[:, :], rA[:, cs], ALU.mult))
                    for qr in range(4):
                        sw = (qr + 1) if qr % 2 == 0 else (qr - 1)
                        p.op("vector", lambda e, qr=qr, sw=sw, cs=cs, rB=rB: e.tensor_tensor(
                            qf[qr * 32:(qr + 1) * 32, :],
                            ps1[sw * 32:(sw + 1) * 32, :],
                            rB[qr * 32:(qr + 1) * 32, cs], ALU.mult))
                    p.op("vector", lambda e, fo=fo, cs=cs: e.tensor_tensor(
                        qkvT[fo][:, cs], sq[:, :], qf[:, :], ALU.add))
            for w in range(NW):
                jt0 = max(0, (w - 1) * 4)
                njt = 4 if w == 0 else 8
                for j in range(4):
                    p.op("vector", lambda e, j=j: e.memset(oTw[j][:, :], 0.0))
                for h in range(HEADS):
                    ft, base = h // 2, (h % 2) * 64
                    qT = qkvT[ft]
                    kT = qkvT[4 + ft]
                    vT = qkvT[8 + ft]
                    for s_ in range(njt):
                        jt = jt0 + s_
                        p.op("tensor", lambda e, jt=jt, vT=vT, base=base: e.transpose(
                            pst[0:128, 0:64],
                            vT[base:base + 64, jt * 128:(jt + 1) * 128],
                            consts["identb64"][base:base + 64, 0:64]))
                        p.op("vector", lambda e, s_=s_: e.tensor_copy(
                            vtok[:, s_ * 64:(s_ + 1) * 64], pst[0:128, 0:64]))
                    for r in range(4):
                        q0 = w * W + r * 128
                        if w == 0:
                            kj0l = 0
                            nk = (r + 1) * 128
                        else:
                            kj0l = r * 128
                            nk = 640
                        kjg = jt0 * 128 + kj0l
                        nk1 = min(nk, 512)
                        p.op("tensor", lambda e, base=base, q0=q0, kjg=kjg, nk1=nk1, qT=qT, kT=kT: e.matmul(
                            psS[:, 0:nk1], qT[base:base + 64, q0:q0 + 128],
                            kT[base:base + 64, kjg:kjg + nk1],
                            start=True, stop=False, skip_group_check=True))
                        if nk > 512:
                            p.op("tensor", lambda e, base=base, q0=q0, kjg=kjg, qT=qT, kT=kT: e.matmul(
                                psB[:, :], qT[base:base + 64, q0:q0 + 128],
                                kT[base:base + 64, kjg + 512:kjg + 640],
                                start=True, stop=False, skip_group_check=True))
                        if w > 0:
                            p.op("tensor", lambda e: e.matmul(
                                psS[:, 0:128], idb[:, :], consts["maskLb"][:, :],
                                start=False, stop=True, skip_group_check=True))
                            p.op("tensor", lambda e: e.matmul(
                                psB[:, :], idb[:, :], consts["maskRb"][:, :],
                                start=False, stop=True, skip_group_check=True))
                        else:
                            p.op("tensor", lambda e, r=r: e.matmul(
                                psS[:, r * 128:(r + 1) * 128], idb[:, :],
                                consts["maskRb"][:, :],
                                start=False, stop=True, skip_group_check=True))
                        if (dbg is not None and "S00" in dbg and l == 0
                                and b == 0 and w == 0 and h == 0 and r == 0):
                            p.op("vector", lambda e: e.tensor_copy(
                                dbgS_sb[:, :], psS[:, 0:128]))
                            p.dma(dbg["S00"][:, :], dbgS_sb[:, :])
                        p.op("vector", lambda e, nk1=nk1: e.reduce_max(
                            den[:, 0:1], psS[:, 0:nk1], AX.X))
                        if nk > 512:
                            p.op("vector", lambda e: e.reduce_max(
                                den[:, 1:2], psB[:, :], AX.X))
                            p.op("vector", lambda e: e.tensor_max(
                                den[:, 0:1], den[:, 0:1], den[:, 1:2]))
                        p.op("vector", lambda e: e.tensor_scalar_mul(
                            den[:, 2:3], den[:, 0:1], -1.0))
                        p.op("scalar", lambda e, nk1=nk1: e.activation(
                            pa[:, 0:nk1], psS[:, 0:nk1], AF.Exp,
                            bias=den[:, 2:3], accum_out=den[:, 0:1]))
                        if nk > 512:
                            p.op("scalar", lambda e: e.activation(
                                pb[:, :], psB[:, :], AF.Exp,
                                bias=den[:, 2:3], accum_out=den[:, 1:2]))
                            p.op("vector", lambda e: e.tensor_add(
                                den[:, 0:1], den[:, 0:1], den[:, 1:2]))
                        p.op("vector", lambda e: e.reciprocal(
                            den[:, 3:4], den[:, 0:1]))
                        ntile = nk // 128
                        for s_ in range(ntile):
                            srcP = pa[:, (s_ % 4) * 128:(s_ % 4) * 128 + 128] \
                                if s_ < 4 else pb[:, 0:128]
                            p.op("tensor", lambda e, srcP=srcP: e.transpose(
                                pst[:, :], srcP, idb[:, :]))
                            p.op("vector", lambda e, s_=s_: e.tensor_copy(
                                pt[:, s_ * 128:(s_ + 1) * 128], pst[:, :]))
                        p.op("tensor", lambda e: e.transpose(
                            psmini[0:1, 0:128], den[:, 3:4],
                            consts["identf"][:, :]))
                        p.op("vector", lambda e: e.tensor_copy(
                            denr[:, :], psmini[0:1, 0:128]))
                        p.op("tensor", lambda e, base=base: e.matmul(
                            psmini[base:base + 64, 0:128],
                            consts["one_row"][0:1, 0:64], denr[:, :],
                            start=True, stop=True))
                        p.op("vector", lambda e, base=base: e.tensor_copy(
                            bca[base:base + 64, :],
                            psmini[base:base + 64, 0:128]))
                        for s_ in range(ntile):
                            sloc = (kj0l // 128) + s_ if w > 0 else s_
                            p.op("tensor", lambda e, s_=s_, sloc=sloc, base=base, ntile=ntile: e.matmul(
                                psav[base:base + 64, :],
                                vtok[:, sloc * 64:(sloc + 1) * 64],
                                pt[:, s_ * 128:(s_ + 1) * 128],
                                start=(s_ == 0), stop=(s_ == ntile - 1)))
                        p.op("vector", lambda e, ft=ft, base=base, r=r: e.tensor_tensor(
                            oTw[ft][base:base + 64, r * 128:(r + 1) * 128],
                            psav[base:base + 64, :], bca[base:base + 64, :],
                            ALU.mult))
                if w == 0 and b == 0:
                    for k in range(4):
                        p.dma(wo_sb[k][:, :],
                              wo[l * 512 + k * 128:l * 512 + (k + 1) * 128, :])
                for sc in range(4):
                    gi = b * (S // 128) + w * 4 + sc
                    p.dma(xres[:, :], src[gi * 128:(gi + 1) * 128, :])
                    for n in range(2):
                        for k in range(4):
                            p.op("tensor", lambda e, n=n, k=k, sc=sc: e.matmul(
                                ps1[:, :], oTw[k][:, sc * 128:(sc + 1) * 128],
                                wo_sb[k][:, n * 512:(n + 1) * 512],
                                start=(k == 0), stop=(k == 3)))
                        p.op("vector", lambda e, n=n: e.tensor_tensor(
                            xres[:, n * 512:(n + 1) * 512], ps1[:, :],
                            xres[:, n * 512:(n + 1) * 512], ALU.add))
                    p.dma(xf_dram[gi * 128:(gi + 1) * 128, :], xres[:, :])
        if dbg is not None and l == 0:
            for nm, tile in (("q0", qkvT[0]), ("k0", qkvT[4]), ("v0", qkvT[8])):
                if nm in dbg:
                    p.dma(dbg[nm][:, :], tile[:, :])
            if "xattn0" in dbg:
                for i in range(NT):
                    p.dma(dbg["xattn0"][i * 128:(i + 1) * 128, :],
                          xf_dram[i * 128:(i + 1) * 128, :])
        p.dma(affr[0:1, 0:E], lng_d[l:l + 1, :])
        p.dma(affr[0:1, E:2 * E], lnb_d[l:l + 1, :])
        p.dma(affr[0:1, 2 * E:3 * E], b2_d[l:l + 1, :])
        for n in range(2):
            p.op("tensor", lambda e, n=n: e.matmul(
                ps1[:, :], consts["one_row"][:, :],
                affr[0:1, n * 512:(n + 1) * 512], start=True, stop=True))
            p.op("vector", lambda e, n=n: e.tensor_copy(
                gbc[:, n * 512:(n + 1) * 512], ps1[:, :]))
            p.op("tensor", lambda e, n=n: e.matmul(
                ps1[:, :], consts["one_row"][:, :],
                affr[0:1, E + n * 512:E + (n + 1) * 512], start=True, stop=True))
            p.op("vector", lambda e, n=n: e.tensor_copy(
                gbc[:, E + n * 512:E + (n + 1) * 512], ps1[:, :]))
        for g in range(T // 512):
            for sc in range(4):
                gi = g * 4 + sc
                p.dma(h_t[:, 0:512], xf_dram[gi * 128:(gi + 1) * 128, 0:512])
                if gi % (S // 128) == 0:
                    p.op("vector", lambda e: e.memset(h_t[0:1, 512:1024], 0.0))
                    p.dma(h_t[1:128, 512:1024],
                          xf_dram[gi * 128:(gi + 1) * 128 - 1, 512:1024])
                else:
                    p.dma(h_t[:, 512:1024],
                          xf_dram[gi * 128 - 1:(gi + 1) * 128 - 1, 512:1024])
                emit_layernorm_tile(p, nc, bufs, h_t, h_t, 1e-5, pref="f_",
                                    sq=xres)
                p.op("vector", lambda e: e.tensor_tensor(
                    h_t[:, :], h_t[:, :], gbc[:, 0:E], ALU.mult))
                p.op("vector", lambda e: e.tensor_tensor(
                    h_t[:, :], h_t[:, :], gbc[:, E:2 * E], ALU.add))
                p.op("vector", lambda e: e.tensor_copy(hb[:, :], h_t[:, :]))
                for f in range(8):
                    p.op("tensor", lambda e, f=f: e.transpose(
                        pst[:, :], hb[:, f * 128:(f + 1) * 128], idb[:, :]))
                    p.op("vector", lambda e, f=f, sc=sc: e.tensor_copy(
                        hnT[f][:, sc * 128:(sc + 1) * 128], pst[:, :]))
            for m in range(NZ):
                mrows = min(128, HID - m * 128)
                for part, psx in ((0, [ps1, psS][m % 2]),
                                  (1, [ps2, psmini][m % 2])):
                    col0 = l * 2 * HID + part * HID + m * 128
                    for kt in range(8):
                        wb_ = w1sb[(m * 2 + part) % 3]
                        p.dma(wb_[:, kt * 128:kt * 128 + mrows],
                              w1[kt * 128:(kt + 1) * 128, col0:col0 + mrows])
                        p.op("tensor", lambda e, kt=kt, mrows=mrows, psx=psx, wb_=wb_: e.matmul(
                            psx[0:mrows, :], wb_[:, kt * 128:kt * 128 + mrows],
                            hnT[kt][:, :], start=(kt == 0), stop=(kt == 7)))
                psa_, psg_ = [ps1, psS][m % 2], [ps2, psmini][m % 2]
                gg_ = gg[m % 2]
                p.op("scalar", lambda e, m=m, mrows=mrows, l=l, psg_=psg_, gg_=gg_: e.activation(
                    gg_[0:mrows, :], psg_[0:mrows, :], AF.Gelu,
                    bias=b1g[l][0:mrows, m:m + 1]))
                p.op("vector", lambda e, m=m, mrows=mrows, l=l, psa_=psa_, gg_=gg_: e.scalar_tensor_tensor(
                    zt[m][0:mrows, :], psa_[0:mrows, :], b1a[l][0:mrows, m:m + 1],
                    gg_[0:mrows, :], ALU.add, ALU.mult))
            for sc in range(4):
                gi = g * 4 + sc
                p.dma(xres[:, :], xf_dram[gi * 128:(gi + 1) * 128, :])
                for n in range(2):
                    psw_ = [ps1, psS][(sc * 2 + n) % 2]
                    for m in range(NZ):
                        mrows = min(128, HID - m * 128)
                        wb_ = w2sb[m % 3]
                        p.dma(wb_[0:mrows, n * 512:(n + 1) * 512],
                              w2[l * HID + m * 128:l * HID + m * 128 + mrows,
                                 n * 512:(n + 1) * 512])
                        p.op("tensor", lambda e, m=m, mrows=mrows, sc=sc, n=n, wb_=wb_, psw_=psw_: e.matmul(
                            psw_[:, :], zt[m][0:mrows, sc * 128:(sc + 1) * 128],
                            wb_[0:mrows, n * 512:(n + 1) * 512],
                            start=(m == 0), stop=False, skip_group_check=True))
                    p.op("tensor", lambda e, n=n, psw_=psw_: e.matmul(
                        psw_[:, :], consts["one_row"][:, :],
                        affr[0:1, 2 * E + n * 512:2 * E + (n + 1) * 512],
                        start=False, stop=True, skip_group_check=True))
                    p.op("vector", lambda e, n=n, psw_=psw_: e.tensor_tensor(
                        xres[:, n * 512:(n + 1) * 512], psw_[:, :],
                        xres[:, n * 512:(n + 1) * 512], ALU.add))
                p.dma(xf_dram[gi * 128:(gi + 1) * 128, :], xres[:, :])
        if dbg is not None and l == 0 and "xf0" in dbg:
            for i in range(NT):
                p.dma(dbg["xf0"][i * 128:(i + 1) * 128, :],
                      xf_dram[i * 128:(i + 1) * 128, :])


def emit_l0_ffn(p, nc, bufs, consts, xn_dram, wdqT, bias_d, f0_dram):
    idb = consts["identb"]
    wt = [bufs.sb(f"lf_w{k}", (128, E), BF16) for k in range(8)]
    for kt in range(8):
        p.dma(wt[kt][:, :], wdqT[kt * 128:(kt + 1) * 128, :])
    brow = bufs.sb("lf_brow", (1, E), F32)
    p.dma(brow[:, :], bias_d[0:E].rearrange("(one n) -> one n", one=1))
    a_r = [bufs.sb(f"lf_a{j}", (128, E), F32) for j in range(2)]
    ab_r = [bufs.sb(f"lf_ab{j}", (128, E), BF16) for j in range(2)]
    xdT_r = [bufs.sb(f"lf_xdT{j}", (128, E), BF16) for j in range(2)]
    res_r = [bufs.sb(f"lf_res{j}", (128, E), F32) for j in range(2)]
    pst_r = [bufs.ps(f"lf_pst{j}", (128, 128), BF16) for j in range(2)]
    ps_all = [bufs.ps(f"lf_ps{n}", (128, 512), F32) for n in range(4)]
    for i in range(NT):
        j = i % 2
        a, ab, xdT, res, pst = a_r[j], ab_r[j], xdT_r[j], res_r[j], pst_r[j]
        ps = ps_all[j * 2:j * 2 + 2]
        p.dma(a[:, :], xn_dram[i * 128:(i + 1) * 128, :])
        emit_quant_tile(p, nc, bufs, a, a, pref="lf_q%d_" % j)
        p.op("vector", lambda e, ab=ab, a=a: e.tensor_copy(ab[:, :], a[:, :]))
        for f in range(8):
            p.op("tensor", lambda e, f=f, pst=pst, ab=ab: e.transpose(
                pst[:, :], ab[:, f * 128:(f + 1) * 128], idb[:, :]))
            p.op("vector", lambda e, f=f, xdT=xdT, pst=pst: e.tensor_copy(
                xdT[:, f * 128:(f + 1) * 128], pst[:, :]))
        for n in range(2):
            for kt in range(8):
                p.op("tensor", lambda e, n=n, kt=kt, ps=ps, xdT=xdT: e.matmul(
                    ps[n][:, :], xdT[:, kt * 128:(kt + 1) * 128],
                    wt[kt][:, n * 512:(n + 1) * 512],
                    start=(kt == 0), stop=False, skip_group_check=True))
            p.op("tensor", lambda e, n=n, ps=ps: e.matmul(
                ps[n][:, :], consts["one_row"][:, :],
                brow[:, n * 512:(n + 1) * 512],
                start=False, stop=True, skip_group_check=True))
            p.op("vector", lambda e, n=n, ps=ps, res=res: e.tensor_copy(
                res[:, n * 512:(n + 1) * 512], ps[n][:, :]))
        p.dma(f0_dram[i * 128:(i + 1) * 128, :], res[:, :])


def emit_prep(p, nc, bufs, consts, ins, outs):
    idb = consts["identb"]
    for nm in ("qw", "kw", "vw", "outw"):
        emit_cast_bf16(p, nc, bufs, ins[nm], outs[nm + "_bf"], E, E)
    emit_cast_bf16(p, nc, bufs, ins["wqkv"], outs["wqkv_bf"], 2 * E, 1536)
    emit_cast_bf16(p, nc, bufs, ins["wo"], outs["wo_bf"], E, E)
    emit_cast_bf16(p, nc, bufs, ins["w1"], outs["w1_bf"], E, 2 * 5460)
    emit_cast_bf16(p, nc, bufs, ins["w2"], outs["w2_bf"], 5460, E)
    emit_transpose_bf16(p, nc, bufs, ins["ffnw"], outs["wdqT"], E, E, idb,
                        quant=True, pref="pq_")
    emit_transpose_bf16(p, nc, bufs, ins["W_in"], outs["WinT"], 2048, E, idb,
                        pref="p1_")
    emit_transpose_bf16(p, nc, bufs, ins["W_x"], outs["WxT"], 320, 2048, idb,
                        pref="p2_")
    emit_transpose_bf16(p, nc, bufs, ins["W_dt"], outs["WdtT"], 128, 64, idb,
                        pref="p3_")
    emit_transpose_bf16(p, nc, bufs, ins["W_out"], outs["WoutT"], E, 2048, idb,
                        pref="p4_")


from contextlib import ExitStack


def _dram(nc, name, shape, dt, kind):
    return nc.dram_tensor(name, list(shape), dt, kind=kind).ap()


def _run_launch(build_fn, in_specs, out_specs, in_arrays_per_core, cneed,
                cvals):
    nc = bass.Bass("TRN2", target_bir_lowering=False)
    ins = {k: _dram(nc, k, sh, dt, "ExternalInput")
           for k, (sh, dt) in in_specs.items()}
    outs = {k: _dram(nc, k, sh, dt, "ExternalOutput")
            for k, (sh, dt) in out_specs.items()}
    p = P(nc)
    with ExitStack() as stk:
        bufs = Bufs(nc, stk)
        consts = load_constants(p, nc, bufs, cvals, cneed)
        build_fn(p, nc, bufs, consts, ins, outs)
        p.emit()
    in_maps = []
    for c in range(NCORES):
        m = dict(in_arrays_per_core[c])
        for k in cneed:
            m["c_" + k] = cvals[k]
        in_maps.append(m)
    r = run_bass_kernel_spmd(nc, in_maps, core_ids=list(range(NCORES)))
    return r.results


def kernel(**inputs):
    cvals = make_constants()
    x_full = np.ascontiguousarray(
        inputs["L0_attn_x"].reshape(32 * S, E).astype(np.float32))
    shards = [np.ascontiguousarray(x_full[c * T:(c + 1) * T])
              for c in range(NCORES)]

    f32c = np.ascontiguousarray
    wq = f32c(inputs["L0_attn_q_w"]); wk = f32c(inputs["L0_attn_k_w"])
    wv = f32c(inputs["L0_attn_v_w"]); wout = f32c(inputs["L0_attn_out_w"])
    ffnw = f32c(inputs["L0_ffn_weight"])
    W_in = f32c(inputs["L1_attn_W_in"])
    conv_w = f32c(inputs["L1_attn_conv_w"].reshape(2048, 4))
    W_x = f32c(inputs["L1_attn_W_x"])
    W_dt = f32c(inputs["L1_attn_W_dt"])
    wqkv = f32c(inputs["L1_ffn_wqkv"].reshape(2 * E, 1536))
    wo = f32c(inputs["L1_ffn_wo"].reshape(E, E))
    w1 = f32c(np.transpose(inputs["L1_ffn_w1"], (1, 0, 2)).reshape(E, 2 * 5460))
    w2 = f32c(inputs["L1_ffn_w2"].reshape(2 * 2730, E))
    W_out = f32c(inputs["L1_attn_W_out"])

    BF16s = lambda sh: (sh, BF16)
    F32s = lambda sh: (sh, F32)

    prep_in = {"qw": F32s([E, E]), "kw": F32s([E, E]), "vw": F32s([E, E]),
               "outw": F32s([E, E]), "ffnw": F32s([E, E]),
               "W_in": F32s([2048, E]), "W_x": F32s([320, 2048]),
               "W_dt": F32s([128, 64]), "W_out": F32s([E, 2048]),
               "wqkv": F32s([2 * E, 1536]), "wo": F32s([E, E]),
               "w1": F32s([E, 2 * 5460]), "w2": F32s([5460, E])}
    prep_out = {"qw_bf": BF16s([E, E]), "kw_bf": BF16s([E, E]),
                "vw_bf": BF16s([E, E]), "outw_bf": BF16s([E, E]),
                "wdqT": BF16s([E, E]), "WinT": BF16s([E, 2048]),
                "WxT": BF16s([2048, 320]), "WdtT": BF16s([64, 128]),
                "WoutT": BF16s([2048, E]), "wqkv_bf": BF16s([2 * E, 1536]),
                "wo_bf": BF16s([E, E]), "w1_bf": BF16s([E, 2 * 5460]),
                "w2_bf": BF16s([5460, E])}
    prep_arrs = {"qw": wq, "kw": wk, "vw": wv, "outw": wout, "ffnw": ffnw,
                 "W_in": W_in, "W_x": W_x, "W_dt": W_dt, "W_out": W_out,
                 "wqkv": wqkv, "wo": wo, "w1": w1, "w2": w2}
    rp = _run_launch(
        lambda p, nc, bufs, consts, ins, outs: emit_prep(p, nc, bufs, consts,
                                                         ins, outs),
        prep_in, prep_out, [prep_arrs] * NCORES, {"identb", "one_row"}, cvals)

    r2 = _run_launch(
        lambda p, nc, bufs, consts, ins, outs: emit_ln0(
            p, nc, bufs, consts, ins["x"], outs["xn"], outs["xnT"]),
        {"x": F32s([T, E])},
        {"xn": F32s([T, E]), "xnT": BF16s([E, T])},
        [{"x": shards[c]} for c in range(NCORES)], {"identb"}, cvals)

    r3 = _run_launch(
        lambda p, nc, bufs, consts, ins, outs: emit_l0_attn(
            p, nc, bufs, consts, ins["xnT"], ins["qw"], ins["kw"], ins["vw"],
            ins["qb"], ins["kb"], ins["vb"], ins["proj"], outs["attnT"]),
        {"xnT": BF16s([E, T]), "qw": BF16s([E, E]), "kw": BF16s([E, E]),
         "vw": BF16s([E, E]), "qb": F32s([E]), "kb": F32s([E]),
         "vb": F32s([E]), "proj": F32s([256, 64])},
        {"attnT": BF16s([E, T])},
        [{"xnT": r2[c]["xnT"], "qw": rp[c]["qw_bf"], "kw": rp[c]["kw_bf"],
          "vw": rp[c]["vw_bf"], "qb": inputs["L0_attn_q_b"],
          "kb": inputs["L0_attn_k_b"], "vb": inputs["L0_attn_v_b"],
          "proj": f32c(inputs["L0_attn_proj"])} for c in range(NCORES)],
        {"identb", "identf", "identb64", "triu_mult", "cols", "negone_row"},
        cvals)

    r4 = _run_launch(
        lambda p, nc, bufs, consts, ins, outs: emit_l0_post(
            p, nc, bufs, consts, ins["attnT"], ins["outw"], ins["outb"],
            ins["xn"], ins["gamma"], ins["beta"], outs["a0"]),
        {"attnT": BF16s([E, T]), "outw": BF16s([E, E]), "outb": F32s([E]),
         "xn": F32s([T, E]), "gamma": F32s([E]), "beta": F32s([E])},
        {"a0": F32s([T, E])},
        [{"attnT": r3[c]["attnT"], "outw": rp[c]["outw_bf"],
          "outb": inputs["L0_attn_out_b"], "xn": r2[c]["xn"],
          "gamma": inputs["L0_attn_gamma"], "beta": inputs["L0_attn_beta"]}
         for c in range(NCORES)], {"one_row"}, cvals)

    r5 = _run_launch(
        lambda p, nc, bufs, consts, ins, outs: emit_l0_ffn(
            p, nc, bufs, consts, ins["xn"], ins["wdqT"], ins["fb"],
            outs["f0"]),
        {"xn": F32s([T, E]), "wdqT": BF16s([E, E]), "fb": F32s([E])},
        {"f0": F32s([T, E])},
        [{"xn": r2[c]["xn"], "wdqT": rp[c]["wdqT"],
          "fb": inputs["L0_ffn_bias"]} for c in range(NCORES)],
        {"identb", "one_row"}, cvals)

    r6 = _run_launch(
        lambda p, nc, bufs, consts, ins, outs: emit_combine_ln(
            p, nc, bufs, consts, [ins["x"], ins["a0"], ins["f0"]],
            outs["x1"], outs["xn1"], outs["xn1T"]),
        {"x": F32s([T, E]), "a0": F32s([T, E]), "f0": F32s([T, E])},
        {"x1": F32s([T, E]), "xn1": F32s([T, E]), "xn1T": BF16s([E, T])},
        [{"x": shards[c], "a0": r4[c]["a0"], "f0": r5[c]["f0"]}
         for c in range(NCORES)], {"identb"}, cvals)

    r7 = _run_launch(
        lambda p, nc, bufs, consts, ins, outs: emit_l1_attn(
            p, nc, bufs, consts, ins["xn1T"], ins["WinT"], ins["convw"],
            ins["convb"], ins["WxT"], ins["WdtT"], ins["bdt"], ins["Alog"],
            ins["wg"], outs["zT"], outs["yw"]),
        {"xn1T": BF16s([E, T]), "WinT": BF16s([E, 2048]),
         "convw": F32s([2048, 4]), "convb": F32s([2048]),
         "WxT": BF16s([2048, 320]), "WdtT": BF16s([64, 128]),
         "bdt": F32s([128]), "Alog": F32s([128]), "wg": F32s([1])},
        {"zT": BF16s([2048, T]), "yw": F32s([BL, S])},
        [{"xn1T": r6[c]["xn1T"], "WinT": rp[c]["WinT"], "convw": conv_w,
          "convb": inputs["L1_attn_conv_b"], "WxT": rp[c]["WxT"],
          "WdtT": rp[c]["WdtT"], "bdt": inputs["L1_attn_b_dt"],
          "Alog": inputs["L1_attn_A_log"], "wg": inputs["L1_attn_w_gate"]}
         for c in range(NCORES)], {"cols", "identf"}, cvals)

    r8 = _run_launch(
        lambda p, nc, bufs, consts, ins, outs: emit_l1_attn_out(
            p, nc, bufs, consts, ins["zT"], ins["yw"], ins["WoutT"],
            outs["a1"]),
        {"zT": BF16s([2048, T]), "yw": F32s([BL, S]),
         "WoutT": BF16s([2048, E])},
        {"a1": F32s([T, E])},
        [{"zT": r7[c]["zT"], "yw": r7[c]["yw"], "WoutT": rp[c]["WoutT"]}
         for c in range(NCORES)], {"identf"}, cvals)

    r9 = _run_launch(
        lambda p, nc, bufs, consts, ins, outs: emit_l1_ffn(
            p, nc, bufs, consts, ins["xn1"], ins["wqkv"], ins["qs"],
            ins["ks"], ins["wo"], ins["lng"], ins["lnb"], ins["w1"],
            ins["b1"], ins["w2"], ins["b2"], outs["xf"]),
        {"xn1": F32s([T, E]), "wqkv": BF16s([2 * E, 1536]),
         "qs": F32s([2, 64]), "ks": F32s([2, 64]), "wo": BF16s([E, E]),
         "lng": F32s([2, E]), "lnb": F32s([2, E]),
         "w1": BF16s([E, 2 * 5460]), "b1": F32s([2 * 5460]),
         "w2": BF16s([5460, E]), "b2": F32s([2, E])},
        {"xf": F32s([T, E])},
        [{"xn1": r6[c]["xn1"], "wqkv": rp[c]["wqkv_bf"],
          "qs": f32c(inputs["L1_ffn_q_scale"]),
          "ks": f32c(inputs["L1_ffn_k_scale"]), "wo": rp[c]["wo_bf"],
          "lng": f32c(inputs["L1_ffn_ln_g"]), "lnb": f32c(inputs["L1_ffn_ln_b"]),
          "w1": rp[c]["w1_bf"], "b1": f32c(inputs["L1_ffn_b1"].reshape(-1)),
          "w2": rp[c]["w2_bf"], "b2": f32c(inputs["L1_ffn_b2"])}
         for c in range(NCORES)],
        {"identb", "identf", "identb64", "maskLb", "maskRb",
         "rot_Aq", "rot_Bq", "rot_Ak", "rot_Bk", "cols", "one_row"}, cvals)

    r10 = _run_launch(
        lambda p, nc, bufs, consts, ins, outs: emit_sum_final(
            p, nc, bufs, [ins["x1"], ins["a1"], ins["xf"]], outs["out"]),
        {"x1": F32s([T, E]), "a1": F32s([T, E]), "xf": F32s([T, E])},
        {"out": F32s([T, E])},
        [{"x1": r6[c]["x1"], "a1": r8[c]["a1"], "xf": r9[c]["xf"]}
         for c in range(NCORES)], set(), cvals)

    out = np.concatenate([r10[c]["out"] for c in range(NCORES)], axis=0)
    return out.reshape(32, S, E).astype(np.float32)


class _RecInst:
    def then_inc(self, *a, **k):
        return self


class _Recorder:

    def __init__(self):
        self.call = None

    def __getattr__(self, name):
        def f(*args, **kwargs):
            assert self.call is None, "op fn must emit exactly one instruction"
            self.call = (name, args, kwargs)
            return _RecInst()
        return f


def _ap_name(a):
    try:
        return a.tensor.name
    except Exception:
        return None


def _ap_is_psum(a):
    try:
        return "PSUM" in str(a.space).upper() or "PSUM" in str(
            getattr(a.tensor, "space", "")).upper()
    except Exception:
        return False


class P2:
    ENGINES = ("sync", "scalar", "vector", "tensor", "gpsimd")
    DMAQS = ("sync", "gpsimd")

    def __init__(self, nc, same_engine_edges=True):
        self.nc = nc
        self.ops = []
        self.last_w = {}
        self.readers = {}
        self.counts = {e: 0 for e in self.ENGINES}
        self.wcount = {}
        self.qof = {}
        self.see = same_engine_edges
        self.rr = 0
        self.barrier_pending = None
        self.barriers = []

    def _dep_events(self, reads, writes):
        deps = []
        for nm in reads:
            ev = self.last_w.get(nm)
            if ev:
                deps.append(ev)
        for nm in writes:
            ev = self.last_w.get(nm)
            if ev:
                deps.append(ev)
            deps.extend(self.readers.get(nm, ()))
        return deps

    def _record(self, engine, call, is_dma, queue=None):
        name, args, kwargs = call
        writes, reads = [], []
        out_kw = kwargs.get("out")
        if out_kw is not None and _ap_name(out_kw):
            writes.append(_ap_name(out_kw))
        acc = kwargs.get("accum_out")
        if acc is not None and _ap_name(acc):
            writes.append(_ap_name(acc))
        psum_reads = []
        for i, a in enumerate(args):
            nm = _ap_name(a)
            if nm is None:
                continue
            if i == 0 and out_kw is None:
                writes.append(nm)
            else:
                reads.append(nm)
                if _ap_is_psum(a):
                    psum_reads.append(nm)
        for k, v in kwargs.items():
            if k in ("out", "accum_out"):
                continue
            nm = _ap_name(v)
            if nm is not None:
                reads.append(nm)
                if _ap_is_psum(v):
                    psum_reads.append(nm)
        deps = self._dep_events(reads, writes)
        for nm in psum_reads:
            for r in self.readers.get(nm, ()):
                if r[0] == "c" and r[1] != engine:
                    deps.append(r)
        if self.barrier_pending is not None:
            deps = deps + [("b", self.barrier_pending, 0)]
        if is_dma:
            wn = writes[0]
            self.wcount[wn] = self.wcount.get(wn, 0) + 1
            self.qof[wn] = queue
            ev = ("t", wn, self.wcount[wn])
        else:
            self.counts[engine] += 1
            ev = ("c", engine, self.counts[engine])
        for nm in reads:
            self.readers.setdefault(nm, []).append(ev)
        for nm in writes:
            self.last_w[nm] = ev
            self.readers[nm] = []
        self.ops.append(dict(engine=engine, call=call, is_dma=is_dma,
                             queue=queue, deps=deps, ev=ev))

    def op(self, engine, fn, inc=1):
        rec = _Recorder()
        fn(rec)
        self._record(engine, rec.call, False)

    def dma(self, out, in_, engine=None, **kw):
        nm = _ap_name(out) or ""
        q = self.DMAQS[(hash(nm) & 0xffff) % len(self.DMAQS)]
        self._record(q, ("dma_start", (), dict(out=out, in_=in_, **kw)),
                     True, queue=q)

    def barrier(self):
        bid = len(self.barriers)
        snap = ([("c", e, self.counts[e]) for e in self.ENGINES
                 if self.counts[e] > 0]
                + [("t", nm, c) for nm, c in self.wcount.items() if c > 0])
        self.barriers.append(snap)
        self.barrier_pending = bid

    def emit(self):
        nc = self.nc
        assign = {}
        pool = {q: [] for q in self.DMAQS}
        sem_carried = {}
        qof_sem = {}
        n_sems = [0]
        covered = {}
        walk_count = {}
        barrier_tr = {}
        last_barrier_seen = [-1]

        def tsem(nm):
            if nm not in assign:
                q = self.qof[nm]
                if pool[q]:
                    si = pool[q].pop()
                else:
                    si = n_sems[0]
                    n_sems[0] += 1
                    sem_carried[si] = 0
                    qof_sem[si] = q
                assign[nm] = (si, sem_carried[si], covered.get(nm, 0))
            return assign[nm]

        def recycle_all():
            for nm, (si, base, sc) in list(assign.items()):
                sem_carried[si] = base + 16 * (walk_count.get(nm, 0) - sc)
                pool[qof_sem[si]].append(si)
            assign.clear()

        plan = []
        for oi, o in enumerate(self.ops):
            waits = {}

            def add_wait(key, val):
                if waits.get(key, 0) < val:
                    waits[key] = val

            bdep = None
            for d in o["deps"]:
                if d[0] == "b":
                    bdep = d[1]
            if bdep is not None and bdep not in barrier_tr:
                tr = []
                for (k2, s2, v2) in self.barriers[bdep]:
                    if k2 == "c":
                        tr.append((("c", s2), v2))
                    elif s2 in assign:
                        si, base, sc = assign[s2]
                        tr.append((("t", si), base + 16 * (v2 - sc)))
                barrier_tr[bdep] = tr
                for (k2, s2, v2) in self.barriers[bdep]:
                    if k2 == "t":
                        covered[s2] = max(covered.get(s2, 0), v2)
                last_barrier_seen[0] = bdep
                recycle_all()
            if bdep is not None:
                for (key, val) in barrier_tr[bdep]:
                    add_wait(key, val)
            for d in o["deps"]:
                if d[0] == "b":
                    continue
                if d[0] == "c":
                    if d[1] == o["engine"] and (d[1] == "tensor"
                                                or not self.see):
                        continue
                    add_wait(("c", d[1]), d[2])
                else:
                    nm, k = d[1], d[2]
                    if k <= covered.get(nm, 0):
                        continue
                    si, base, sc = tsem(nm)
                    add_wait(("t", si), base + 16 * (k - sc))
            if o["is_dma"]:
                wn = o["ev"][1]
                si, base, sc = tsem(wn)
                walk_count[wn] = walk_count.get(wn, 0) + 1
                inc = ("t", si)
            else:
                inc = ("c", o["engine"])
            plan.append((sorted(waits.items()), inc))
        final_c = {e: self.counts[e] for e in self.ENGINES if self.counts[e]}
        final_t = {}
        for nm, (si, base, sc) in assign.items():
            final_t[si] = max(final_t.get(si, 0),
                              base + 16 * (walk_count.get(nm, 0) - sc))
        for si, v in sem_carried.items():
            if v > 0:
                final_t[si] = max(final_t.get(si, 0), v)
        by_eng = {}
        for oi, o in enumerate(self.ops):
            by_eng.setdefault(o["engine"], []).append(oi)
        with ExitStack() as stk:
            csem = {e: stk.enter_context(nc.semaphore("s_" + e))
                    for e in self.ENGINES}
            tsems = [stk.enter_context(nc.semaphore(f"st_{i}"))
                     for i in range(n_sems[0])]
            with nc.Block() as block:
                def mk(ename):
                    def body(eng):
                        waited = {}
                        for oi in by_eng.get(ename, []):
                            o = self.ops[oi]
                            w8, inc = plan[oi]
                            for (key, val) in w8:
                                if waited.get(key, 0) >= val:
                                    continue
                                sem = (csem[key[1]] if key[0] == "c"
                                       else tsems[key[1]])
                                eng.wait_ge(sem, val)
                                waited[key] = val
                            name, args, kwargs = o["call"]
                            inst = getattr(eng, name)(*args, **kwargs)
                            if inc[0] == "t":
                                inst.then_inc(tsems[inc[1]], 16)
                            else:
                                inst.then_inc(csem[ename], 1)
                        if ename == "sync":
                            for e2, v in final_c.items():
                                eng.wait_ge(csem[e2], v)
                            for si, v in final_t.items():
                                eng.wait_ge(tsems[si], v)
                    body.__name__ = "body_" + ename
                    return body
                for ename in self.ENGINES:
                    if ename in by_eng or ename == "sync":
                        getattr(block, ename)(mk(ename))
        return len(self.ops)


USE_SCHED = True
def kernel_fused(**inputs):
    cvals = make_constants()
    x_full = np.ascontiguousarray(
        inputs["L0_attn_x"].reshape(32 * S, E).astype(np.float32))
    shards = [np.ascontiguousarray(x_full[c * T:(c + 1) * T])
              for c in range(NCORES)]
    f32c = np.ascontiguousarray

    nc = bass.Bass("TRN2", target_bir_lowering=False)
    EI, INT, EO = "ExternalInput", "Internal", "ExternalOutput"
    di = {}
    def ein(name, shape, dt):
        di[name] = _dram(nc, name, shape, dt, EI)
        return di[name]
    x_d = ein("x", [T, E], F32)
    qw_d = ein("qw", [E, E], F32); kw_d = ein("kw", [E, E], F32)
    vw_d = ein("vw", [E, E], F32); outw_d = ein("outw", [E, E], F32)
    qb_d = ein("qb", [E], F32); kb_d = ein("kb", [E], F32)
    vb_d = ein("vb", [E], F32); outb_d = ein("outb", [E], F32)
    proj_d = ein("proj", [256, 64], F32)
    gamma_d = ein("gamma", [E], F32); beta_d = ein("beta", [E], F32)
    ffnw_d = ein("ffnw", [E, E], F32); fb_d = ein("fb", [E], F32)
    Win_d = ein("W_in", [2048, E], F32)
    convw_d = ein("convw", [2048, 4], F32); convb_d = ein("convb", [2048], F32)
    Wx_d = ein("W_x", [320, 2048], F32); Wdt_d = ein("W_dt", [128, 64], F32)
    bdt_d = ein("bdt", [128], F32); Alog_d = ein("Alog", [128], F32)
    wg_d = ein("wg", [1], F32); Wout_d = ein("W_out", [E, 2048], F32)
    wqkv_d = ein("wqkv", [2 * E, 1536], F32)
    qs_d = ein("qs", [2, 64], F32); ks_d = ein("ks", [2, 64], F32)
    wo_d = ein("wo", [E, E], F32)
    lng_d = ein("lng", [2, E], F32); lnb_d = ein("lnb", [2, E], F32)
    w1_d = ein("w1", [E, 2 * 5460], F32); b1_d = ein("b1", [2 * 5460], F32)
    w2_d = ein("w2", [5460, E], F32); b2_d = ein("b2", [2, E], F32)
    out_d = _dram(nc, "out", [T, E], F32, EO)

    sc = {}
    for name, shape, dt in (
        ("qw_bf", [E, E], BF16), ("kw_bf", [E, E], BF16),
        ("vw_bf", [E, E], BF16), ("outw_bf", [E, E], BF16),
        ("wdqT", [E, E], BF16), ("WinT", [E, 2048], BF16),
        ("WxT", [2048, 320], BF16), ("WdtT", [64, 128], BF16),
        ("WoutT", [2048, E], BF16), ("wqkv_bf", [2 * E, 1536], BF16),
        ("wo_bf", [E, E], BF16), ("w1_bf", [E, 2 * 5460], BF16),
        ("w2_bf", [5460, E], BF16),
        ("xn", [T, E], F32), ("xnT", [E, T], BF16),
        ("attnT", [E, T], BF16), ("a0", [T, E], F32), ("f0", [T, E], F32),
        ("x1", [T, E], F32), ("xn1", [T, E], F32), ("xn1T", [E, T], BF16),
        ("zT", [2048, T], BF16), ("yw", [BL, S], F32), ("a1", [T, E], F32),
        ("xf", [T, E], F32),
    ):
        sc[name] = _dram(nc, name, shape, dt, INT)

    p = P2(nc) if USE_SCHED else P(nc)
    with ExitStack() as stk0:
        cb = Bufs(nc, stk0)
        consts = load_constants(p, nc, cb, cvals)

        def stage(fn):
            with ExitStack() as stks:
                bufs = Bufs(nc, stks)
                fn(bufs)
            if USE_SCHED:
                p.barrier()

        stage(lambda bufs: emit_prep(p, nc, bufs, consts,
              {"qw": qw_d, "kw": kw_d, "vw": vw_d, "outw": outw_d,
               "ffnw": ffnw_d, "W_in": Win_d, "W_x": Wx_d, "W_dt": Wdt_d,
               "W_out": Wout_d, "wqkv": wqkv_d, "wo": wo_d, "w1": w1_d,
               "w2": w2_d}, sc))
        stage(lambda bufs: emit_ln0(p, nc, bufs, consts, x_d, sc["xn"],
                                    sc["xnT"]))
        stage(lambda bufs: emit_l0_attn(p, nc, bufs, consts, sc["xnT"],
              sc["qw_bf"], sc["kw_bf"], sc["vw_bf"], qb_d, kb_d, vb_d,
              proj_d, sc["attnT"]))
        stage(lambda bufs: emit_l0_post(p, nc, bufs, consts, sc["attnT"],
              sc["outw_bf"], outb_d, sc["xn"], gamma_d, beta_d, sc["a0"]))
        stage(lambda bufs: emit_l0_ffn(p, nc, bufs, consts, sc["xn"],
              sc["wdqT"], fb_d, sc["f0"]))
        stage(lambda bufs: emit_combine_ln(p, nc, bufs, consts,
              [x_d, sc["a0"], sc["f0"]], sc["x1"], sc["xn1"], sc["xn1T"]))
        stage(lambda bufs: emit_l1_attn(p, nc, bufs, consts, sc["xn1T"],
              sc["WinT"], convw_d, convb_d, sc["WxT"], sc["WdtT"], bdt_d,
              Alog_d, wg_d, sc["zT"], sc["yw"]))
        stage(lambda bufs: emit_l1_attn_out(p, nc, bufs, consts, sc["zT"],
              sc["yw"], sc["WoutT"], sc["a1"]))
        stage(lambda bufs: emit_l1_ffn(p, nc, bufs, consts, sc["xn1"],
              sc["wqkv_bf"], qs_d, ks_d, sc["wo_bf"], lng_d, lnb_d,
              sc["w1_bf"], b1_d, sc["w2_bf"], b2_d, sc["xf"]))
        stage(lambda bufs: emit_sum_final(p, nc, bufs,
              [sc["x1"], sc["a1"], sc["xf"]], out_d))
        p.emit()

    shared = {
        "qw": f32c(inputs["L0_attn_q_w"]), "kw": f32c(inputs["L0_attn_k_w"]),
        "vw": f32c(inputs["L0_attn_v_w"]), "outw": f32c(inputs["L0_attn_out_w"]),
        "qb": f32c(inputs["L0_attn_q_b"]), "kb": f32c(inputs["L0_attn_k_b"]),
        "vb": f32c(inputs["L0_attn_v_b"]), "outb": f32c(inputs["L0_attn_out_b"]),
        "proj": f32c(inputs["L0_attn_proj"]),
        "gamma": f32c(inputs["L0_attn_gamma"]), "beta": f32c(inputs["L0_attn_beta"]),
        "ffnw": f32c(inputs["L0_ffn_weight"]), "fb": f32c(inputs["L0_ffn_bias"]),
        "W_in": f32c(inputs["L1_attn_W_in"]),
        "convw": f32c(inputs["L1_attn_conv_w"].reshape(2048, 4)),
        "convb": f32c(inputs["L1_attn_conv_b"]),
        "W_x": f32c(inputs["L1_attn_W_x"]), "W_dt": f32c(inputs["L1_attn_W_dt"]),
        "bdt": f32c(inputs["L1_attn_b_dt"]), "Alog": f32c(inputs["L1_attn_A_log"]),
        "wg": f32c(inputs["L1_attn_w_gate"]), "W_out": f32c(inputs["L1_attn_W_out"]),
        "wqkv": f32c(inputs["L1_ffn_wqkv"].reshape(2 * E, 1536)),
        "qs": f32c(inputs["L1_ffn_q_scale"]), "ks": f32c(inputs["L1_ffn_k_scale"]),
        "wo": f32c(inputs["L1_ffn_wo"].reshape(E, E)),
        "lng": f32c(inputs["L1_ffn_ln_g"]), "lnb": f32c(inputs["L1_ffn_ln_b"]),
        "w1": f32c(np.transpose(inputs["L1_ffn_w1"], (1, 0, 2)).reshape(E, 2 * 5460)),
        "b1": f32c(inputs["L1_ffn_b1"].reshape(-1)),
        "w2": f32c(inputs["L1_ffn_w2"].reshape(2 * 2730, E)),
        "b2": f32c(inputs["L1_ffn_b2"]),
    }
    for k2 in CONST_SPECS:
        shared["c_" + k2] = cvals[k2]
    in_maps = [dict(shared, x=shards[c]) for c in range(NCORES)]
    r = run_bass_kernel_spmd(nc, in_maps, core_ids=list(range(NCORES)))
    out = np.concatenate([r.results[c]["out"] for c in range(NCORES)], axis=0)
    return out.reshape(32, S, E).astype(np.float32)




def p2_truncate(p, K):
    p.ops = p.ops[:K]
    p.counts = {e: 0 for e in p.ENGINES}
    p.wcount = {}
    for o in p.ops:
        k, s, v = o["ev"]
        if k == "c":
            p.counts[s] = v
        else:
            p.wcount[s] = v


kernel = kernel_fused


# version 05 (final)
# speedup vs baseline: 2.1561x; 1.4914x over previous
import numpy as np
import ml_dtypes
import concourse.bass as bass
import concourse.mybir as mybir
from concourse.bass_utils import run_bass_kernel_spmd

F32 = mybir.dt.float32
BF16 = mybir.dt.bfloat16
FP8 = mybir.dt.float8e4
AX = mybir.AxisListType
ALU = mybir.AluOpType
AF = mybir.ActivationFunctionType

NCORES = 8
BL = 4
S = 2048
T = BL * S
E = 1024
NT = T // 128

BIG_NEG = np.float32(-3.4028235e38)
BF = ml_dtypes.bfloat16


class P:
    ENGINES = ("sync", "scalar", "vector", "tensor", "gpsimd")

    def __init__(self, nc):
        self.nc = nc
        self.ops = []

    def op(self, engine, fn, inc=1):
        self.ops.append((engine, fn, inc))

    def dma(self, out, in_, engine="sync", **kw):
        self.ops.append((engine, lambda e: e.dma_start(out=out, in_=in_, **kw), 16))

    def emit(self):
        nc = self.nc
        starts, c = [], 0
        for (_, _, inc) in self.ops:
            starts.append(c)
            c += inc
        total = c
        by_eng = {}
        for i, o in enumerate(self.ops):
            by_eng.setdefault(o[0], []).append((i, o))
        with nc.semaphore("tick") as tick, nc.Block() as block:
            def mk(ename):
                def body(eng):
                    for i, (_, fn, inc) in by_eng.get(ename, []):
                        if starts[i] > 0:
                            eng.wait_ge(tick, starts[i])
                        fn(eng).then_inc(tick, inc)
                    if ename == "sync":
                        eng.wait_ge(tick, total)
                body.__name__ = "body_" + ename
                return body
            for ename in self.ENGINES:
                if ename in by_eng or ename == "sync":
                    getattr(block, ename)(mk(ename))
        return total


class Bufs:

    def __init__(self, nc, stack):
        self.nc, self.stack, self.cache = nc, stack, {}

    def sb(self, name, shape, dtype=F32):
        if name not in self.cache:
            self.cache[name] = self.stack.enter_context(
                self.nc.sbuf_tensor(name, list(shape), dtype))
        return self.cache[name]

    def ps(self, name, shape, dtype=F32):
        if name not in self.cache:
            esz = 4 if dtype == F32 else 2
            free = max(shape[1], 2048 // esz)
            self.cache[name] = self.stack.enter_context(
                self.nc.psum_tensor(name, [shape[0], free], dtype))
        return self.cache[name][0:shape[0], 0:shape[1]]


def make_constants():
    c = {}
    c["identb"] = np.eye(128).astype(BF)
    c["identf"] = np.eye(128, dtype=np.float32)
    e64 = np.eye(64)
    c["identb64"] = np.vstack([e64, e64]).astype(BF)
    row, col = np.mgrid[0:128, 0:128]
    c["triu_mult"] = (row <= col).astype(np.float32)
    NEGB = -1e38
    c["maskLb"] = np.where(col >= row, 0.0, NEGB).astype(BF)
    c["maskRb"] = np.where(col <= row, 0.0, NEGB).astype(BF)
    DH, WINDOW = 64, 512
    half = np.arange(0, DH, 2, dtype=np.float64)
    inv_freq = 1.0 / 10000.0 ** (half / DH)
    t = np.arange(S, dtype=np.float64)
    freqs = np.concatenate([t[:, None] * inv_freq[None, :]] * 2, -1)
    base = (half + 0.4 * DH) / (1.4 * DH)
    power = (t - S // 2) / float(WINDOW // 2)
    scale = np.concatenate([base[None, :] ** power[:, None]] * 2, -1)
    cos, sin = np.cos(freqs), np.sin(freqs)
    Sq, Sk = (sin * scale).T, (sin / scale).T
    Aq, Ak = (cos * scale).T, (cos / scale).T
    Bq = np.concatenate([-Sq[0:32], Sq[32:64]], 0)
    Bk = np.concatenate([-Sk[0:32], Sk[32:64]], 0)
    c["rot_Aq"] = np.vstack([Aq, Aq]).astype(np.float32)
    c["rot_Ak"] = np.vstack([Ak, Ak]).astype(np.float32)
    c["rot_Bq"] = np.vstack([Bq, Bq]).astype(np.float32)
    c["rot_Bk"] = np.vstack([Bk, Bk]).astype(np.float32)
    cols = np.zeros((128, 8), dtype=np.float32)
    cols[:, 0] = 1.0
    cols[0:64, 1] = 1.0
    cols[64:128, 2] = 1.0
    cols[0:64, 3] = 1.0 / 128.0
    cols[64:128, 4] = 1.0 / 128.0
    c["cols"] = cols
    c["one_row"] = np.ones((1, 128), dtype=np.float32)
    c["negone_row"] = np.full((1, 128), -1.0, dtype=np.float32)
    return c


CONST_SPECS = {
    "identb": BF16, "identf": F32, "identb64": BF16,
    "triu_mult": F32, "maskLb": BF16, "maskRb": BF16,
    "rot_Aq": F32, "rot_Bq": F32, "rot_Ak": F32, "rot_Bk": F32,
    "cols": F32, "one_row": F32, "negone_row": F32,
}


def load_constants(p, nc, bufs, cvals, names=None):
    tiles = {}
    for name, dt in CONST_SPECS.items():
        if names is not None and name not in names:
            continue
        arr = cvals[name]
        d = nc.dram_tensor("c_" + name, list(arr.shape), dt, kind="ExternalInput").ap()
        tsb = bufs.sb("sb_" + name, arr.shape, dt)
        p.dma(tsb[:, :], d[:, :])
        tiles[name] = tsb
    return tiles


def load_col(p, nc, bufs, dram_vec, lo, n, name):
    t = bufs.sb(name, (128, 1), F32)
    src = dram_vec[lo:lo + n].rearrange("(p one) -> p one", one=1)
    p.dma(t[0:n, :], src, allow_slow_non_contiguous=True)
    return t


def emit_layernorm_tile(p, nc, bufs, xt, xn, eps, n=E, pref="", sq=None):
    st = bufs.sb(pref + "ln_stat", (128, 8), F32)
    if sq is None:
        sq = bufs.sb(pref + "ln_sq", (128, n), F32)
    p.op("vector", lambda e: e.reduce_sum(st[:, 0:1], xt[:, 0:n], AX.X))
    p.op("vector", lambda e: e.tensor_scalar_mul(st[:, 1:2], st[:, 0:1], -1.0 / n))
    p.op("vector", lambda e: e.tensor_scalar_add(xn[:, 0:n], xt[:, 0:n], st[:, 1:2]))
    p.op("scalar", lambda e: e.activation(sq[:, 0:n], xn[:, 0:n], AF.Square,
                                          accum_out=st[:, 2:3]))
    p.op("scalar", lambda e: e.activation(st[:, 3:4], st[:, 2:3], AF.Sqrt,
                                          bias=np.float32(eps), scale=1.0 / n))
    p.op("vector", lambda e: e.reciprocal(st[:, 4:5], st[:, 3:4]))
    p.op("vector", lambda e: e.tensor_scalar_mul(xn[:, 0:n], xn[:, 0:n], st[:, 4:5]))


def emit_transpose_store(p, nc, bufs, srcb, dst_dram, row0, col0, nf, idb, pref=""):
    pst_r = [bufs.ps(pref + "tr_ps%d" % j, (128, 128), BF16) for j in range(2)]
    tb_r = [bufs.sb(pref + "tr_out%d" % j, (128, 128), BF16) for j in range(2)]
    for f in range(nf):
        pst, tb = pst_r[f % 2], tb_r[f % 2]
        p.op("tensor", lambda e, f=f, pst=pst: e.transpose(pst[:, :], srcb[:, f * 128:(f + 1) * 128], idb[:, :]))
        p.op("vector", lambda e, tb=tb, pst=pst: e.tensor_copy(tb[:, :], pst[:, :]))
        p.dma(dst_dram[row0 + f * 128:row0 + (f + 1) * 128, col0:col0 + 128], tb[:, :])


def emit_ln0(p, nc, bufs, consts, x_dram, xn_dram, xnT_dram, eps=1e-5):
    xt_r = [bufs.sb(f"ln0_x{j}", (128, E), F32) for j in range(2)]
    xb_r = [bufs.sb(f"ln0_xb{j}", (128, E), BF16) for j in range(2)]
    for i in range(NT):
        j = i % 2
        xt, xb = xt_r[j], xb_r[j]
        p.dma(xt[:, :], x_dram[i * 128:(i + 1) * 128, :])
        emit_layernorm_tile(p, nc, bufs, xt, xt, eps, pref="ln0_%d" % j)
        if xn_dram is not None:
            p.dma(xn_dram[i * 128:(i + 1) * 128, :], xt[:, :])
        if xnT_dram is not None:
            p.op("vector", lambda e, xb=xb, xt=xt: e.tensor_copy(xb[:, :], xt[:, :]))
            emit_transpose_store(p, nc, bufs, xb, xnT_dram, 0, i * 128, 8,
                                 consts["identb"], pref="ln0_%d" % j)


def np_layernorm(x, eps=1e-5):
    m = x.mean(-1, keepdims=True)
    v = x.var(-1, keepdims=True)
    return (x - m) / np.sqrt(v + eps)


M1 = 1572864.0


def emit_quant_tile(p, nc, bufs, xt, xd, n=E, pref="q_"):
    nb = n // 32
    am = bufs.sb(pref + "am", (128, E // 32), F32)
    y = bufs.sb(pref + "y", (128, E), F32)
    g = bufs.sb(pref + "g", (128, E), F32)
    f8 = bufs.sb(pref + "f8", (128, E), FP8)
    r1 = bufs.sb(pref + "r1", (128, E), F32)
    xr = xt[:, 0:n].rearrange("p (k c) -> p k c", c=32)
    p.op("vector", lambda e: e.tensor_reduce(am[:, 0:nb], xr, AX.X, ALU.max,
                                             apply_absolute_value=True))
    p.op("vector", lambda e: e.tensor_scalar(am[:, 0:nb], am[:, 0:nb],
                                             1.0 / 7.5, 1e-30, ALU.mult, ALU.max))
    rs = bufs.sb(pref + "rs", (128, E // 32), F32)
    p.op("vector", lambda e: e.reciprocal(rs[:, 0:nb], am[:, 0:nb]))
    bc = lambda t: t[:, 0:nb].unsqueeze(2).broadcast_to([128, nb, 32])
    yr = y[:, 0:n].rearrange("p (k c) -> p k c", c=32)
    p.op("vector", lambda e: e.tensor_tensor(yr, xr, bc(rs), ALU.mult))
    p.op("vector", lambda e: e.tensor_copy(f8[:, 0:n], y[:, 0:n]))
    p.op("vector", lambda e: e.tensor_copy(g[:, 0:n], f8[:, 0:n]))
    p.op("vector", lambda e: e.tensor_scalar_add(r1[:, 0:n], y[:, 0:n], M1))
    p.op("vector", lambda e: e.tensor_scalar_add(r1[:, 0:n], r1[:, 0:n], -M1))
    p.op("vector", lambda e: e.tensor_scalar(r1[:, 0:n], r1[:, 0:n], 1.0, -1.0,
                                             ALU.min, ALU.max))
    p.op("vector", lambda e: e.tensor_scalar(y[:, 0:n], g[:, 0:n], 1.0, -1.0,
                                             ALU.min, ALU.max))
    p.op("vector", lambda e: e.tensor_sub(g[:, 0:n], g[:, 0:n], y[:, 0:n]))
    p.op("vector", lambda e: e.tensor_add(g[:, 0:n], g[:, 0:n], r1[:, 0:n]))
    xdr = xd[:, 0:n].rearrange("p (k c) -> p k c", c=32)
    gr = g[:, 0:n].rearrange("p (k c) -> p k c", c=32)
    p.op("vector", lambda e: e.tensor_tensor(xdr, gr, bc(am), ALU.mult))


def np_quant(x):
    vals = set()
    for bits in range(64):
        s = bits >> 5 & 1
        ee = bits >> 3 & 3
        m = bits & 7
        v = m / 8.0 if ee == 0 else (1.0 + m / 8.0) * 2.0 ** (ee - 1)
        vals.add(-v if s else v)
    grid = np.array(sorted(vals), dtype=np.float32)
    rows, K = x.shape
    b = x.reshape(rows, K // 32, 32)
    scales = np.maximum(np.max(np.abs(b), -1) / 7.5, 1e-30).astype(np.float32)
    y = (b / scales[..., None]).astype(np.float32)
    idx = np.clip(np.searchsorted(grid, y), 0, grid.shape[0] - 1)
    lo = np.maximum(idx - 1, 0)
    vhi, vlo = grid[idx], grid[lo]
    q = np.where(np.abs(y - vlo) <= np.abs(y - vhi), vlo, vhi)
    return (q * scales[..., None]).reshape(rows, K).astype(np.float32)


def emit_cast_bf16(p, nc, bufs, src_dram, dst_dram, R, C, pref="cb_"):
    a = bufs.sb(pref + "a", (128, 2048), F32)
    b = bufs.sb(pref + "b", (128, 2048), BF16)
    for r0 in range(0, R, 128):
        nr = min(128, R - r0)
        for c0 in range(0, C, 2048):
            ncol = min(2048, C - c0)
            p.dma(a[0:nr, 0:ncol], src_dram[r0:r0 + nr, c0:c0 + ncol])
            p.op("vector", lambda e, nr=nr, ncol=ncol: e.tensor_copy(
                b[0:nr, 0:ncol], a[0:nr, 0:ncol]))
            p.dma(dst_dram[r0:r0 + nr, c0:c0 + ncol], b[0:nr, 0:ncol])


def emit_transpose_bf16(p, nc, bufs, src_dram, dst_dram, R, C, idb,
                        quant=False, scale=None, pref="tb_"):
    a = bufs.sb(pref + "a", (128, max(C, 32)), F32)
    b = bufs.sb(pref + "b", (128, max(C, 32)), BF16)
    pst = bufs.ps(pref + "ps", (128, 128), BF16)
    o = bufs.sb(pref + "o", (128, 128), BF16)
    assert C <= 8192
    for r0 in range(0, R, 128):
        nr = min(128, R - r0)
        p.dma(a[0:nr, 0:C], src_dram[r0:r0 + nr, 0:C])
        if quant:
            assert nr == 128
            emit_quant_tile(p, nc, bufs, a, a, n=C, pref=pref + "q_")
        if scale is not None:
            p.op("vector", lambda e, nr=nr: e.tensor_scalar_mul(
                a[0:nr, 0:C], a[0:nr, 0:C], float(scale)))
        p.op("vector", lambda e, nr=nr: e.tensor_copy(b[0:nr, 0:C], a[0:nr, 0:C]))
        for c0 in range(0, C, 128):
            ncol = min(128, C - c0)
            p.op("tensor", lambda e, nr=nr, c0=c0, ncol=ncol: e.transpose(
                pst[0:ncol, 0:nr], b[0:nr, c0:c0 + ncol], idb[0:nr, 0:nr]))
            p.op("vector", lambda e, nr=nr, ncol=ncol: e.tensor_copy(
                o[0:ncol, 0:nr], pst[0:ncol, 0:nr]))
            p.dma(dst_dram[c0:c0 + ncol, r0:r0 + nr], o[0:ncol, 0:nr])


def load_bias_cols(p, nc, bufs, dram_vec, n, name):
    nf = n // 128
    t = bufs.sb(name, (128, nf), F32)
    src = dram_vec[0:n].rearrange("(f p) -> p f", p=128)
    p.dma(t[:, 0:nf], src, allow_slow_non_contiguous=True)
    return t


def emit_l0_attn(p, nc, bufs, consts, xnT, qw, kw, vw, qb_d, kb_d, vb_d,
                 proj_d, attnT):
    idb = consts["identb"]
    H, D, Mf = 16, 64, 256
    NCH = S // 128

    qb = load_bias_cols(p, nc, bufs, qb_d, E, "l0_qb")
    kb = load_bias_cols(p, nc, bufs, kb_d, E, "l0_kb")
    vb = load_bias_cols(p, nc, bufs, vb_d, E, "l0_vb")

    projT = bufs.sb("l0_projT", (128, 256), BF16)
    pj = bufs.sb("l0_pj", (128, 64), F32)
    pps = bufs.ps("l0_psmisc", (128, 512), F32)[0:64, 0:128]
    for half in range(2):
        p.dma(pj[:, :], proj_d[half * 128:(half + 1) * 128, :])
        p.op("tensor", lambda e: e.transpose(pps, pj[:, :],
                                             consts["identf"][:, :]))
        p.op("scalar", lambda e, half=half: e.activation(
            projT[0:64, half * 128:(half + 1) * 128], pps, AF.Copy,
            scale=0.125))
        p.op("scalar", lambda e, half=half: e.activation(
            projT[64:128, half * 128:(half + 1) * 128], pps, AF.Copy,
            scale=0.125))

    xnb = [bufs.sb(f"l0_xn{k}", (128, S), BF16) for k in range(8)]
    wsb_r = [bufs.sb(f"l0_w{j}", (128, 3 * E), BF16) for j in range(2)]
    qkvT = [bufs.sb(f"l0_qkvT{j}", (128, S), BF16) for j in range(6)]
    corrK = [bufs.sb(f"l0_corrK{h}", (1, S), F32) for h in range(4)]
    sqt = bufs.sb("l0_sq", (128, 512), F32)
    qp = [bufs.sb(f"l0_qp{j}", (128, S), BF16) for j in range(2)]
    kp = [bufs.sb(f"l0_kp{j}", (128, S), BF16) for j in range(2)]
    vtok = bufs.sb("l0_vtok", (128, NCH * 64), BF16)
    kptok = bufs.sb("l0_kptok", (128, 256), BF16)
    smT = bufs.sb("l0_smT", (128, 128), BF16)
    kvsb = bufs.sb("l0_kvsb", (128, 2 * 65), BF16)
    numsb = bufs.sb("l0_numsb", (128, 65), F32)
    att = bufs.sb("l0_att", (128, 64), BF16)
    attT = bufs.sb("l0_attT", (64, 128), BF16)
    onesb = bufs.sb("l0_onesb", (128, 1), BF16)
    triub = bufs.sb("l0_triub", (128, 128), F32)
    p.op("vector", lambda e: e.memset(onesb[:, :], 1.0))
    p.op("vector", lambda e: e.tensor_copy(triub[:, :], consts["triu_mult"][:, :]))

    ps_mm = [bufs.ps(f"l0_psmm{j}", (128, 512), F32) for j in range(2)]
    ps_map = [bufs.ps("l0_psmap", (128, 512), F32), ps_mm[1]]
    ps_misc = bufs.ps("l0_psmisc", (128, 512), F32)
    ps_corr = ps_misc[0:1, 0:512]
    ps_sqk = ps_misc[:, 0:128]
    ps_nd = bufs.ps("l0_psnd", (128, 65), F32)
    ps_kv = [bufs.ps(f"l0_pskv{j}", (128, 65), F32) for j in range(2)]
    ps_trbf = bufs.ps("l0_pstrbf", (128, 128), BF16)
    ps_trb = ps_trbf[:, 0:128]
    ps_tr64 = ps_trbf[:, 0:64]

    for b in range(BL):
        for k in range(8):
            p.dma(xnb[k][:, :], xnT[k * 128:(k + 1) * 128, b * S:(b + 1) * S])
        for hg in range(4):
            for ft in range(2):
                frow = hg * 256 + ft * 128
                wsb = wsb_r[ft % 2]
                for wi, (wd, bias) in enumerate(((qw, qb), (kw, kb), (vw, vb))):
                    for kt in range(8):
                        p.dma(wsb[:, wi * E + kt * 128:wi * E + (kt + 1) * 128],
                              wd[kt * 128:(kt + 1) * 128, frow:frow + 128])
                for wi, (wd, bias) in enumerate(((qw, qb), (kw, kb), (vw, vb))):
                    bcol = bias[:, frow // 128:frow // 128 + 1]
                    for c in range(4):
                        psm_ = ps_mm[(wi * 4 + c) % 2]
                        for kt in range(8):
                            p.op("tensor", lambda e, wi=wi, kt=kt, c=c, wsb=wsb, psm_=psm_: e.matmul(
                                psm_[:, :],
                                wsb[:, wi * E + kt * 128:wi * E + (kt + 1) * 128],
                                xnb[kt][:, c * 512:(c + 1) * 512],
                                start=(kt == 0), stop=(kt == 7)))
                        dst = qkvT[wi * 2 + ft]
                        p.op("vector", lambda e, dst=dst, c=c, bcol=bcol, psm_=psm_: e.tensor_scalar_add(
                            dst[:, c * 512:(c + 1) * 512], psm_[:, :], bcol))
                        if wi == 1:
                            p.op("scalar", lambda e, bcol=bcol, psm_=psm_: e.activation(
                                sqt[:, :], psm_[:, :], AF.Square, bias=bcol))
                            for hh in range(2):
                                hloc = ft * 2 + hh
                                p.op("tensor", lambda e, hh=hh: e.matmul(
                                    ps_corr[:, :], consts["cols"][:, 3 + hh:4 + hh],
                                    sqt[:, :], start=True, stop=True))
                                p.op("vector", lambda e, hloc=hloc, c=c: e.tensor_copy(
                                    corrK[hloc][:, c * 512:(c + 1) * 512],
                                    ps_corr[:, :]))
            for h in range(4):
                ft, base = h // 2, (h % 2) * 64
                hglob = hg * 4 + h
                qh = qkvT[ft]
                kh = qkvT[2 + ft]
                vh = qkvT[4 + ft]
                for dst, srcT, corr in ((qp, qh, None), (kp, kh, corrK[h])):
                    for mt in range(2):
                        for c in range(4):
                            psm_ = ps_map[(mt * 4 + c) % 2]
                            p.op("tensor", lambda e, mt=mt, c=c, srcT=srcT, base=base, psm_=psm_: e.matmul(
                                psm_[:, :],
                                projT[base:base + 64, mt * 128:(mt + 1) * 128],
                                srcT[base:base + 64, c * 512:(c + 1) * 512],
                                start=True, stop=(corr is None)))
                            if corr is not None:
                                p.op("tensor", lambda e, c=c, corr=corr, psm_=psm_: e.matmul(
                                    psm_[:, :], consts["negone_row"][:, :],
                                    corr[:, c * 512:(c + 1) * 512],
                                    start=False, stop=True))
                            p.op("scalar", lambda e, dst=dst, mt=mt, c=c, psm_=psm_: e.activation(
                                dst[mt][:, c * 512:(c + 1) * 512], psm_[:, :],
                                AF.Exp))
                for c in range(NCH):
                    p.op("tensor", lambda e, c=c, vh=vh, base=base: e.transpose(
                        ps_tr64[:, :], vh[base:base + 64, c * 128:(c + 1) * 128],
                        consts["identb64"][base:base + 64, 0:64]))
                    p.op("vector", lambda e, c=c: e.tensor_copy(
                        vtok[:, c * 64:(c + 1) * 64], ps_tr64[:, :]))
                for c in range(NCH):
                    cs = slice(c * 128, (c + 1) * 128)
                    for mt in range(2):
                        p.op("tensor", lambda e, mt=mt, cs=cs: e.matmul(
                            ps_sqk[:, :], kp[mt][:, cs], qp[mt][:, cs],
                            start=(mt == 0), stop=(mt == 1)))
                    p.op("vector", lambda e: e.tensor_tensor(
                        smT[:, :], ps_sqk[:, :], triub[:, :], ALU.mult))
                    p.op("tensor", lambda e, c=c: e.matmul(
                        ps_nd[:, 0:64], smT[:, :], vtok[:, c * 64:(c + 1) * 64],
                        start=True, stop=False, skip_group_check=True))
                    p.op("tensor", lambda e: e.matmul(
                        ps_nd[:, 64:65], smT[:, :], onesb[:, :],
                        start=False, stop=False, skip_group_check=True))
                    for mt in range(2):
                        if c > 0:
                            p.op("tensor", lambda e, mt=mt, cs=cs: e.matmul(
                                ps_nd[:, 0:65], qp[mt][:, cs],
                                kvsb[:, mt * 65:(mt + 1) * 65],
                                start=False, stop=(mt == 1),
                                skip_group_check=True))
                    for mt in range(2):
                        p.op("tensor", lambda e, mt=mt, cs=cs: e.transpose(
                            ps_trb[:, :], kp[mt][:, cs], idb[:, :]))
                        p.op("vector", lambda e, mt=mt: e.tensor_copy(
                            kptok[:, mt * 128:(mt + 1) * 128], ps_trb[:, :]))
                    for mt in range(2):
                        p.op("tensor", lambda e, mt=mt, c=c: e.matmul(
                            ps_kv[mt][:, 0:64], kptok[:, mt * 128:(mt + 1) * 128],
                            vtok[:, c * 64:(c + 1) * 64],
                            start=(c == 0), stop=False, skip_group_check=True))
                        p.op("tensor", lambda e, mt=mt, c=c: e.matmul(
                            ps_kv[mt][:, 64:65], kptok[:, mt * 128:(mt + 1) * 128],
                            onesb[:, :],
                            start=False, stop=(c == NCH - 1),
                            skip_group_check=True))
                        p.op("vector", lambda e, mt=mt: e.tensor_copy(
                            kvsb[:, mt * 65:(mt + 1) * 65], ps_kv[mt][:, :]))
                    p.op("vector", lambda e: e.reciprocal(numsb[:, 64:65],
                                                          ps_nd[:, 64:65]))
                    p.op("vector", lambda e: e.tensor_scalar(
                        att[:, :], ps_nd[:, 0:64], numsb[:, 64:65], None,
                        ALU.mult))
                    p.op("tensor", lambda e: e.transpose(ps_trb[0:64, 0:128],
                                                         att[:, :], idb[:, :]))
                    p.op("vector", lambda e: e.tensor_copy(attT[:, :],
                                                           ps_trb[0:64, 0:128]))
                    p.dma(attnT[hglob * 64:(hglob + 1) * 64,
                                b * S + c * 128:b * S + (c + 1) * 128],
                          attT[:, :])




def emit_l0_post(p, nc, bufs, consts, attnT, outw, outb_d, xn_dram,
                 gamma_d, beta_d, a0_dram):
    wt = [bufs.sb(f"lp_w{k}", (128, E), BF16) for k in range(8)]
    for kt in range(8):
        p.dma(wt[kt][:, :], outw[kt * 128:(kt + 1) * 128, :])
    brow = bufs.sb("lp_brow", (1, E), F32)
    p.dma(brow[:, :], outb_d[0:E].rearrange("(one n) -> one n", one=1))
    grow = bufs.sb("lp_grow", (1, E), F32)
    berow = bufs.sb("lp_berow", (1, E), F32)
    p.dma(grow[:, :], gamma_d[0:E].rearrange("(one n) -> one n", one=1))
    p.dma(berow[:, :], beta_d[0:E].rearrange("(one n) -> one n", one=1))
    ps_all = [bufs.ps(f"lp_ps{k}", (128, 512), F32) for k in range(4)]
    gb = bufs.sb("lp_gb", (128, E), F32)
    bb = bufs.sb("lp_bb", (128, E), F32)
    ps = ps_all[0:2]
    for n in range(2):
        p.op("tensor", lambda e, n=n: e.matmul(
            ps[0][:, :], consts["one_row"][:, :], grow[:, n * 512:(n + 1) * 512],
            start=True, stop=True))
        p.op("vector", lambda e, n=n: e.tensor_copy(gb[:, n * 512:(n + 1) * 512],
                                                    ps[0][:, :]))
        p.op("tensor", lambda e, n=n: e.matmul(
            ps[0][:, :], consts["one_row"][:, :], berow[:, n * 512:(n + 1) * 512],
            start=True, stop=True))
        p.op("vector", lambda e, n=n: e.tensor_copy(bb[:, n * 512:(n + 1) * 512],
                                                    ps[0][:, :]))
    at_r = [bufs.sb(f"lp_at{j}", (128, E), BF16) for j in range(2)]
    res_r = [bufs.sb(f"lp_res{j}", (128, E), F32) for j in range(2)]
    for i in range(NT):
        at = at_r[i % 2]
        res = res_r[i % 2]
        ps = ps_all[(i % 2) * 2:(i % 2) * 2 + 2]
        for kt in range(8):
            p.dma(at[:, kt * 128:(kt + 1) * 128],
                  attnT[kt * 128:(kt + 1) * 128, i * 128:(i + 1) * 128])
        for n in range(2):
            for kt in range(8):
                p.op("tensor", lambda e, n=n, kt=kt, at=at: e.matmul(
                    ps[n][:, :], at[:, kt * 128:(kt + 1) * 128],
                    wt[kt][:, n * 512:(n + 1) * 512],
                    start=(kt == 0), stop=False, skip_group_check=True))
            p.op("tensor", lambda e, n=n: e.matmul(
                ps[n][:, :], consts["one_row"][:, :],
                brow[:, n * 512:(n + 1) * 512],
                start=False, stop=True, skip_group_check=True))
        p.dma(res[:, :], xn_dram[i * 128:(i + 1) * 128, :])
        for n in range(2):
            p.op("vector", lambda e, n=n, res=res: e.tensor_tensor(
                res[:, n * 512:(n + 1) * 512], ps[n][:, :],
                res[:, n * 512:(n + 1) * 512], ALU.add))
        emit_layernorm_tile(p, nc, bufs, res, res, 1e-6, pref="lp_" + str(i % 2))
        p.op("vector", lambda e, res=res: e.tensor_tensor(res[:, :], res[:, :], gb[:, :],
                                                 ALU.mult))
        p.op("vector", lambda e, res=res: e.tensor_tensor(res[:, :], res[:, :], bb[:, :],
                                                 ALU.add))
        p.dma(a0_dram[i * 128:(i + 1) * 128, :], res[:, :])


def np_l0_post(attn, xn, outw, outb, gamma, beta):
    main = attn.astype(np.float64) @ outw.astype(np.float64) + outb
    res = main + xn
    mu = res.mean(-1, keepdims=True)
    var = ((res - mu) ** 2).mean(-1, keepdims=True)
    return (gamma * (res - mu) / np.sqrt(var + 1e-6) + beta).astype(np.float32)


def emit_combine_ln(p, nc, bufs, consts, srcs, x_out, xn_out, xnT_out,
                    eps=1e-5):
    a_r = [bufs.sb(f"cb_x{j}", (128, E), F32) for j in range(2)]
    t_r = [bufs.sb(f"cb_t{j}{k}", (128, E), F32) for j in range(2) for k in range(2)]
    xb_r = [bufs.sb(f"cb_xb{j}", (128, E), BF16) for j in range(2)]
    for i in range(NT):
        j = i % 2
        a, xb = a_r[j], xb_r[j]
        p.dma(a[:, :], srcs[0][i * 128:(i + 1) * 128, :])
        for si, s in enumerate(srcs[1:]):
            t = t_r[j * 2 + si]
            p.dma(t[:, :], s[i * 128:(i + 1) * 128, :])
            p.op("vector", lambda e, a=a, t=t: e.tensor_add(a[:, :], a[:, :], t[:, :]))
        if x_out is not None:
            p.dma(x_out[i * 128:(i + 1) * 128, :], a[:, :])
        emit_layernorm_tile(p, nc, bufs, a, a, eps, pref="cb_%d" % j)
        if xn_out is not None:
            p.dma(xn_out[i * 128:(i + 1) * 128, :], a[:, :])
        if xnT_out is not None:
            p.op("vector", lambda e, xb=xb, a=a: e.tensor_copy(xb[:, :], a[:, :]))
            emit_transpose_store(p, nc, bufs, xb, xnT_out, 0, i * 128, 8,
                                 consts["identb"], pref="cb_%d" % j)


def emit_sum_final(p, nc, bufs, srcs, out_dram):
    a_r = [bufs.sb(f"fi_x{j}", (128, E), F32) for j in range(2)]
    t_r = [bufs.sb(f"fi_t{j}{k}", (128, E), F32) for j in range(2) for k in range(2)]
    for i in range(NT):
        j = i % 2
        a = a_r[j]
        p.dma(a[:, :], srcs[0][i * 128:(i + 1) * 128, :])
        for si, s in enumerate(srcs[1:]):
            t = t_r[j * 2 + si]
            p.dma(t[:, :], s[i * 128:(i + 1) * 128, :])
            p.op("vector", lambda e, a=a, t=t: e.tensor_add(a[:, :], a[:, :], t[:, :]))
        p.dma(out_dram[i * 128:(i + 1) * 128, :], a[:, :])


def emit_l1_attn(p, nc, bufs, consts, xnT, WinT, convw_d, convb_d, WxT,
                 WdtT, bdt_d, Alog_d, wgate_d, zT_dram, yw_dram, dbg=None):
    DI = 2048
    cb = load_bias_cols(p, nc, bufs, convb_d, DI, "m_cb")
    bdt = load_col(p, nc, bufs, bdt_d, 0, 128, "m_bdt")
    alog = load_col(p, nc, bufs, Alog_d, 0, 128, "m_alog")
    wg = bufs.sb("m_wg", (1, 1), F32)
    p.dma(wg[:, :], wgate_d[0:1].rearrange("(a b) -> a b", a=1))
    acol = bufs.sb("m_acol", (128, 1), F32)
    p.op("scalar", lambda e: e.activation(acol[:, :], alog[:, :], AF.Exp))
    p.op("vector", lambda e: e.tensor_scalar_mul(acol[:, :], acol[:, :], -1.0))

    xnb = [bufs.sb(f"m_xn{k}", (128, S), BF16) for k in range(8)]
    wsb_r = [bufs.sb(f"m_w{j}", (128, E), BF16) for j in range(2)]
    cw = bufs.sb("m_cw", (128, 4), F32)
    cbuf = bufs.sb("m_cbuf", (128, 4 + S), F32)
    acc = [bufs.sb(f"m_acc{k}", (128, S), F32) for k in range(2)]
    xcT_dram = nc.dram_tensor("m_xcT", [2048, S], BF16, kind="Internal").ap()
    xcs = bufs.sb("m_xcs", (128, S), BF16)
    xck = [bufs.sb(f"m_xck{j}", (128, 512), BF16) for j in range(3)]
    zbf = [bufs.sb(f"m_zbf{j}", (128, 512), BF16) for j in range(2)]
    wx = bufs.sb("m_wx", (128, 2048), BF16)
    dtT = bufs.sb("m_dtT", (64, S), BF16)
    BmT = bufs.sb("m_BmT", (128, S), F32)
    CmT = bufs.sb("m_CmT", (128, S), F32)
    delta = bufs.sb("m_delta", (128, S), F32)
    u1 = bufs.sb("m_u1", (128, S), F32)
    u2 = bufs.sb("m_u2", (128, S), F32)
    hT = bufs.sb("m_hT", (128, S), F32)
    ywrow = bufs.sb("m_ywrow", (1, S), F32)
    wdt = bufs.sb("m_wdt", (64, 128), BF16)
    p.dma(wdt[:, :], WdtT[:, :])
    ps = bufs.ps("m_ps", (128, 512), F32)
    ps2 = bufs.ps("m_ps2", (128, 512), F32)
    psy = bufs.ps("m_psy", (1, 512), F32)

    for b in range(BL):
        for k in range(8):
            p.dma(xnb[k][:, :], xnT[k * 128:(k + 1) * 128, b * S:(b + 1) * S])
        for dt in range(16):
            wsb = wsb_r[dt % 2]
            for kt in range(8):
                p.dma(wsb[:, kt * 128:(kt + 1) * 128],
                      WinT[kt * 128:(kt + 1) * 128, dt * 128:(dt + 1) * 128])
            p.dma(cw[:, :], convw_d[dt * 128:(dt + 1) * 128, :])
            p.op("vector", lambda e: e.memset(cbuf[:, 0:4], 0.0))
            for c in range(4):
                ps_ = [ps, ps2][c % 2]
                for kt in range(8):
                    p.op("tensor", lambda e, kt=kt, c=c, wsb=wsb, ps_=ps_: e.matmul(
                        ps_[:, :], wsb[:, kt * 128:(kt + 1) * 128],
                        xnb[kt][:, c * 512:(c + 1) * 512],
                        start=(kt == 0), stop=(kt == 7)))
                zb_ = zbf[c % 2]
                p.op("scalar", lambda e, ps_=ps_, zb_=zb_: e.activation(zb_[:, :], ps_[:, :], AF.Silu))
                p.dma(zT_dram[dt * 128:(dt + 1) * 128,
                              b * S + c * 512:b * S + (c + 1) * 512], zb_[:, :])
                p.op("vector", lambda e, c=c, ps_=ps_: e.tensor_copy(
                    cbuf[:, 3 + c * 512:3 + (c + 1) * 512], ps_[:, :]))
            p.op("vector", lambda e, dt=dt: e.tensor_scalar(
                acc[0][:, :], cbuf[:, 0:S], cw[:, 0:1],
                cb[:, dt:dt + 1], ALU.mult, ALU.add))
            for j in (1, 2, 3):
                src, dst = acc[(j + 1) % 2], acc[j % 2]
                p.op("vector", lambda e, j=j, src=src, dst=dst: e.scalar_tensor_tensor(
                    dst[:, :], cbuf[:, j:j + S], cw[:, j:j + 1], src[:, :],
                    ALU.mult, ALU.add))
            p.op("vector", lambda e: e.tensor_copy(xcs[:, :], acc[1][:, :]))
            p.dma(xcT_dram[dt * 128:(dt + 1) * 128, :], xcs[:, :])
        for (mlo, msz, dst, dtyp) in ((0, 64, dtT, BF16), (64, 128, BmT, F32),
                                      (192, 128, CmT, F32)):
            for kt in range(16):
                p.dma(wx[0:128, kt * 128:kt * 128 + msz],
                      WxT[kt * 128:(kt + 1) * 128, mlo:mlo + msz])
            for c in range(4):
                for kt in range(16):
                    xk_ = xck[kt % 3]
                    p.dma(xk_[:, :], xcT_dram[kt * 128:(kt + 1) * 128,
                                              c * 512:(c + 1) * 512])
                    p.op("tensor", lambda e, kt=kt, msz=msz, xk_=xk_: e.matmul(
                        ps[0:msz, :], wx[:, kt * 128:kt * 128 + msz],
                        xk_[:, :], start=(kt == 0), stop=(kt == 15)))
                p.op("vector", lambda e, dst=dst, c=c, msz=msz: e.tensor_copy(
                    dst[0:msz, c * 512:(c + 1) * 512], ps[0:msz, :]))
        for c in range(4):
            p.op("tensor", lambda e, c=c: e.matmul(
                ps2[:, :], wdt[:, :], dtT[:, c * 512:(c + 1) * 512],
                start=True, stop=True))
            p.op("vector", lambda e, c=c: e.tensor_scalar(
                u1[:, c * 512:(c + 1) * 512], ps2[:, :], bdt[:, 0:1], 0.0,
                ALU.add, ALU.max))
            p.op("scalar", lambda e, c=c: e.activation(
                u2[:, c * 512:(c + 1) * 512], ps2[:, :], AF.Abs, bias=bdt[:, 0:1]))
        p.op("scalar", lambda e: e.activation(u2[:, :], u2[:, :], AF.Exp, scale=-1.0))
        p.op("scalar", lambda e: e.activation(u2[:, :], u2[:, :], AF.Ln, bias=1.0))
        p.op("vector", lambda e: e.tensor_add(delta[:, :], u1[:, :], u2[:, :]))
        p.op("scalar", lambda e: e.activation(u1[:, :], delta[:, :], AF.Exp,
                                              scale=acol[:, 0:1]))
        p.op("vector", lambda e: e.tensor_mul(u2[:, :], delta[:, :], BmT[:, :]))
        p.op("vector", lambda e: e.tensor_tensor_scan(hT[:, :], u1[:, :],
                                                      u2[:, :], 0.0,
                                                      ALU.mult, ALU.add))
        p.op("vector", lambda e: e.tensor_mul(hT[:, :], hT[:, :], CmT[:, :]))
        for c in range(4):
            p.op("tensor", lambda e, c=c: e.matmul(
                psy[:, :], consts["cols"][:, 0:1], hT[:, c * 512:(c + 1) * 512],
                start=True, stop=True))
            p.op("vector", lambda e, c=c: e.tensor_scalar(
                ywrow[:, c * 512:(c + 1) * 512], psy[:, :], wg[0:1, 0:1], None,
                ALU.mult))
        p.dma(yw_dram[b:b + 1, :], ywrow[:, :])
        if dbg is not None and b == 0:
            for nm, tile in (("xc0", xcs), ("dtT", dtT), ("BmT", BmT),
                             ("CmT", CmT), ("delta", delta), ("Abar", u1),
                             ("hT", hT)):
                if nm in dbg:
                    p.dma(dbg[nm][:, :], tile[:, :] if nm != "dtT"
                          else tile[0:64, :])


def np_l1_attn(xn, W_in, conv_w, conv_b, W_x, W_dt, b_dt, A_log, w_gate):
    DI = 2048
    z = np.zeros((T, DI), dtype=np.float32)
    yw = np.zeros((BL, S), dtype=np.float32)
    for b in range(BL):
        x = xn[b * S:(b + 1) * S].astype(np.float64)
        xi = x @ W_in.T.astype(np.float64)
        xc = np.zeros_like(xi)
        for j in range(4):
            lo = 3 - j
            xc[lo:, :] += xi[:S - lo, :] * conv_w[:, j]
        xc += conv_b
        ssm = xc @ W_x.T.astype(np.float64)
        dt, Bm, Cm = ssm[:, 0:64], ssm[:, 64:192], ssm[:, 192:320]
        da = dt @ W_dt.T.astype(np.float64) + b_dt
        delta = np.where(da > 30, da, np.log1p(np.exp(np.minimum(da, 30.0))))
        A = -np.exp(A_log.astype(np.float64))
        Ab = np.exp(delta * A)
        Bb = delta * Bm
        h = np.zeros((S, 128))
        hp = np.zeros(128)
        for t in range(S):
            hp = Ab[t] * hp + Bb[t]
            h[t] = hp
        y = (Cm * h).sum(-1)
        yw[b] = (y * w_gate[0]).astype(np.float32)
        z[b * S:(b + 1) * S] = (xi / (1 + np.exp(-xi))).astype(np.float32)
    return z, yw


def emit_l1_attn_out(p, nc, bufs, consts, zT_dram, yw_dram, WoutT, a1_dram):
    DI = 2048
    ywt = bufs.sb("ao_ywt", (64, 128), F32)
    ywc = bufs.sb("ao_ywc", (128, 64), F32)
    psy = bufs.ps("ao_psy", (128, 128), F32)
    p.dma(ywt[:, :], yw_dram.rearrange("b (ch p) -> (b ch) p", p=128))
    p.op("tensor", lambda e: e.transpose(psy[0:128, 0:64], ywt[:, :],
                                         consts["identf"][0:64, 0:64]))
    p.op("vector", lambda e: e.tensor_copy(ywc[:, :], psy[0:128, 0:64]))
    wt = [bufs.sb(f"ao_w{k}", (128, E), BF16) for k in range(16)]
    for kt in range(16):
        p.dma(wt[kt][:, :], WoutT[kt * 128:(kt + 1) * 128, :])
    zt = bufs.sb("ao_zt", (128, DI), BF16)
    res = bufs.sb("ao_res", (128, E), F32)
    ps = [bufs.ps(f"ao_ps{n}", (128, 512), F32) for n in range(2)]
    for i in range(NT):
        for kt in range(16):
            p.dma(zt[:, kt * 128:(kt + 1) * 128],
                  zT_dram[kt * 128:(kt + 1) * 128, i * 128:(i + 1) * 128])
        for n in range(2):
            for kt in range(16):
                p.op("tensor", lambda e, n=n, kt=kt: e.matmul(
                    ps[n][:, :], zt[:, kt * 128:(kt + 1) * 128],
                    wt[kt][:, n * 512:(n + 1) * 512],
                    start=(kt == 0), stop=(kt == 15)))
            p.op("vector", lambda e, n=n, i=i: e.tensor_scalar(
                res[:, n * 512:(n + 1) * 512], ps[n][:, :],
                ywc[:, i:i + 1], None, ALU.mult))
        p.dma(a1_dram[i * 128:(i + 1) * 128, :], res[:, :])



def load_bias_cols_partial(p, nc, bufs, dram_vec, lo, n, name):
    nf = (n + 127) // 128
    t = bufs.sb(name, (128, nf), F32)
    full = (n // 128) * 128
    src = dram_vec[lo:lo + full].rearrange("(f p) -> p f", p=128)
    p.dma(t[:, 0:n // 128], src, allow_slow_non_contiguous=True)
    if n % 128:
        src2 = dram_vec[lo + full:lo + n].rearrange("(p one) -> p one", one=1)
        p.dma(t[0:n % 128, nf - 1:nf], src2, allow_slow_non_contiguous=True)
    return t


def emit_l1_ffn(p, nc, bufs, consts, xn1_dram, wqkv, qs_d, ks_d, wo,
                lng_d, lnb_d, w1, b1_d, w2, b2_d, xf_dram, dbg=None,
                phases="ABCD"):
    HEADS, DH, INNER, HID = 8, 64, 512, 2730
    W = 512
    NW = S // W
    NZ = (HID + 127) // 128
    idb = consts["identb"]
    hT_dram = nc.dram_tensor("ffn_hT", [E, T], BF16, kind="Internal").ap()

    h_t_r = [bufs.sb(f"f_h{j}", (128, E), F32) for j in range(2)]
    hb_r = [bufs.sb(f"f_hb{j}", (128, E), BF16) for j in range(2)]
    qsr = bufs.sb("f_qsr", (1, 2 * DH), F32)
    ksr = bufs.sb("f_ksr", (1, 2 * DH), F32)
    p.dma(qsr[:, :], qs_d.rearrange("l d -> (l d)").rearrange("(a n) -> a n", a=1))
    p.dma(ksr[:, :], ks_d.rearrange("l d -> (l d)").rearrange("(a n) -> a n", a=1))
    p.op("vector", lambda e: e.tensor_scalar_mul(qsr[:, :], qsr[:, :], 8.0))
    affr = bufs.sb("f_affr", (1, E), F32)
    b2row = affr
    b1a = [load_bias_cols_partial(p, nc, bufs, b1_d, l * 2 * HID, HID, f"f_b1a{l}")
           for l in range(2)]
    b1g = [load_bias_cols_partial(p, nc, bufs, b1_d, l * 2 * HID + HID, HID,
                                  f"f_b1g{l}") for l in range(2)]

    wq_sb = [bufs.sb(f"f_wq{j}", (128, E), BF16) for j in range(2)]
    hTcs = [[bufs.sb(f"f_hTc{j}_{k}", (128, 512), BF16) for k in range(8)]
            for j in range(2)]
    hnTs = hTcs
    qkvT = [bufs.sb(f"f_qkvT{j}", (128, S), BF16) for j in range(12)]
    qf = bufs.sb("f_qf", (128, 512), F32)
    sq = bufs.sb("f_sq", (128, 512), F32)
    n2r = [bufs.sb(f"f_n2r{h}", (1, 512), F32) for h in range(2)]
    vtok = bufs.sb("f_vtok", (128, 8 * 64), BF16)
    pa = bufs.sb("f_pa", (128, 512), BF16)
    pb = bufs.sb("f_pb", (128, 128), BF16)
    pt = bufs.sb("f_pt", (128, 5 * 128), BF16)
    oTw = [bufs.sb(f"f_oTw{j}", (128, 512), BF16) for j in range(4)]
    den = bufs.sb("f_den", (128, 4), F32)
    denr = bufs.sb("f_denr", (1, 128), F32)
    bca = bufs.sb("f_bca", (128, 128), F32)
    wo_sb = [bufs.sb(f"f_wo{k}", (128, E), BF16) for k in range(4)]
    xres_r = [bufs.sb(f"f_xres{j}", (128, E), F32) for j in range(2)]
    xres = xres_r[0]
    zt = [bufs.sb(f"f_z{m}", (128, 512), BF16) for m in range(NZ)]
    w1sb = [bufs.sb(f"f_w1sb{j}", (128, 1024), BF16) for j in range(2)]
    w2sb = [bufs.sb(f"f_w2sb{j}", (128, 1024), BF16) for j in range(2)]
    gg = [bufs.sb(f"f_gg{j}", (128, 512), F32) for j in range(2)]
    gbc = bufs.sb("f_gbc", (128, 2 * E), F32)

    dbgS_sb = bufs.sb("f_dbgS", (128, 128), F32) if dbg is not None else None
    ps1 = bufs.ps("f_ps1", (128, 512), F32)
    ps2 = bufs.ps("f_ps2", (128, 512), F32)
    psS = bufs.ps("f_psS", (128, 512), F32)
    psB = bufs.ps("f_psB", (128, 128), F32)
    pst_r2 = [bufs.ps("f_0tr_ps0", (128, 128), BF16),
              bufs.ps("f_0tr_ps1", (128, 128), BF16)]
    pst = pst_r2[0]
    psmini = bufs.ps("f_psmini", (128, 512), F32)
    psav = bufs.ps("f_psav", (128, 128), F32)

    for l in range(2):
        src = xn1_dram if l == 0 else xf_dram
        for i in range(NT):
            h_t, hb = h_t_r[i % 2], hb_r[i % 2]
            p.dma(h_t[:, 0:512], src[i * 128:(i + 1) * 128, 0:512])
            if i % (S // 128) == 0:
                p.op("vector", lambda e, h_t=h_t: e.memset(h_t[0:1, 512:1024], 0.0))
                p.dma(h_t[1:128, 512:1024],
                      src[i * 128:(i + 1) * 128 - 1, 512:1024])
            else:
                p.dma(h_t[:, 512:1024],
                      src[i * 128 - 1:(i + 1) * 128 - 1, 512:1024])
            p.op("vector", lambda e, hb=hb, h_t=h_t: e.tensor_copy(hb[:, :], h_t[:, :]))
            emit_transpose_store(p, nc, bufs, hb, hT_dram, 0, i * 128, 8, idb,
                                 pref="f_0")
        for b in (range(BL) if "B" in phases else ()):
            for c in range(4):
                hTc = hTcs[c % 2]
                for k in range(8):
                    p.dma(hTc[k][:, :], hT_dram[k * 128:(k + 1) * 128,
                                                b * S + c * 512:b * S + (c + 1) * 512],
                          )
                for fo in range(12):
                    wq_ = wq_sb[fo % 2]
                    p.dma(wq_[:, 0:1024].rearrange("p (kt c) -> p kt c", c=128),
                          wqkv[l * E:l * E + 1024,
                               fo * 128:(fo + 1) * 128].rearrange(
                              "(kt p) c -> p kt c", p=128))
                    psq_ = [ps1, psS][fo % 2]
                    for kt in range(8):
                        p.op("tensor", lambda e, kt=kt, wq_=wq_, psq_=psq_: e.matmul(
                            psq_[:, :], wq_[:, kt * 128:(kt + 1) * 128],
                            hTc[kt][:, :], start=(kt == 0), stop=(kt == 7)))
                    if fo >= 8:
                        p.op("vector", lambda e, fo=fo, c=c, psq_=psq_: e.tensor_copy(
                            qkvT[fo][:, c * 512:(c + 1) * 512], psq_[:, :]))
                        continue
                    p.op("vector", lambda e, psq_=psq_: e.tensor_copy(qf[:, :], psq_[:, :]))
                    isq = fo < 4
                    gr = qsr if isq else ksr
                    rA = consts["rot_Aq"] if isq else consts["rot_Ak"]
                    rB = consts["rot_Bq"] if isq else consts["rot_Bk"]
                    p.op("scalar", lambda e: e.activation(sq[:, :], qf[:, :],
                                                          AF.Square))
                    for hh in range(2):
                        p.op("tensor", lambda e, hh=hh: e.matmul(
                            psmini[0:1, :], consts["cols"][:, 1 + hh:2 + hh],
                            sq[:, :], start=True, stop=True))
                        p.op("vector", lambda e, hh=hh: e.tensor_copy(
                            n2r[hh][:, :], psmini[0:1, :]))
                        p.op("scalar", lambda e, hh=hh: e.activation(
                            n2r[hh][:, :], n2r[hh][:, :], AF.Sqrt))
                        p.op("vector", lambda e, hh=hh: e.tensor_scalar_max(
                            n2r[hh][:, :], n2r[hh][:, :], 1e-12))
                        p.op("vector", lambda e, hh=hh: e.reciprocal(
                            n2r[hh][:, :], n2r[hh][:, :]))
                        p.op("tensor", lambda e, hh=hh, gr=gr, l=l: e.matmul(
                            ps2[hh * 64:hh * 64 + 64, :],
                            gr[0:1, l * 64:(l + 1) * 64], n2r[hh][:, :],
                            start=True, stop=True))
                    cs = slice(c * 512, (c + 1) * 512)
                    p.op("vector", lambda e: e.tensor_tensor(
                        ps1[:, :], qf[:, :], ps2[:, :], ALU.mult))
                    p.op("vector", lambda e, cs=cs, rA=rA: e.tensor_tensor(
                        sq[:, :], ps1[:, :], rA[:, cs], ALU.mult))
                    for qr in range(4):
                        sw = (qr + 1) if qr % 2 == 0 else (qr - 1)
                        p.op("vector", lambda e, qr=qr, sw=sw, cs=cs, rB=rB: e.tensor_tensor(
                            qf[qr * 32:(qr + 1) * 32, :],
                            ps1[sw * 32:(sw + 1) * 32, :],
                            rB[qr * 32:(qr + 1) * 32, cs], ALU.mult))
                    p.op("vector", lambda e, fo=fo, cs=cs: e.tensor_tensor(
                        qkvT[fo][:, cs], sq[:, :], qf[:, :], ALU.add))
            for w in (range(NW) if "C" in phases else ()):
                jt0 = max(0, (w - 1) * 4)
                njt = 4 if w == 0 else 8
                for j in range(4):
                    p.op("vector", lambda e, j=j: e.memset(oTw[j][:, :], 0.0))
                for h in range(HEADS):
                    ft, base = h // 2, (h % 2) * 64
                    qT = qkvT[ft]
                    kT = qkvT[4 + ft]
                    vT = qkvT[8 + ft]
                    for s_ in range(njt):
                        jt = jt0 + s_
                        p.op("tensor", lambda e, jt=jt, vT=vT, base=base: e.transpose(
                            pst[0:128, 0:64],
                            vT[base:base + 64, jt * 128:(jt + 1) * 128],
                            consts["identb64"][base:base + 64, 0:64]))
                        p.op("vector", lambda e, s_=s_: e.tensor_copy(
                            vtok[:, s_ * 64:(s_ + 1) * 64], pst[0:128, 0:64]))
                    for r in range(4):
                        q0 = w * W + r * 128
                        if w == 0:
                            kj0l = 0
                            nk = (r + 1) * 128
                        else:
                            kj0l = r * 128
                            nk = 640
                        kjg = jt0 * 128 + kj0l
                        nk1 = min(nk, 512)
                        p.op("tensor", lambda e, base=base, q0=q0, kjg=kjg, nk1=nk1, qT=qT, kT=kT: e.matmul(
                            psS[:, 0:nk1], qT[base:base + 64, q0:q0 + 128],
                            kT[base:base + 64, kjg:kjg + nk1],
                            start=True, stop=False, skip_group_check=True))
                        if nk > 512:
                            p.op("tensor", lambda e, base=base, q0=q0, kjg=kjg, qT=qT, kT=kT: e.matmul(
                                psB[:, :], qT[base:base + 64, q0:q0 + 128],
                                kT[base:base + 64, kjg + 512:kjg + 640],
                                start=True, stop=False, skip_group_check=True))
                        if w > 0:
                            p.op("tensor", lambda e: e.matmul(
                                psS[:, 0:128], idb[:, :], consts["maskLb"][:, :],
                                start=False, stop=True, skip_group_check=True))
                            p.op("tensor", lambda e: e.matmul(
                                psB[:, :], idb[:, :], consts["maskRb"][:, :],
                                start=False, stop=True, skip_group_check=True))
                        else:
                            p.op("tensor", lambda e, r=r: e.matmul(
                                psS[:, r * 128:(r + 1) * 128], idb[:, :],
                                consts["maskRb"][:, :],
                                start=False, stop=True, skip_group_check=True))
                        if (dbg is not None and "S00" in dbg and l == 0
                                and b == 0 and w == 0 and h == 0 and r == 0):
                            p.op("vector", lambda e: e.tensor_copy(
                                dbgS_sb[:, :], psS[:, 0:128]))
                            p.dma(dbg["S00"][:, :], dbgS_sb[:, :])
                        p.op("vector", lambda e, nk1=nk1: e.reduce_max(
                            den[:, 0:1], psS[:, 0:nk1], AX.X))
                        if nk > 512:
                            p.op("vector", lambda e: e.reduce_max(
                                den[:, 1:2], psB[:, :], AX.X))
                            p.op("vector", lambda e: e.tensor_max(
                                den[:, 0:1], den[:, 0:1], den[:, 1:2]))
                        p.op("vector", lambda e: e.tensor_scalar_mul(
                            den[:, 2:3], den[:, 0:1], -1.0))
                        p.op("scalar", lambda e, nk1=nk1: e.activation(
                            pa[:, 0:nk1], psS[:, 0:nk1], AF.Exp,
                            bias=den[:, 2:3], accum_out=den[:, 0:1]))
                        if nk > 512:
                            p.op("scalar", lambda e: e.activation(
                                pb[:, :], psB[:, :], AF.Exp,
                                bias=den[:, 2:3], accum_out=den[:, 1:2]))
                            p.op("vector", lambda e: e.tensor_add(
                                den[:, 0:1], den[:, 0:1], den[:, 1:2]))
                        p.op("vector", lambda e: e.reciprocal(
                            den[:, 3:4], den[:, 0:1]))
                        ntile = nk // 128
                        for s_ in range(ntile):
                            srcP = pa[:, (s_ % 4) * 128:(s_ % 4) * 128 + 128] \
                                if s_ < 4 else pb[:, 0:128]
                            p.op("tensor", lambda e, srcP=srcP: e.transpose(
                                pst[:, :], srcP, idb[:, :]))
                            p.op("vector", lambda e, s_=s_: e.tensor_copy(
                                pt[:, s_ * 128:(s_ + 1) * 128], pst[:, :]))
                        p.op("tensor", lambda e: e.transpose(
                            psmini[0:1, 0:128], den[:, 3:4],
                            consts["identf"][:, :]))
                        p.op("vector", lambda e: e.tensor_copy(
                            denr[:, :], psmini[0:1, 0:128]))
                        p.op("tensor", lambda e, base=base: e.matmul(
                            psmini[base:base + 64, 0:128],
                            consts["one_row"][0:1, 0:64], denr[:, :],
                            start=True, stop=True))
                        p.op("vector", lambda e, base=base: e.tensor_copy(
                            bca[base:base + 64, :],
                            psmini[base:base + 64, 0:128]))
                        for s_ in range(ntile):
                            sloc = (kj0l // 128) + s_ if w > 0 else s_
                            p.op("tensor", lambda e, s_=s_, sloc=sloc, base=base, ntile=ntile: e.matmul(
                                psav[base:base + 64, :],
                                vtok[:, sloc * 64:(sloc + 1) * 64],
                                pt[:, s_ * 128:(s_ + 1) * 128],
                                start=(s_ == 0), stop=(s_ == ntile - 1)))
                        p.op("vector", lambda e, ft=ft, base=base, r=r: e.tensor_tensor(
                            oTw[ft][base:base + 64, r * 128:(r + 1) * 128],
                            psav[base:base + 64, :], bca[base:base + 64, :],
                            ALU.mult))
                if w == 0 and b == 0:
                    for k in range(4):
                        p.dma(wo_sb[k][:, :],
                              wo[l * 512 + k * 128:l * 512 + (k + 1) * 128, :])
                for sc in range(4):
                    gi = b * (S // 128) + w * 4 + sc
                    xres = xres_r[sc % 2]
                    p.dma(xres[:, :], src[gi * 128:(gi + 1) * 128, :])
                    for n in range(2):
                        psw_ = [ps1, psS][(sc * 2 + n) % 2]
                        for k in range(4):
                            p.op("tensor", lambda e, n=n, k=k, sc=sc, psw_=psw_: e.matmul(
                                psw_[:, :], oTw[k][:, sc * 128:(sc + 1) * 128],
                                wo_sb[k][:, n * 512:(n + 1) * 512],
                                start=(k == 0), stop=(k == 3)))
                        p.op("vector", lambda e, n=n, xres=xres, psw_=psw_: e.tensor_tensor(
                            xres[:, n * 512:(n + 1) * 512], psw_[:, :],
                            xres[:, n * 512:(n + 1) * 512], ALU.add))
                    p.dma(xf_dram[gi * 128:(gi + 1) * 128, :], xres[:, :])
        if dbg is not None and l == 0:
            for nm, tile in (("q0", qkvT[0]), ("k0", qkvT[4]), ("v0", qkvT[8])):
                if nm in dbg:
                    p.dma(dbg[nm][:, :], tile[:, :])
            if "xattn0" in dbg:
                for i in range(NT):
                    p.dma(dbg["xattn0"][i * 128:(i + 1) * 128, :],
                          xf_dram[i * 128:(i + 1) * 128, :])
        p.dma(affr[0:1, 0:E], lng_d[l:l + 1, :])
        for n in range(2):
            p.op("tensor", lambda e, n=n: e.matmul(
                ps1[:, :], consts["one_row"][:, :],
                affr[0:1, n * 512:(n + 1) * 512], start=True, stop=True))
            p.op("vector", lambda e, n=n: e.tensor_copy(
                gbc[:, n * 512:(n + 1) * 512], ps1[:, :]))
        p.dma(affr[0:1, 0:E], lnb_d[l:l + 1, :])
        for n in range(2):
            p.op("tensor", lambda e, n=n: e.matmul(
                ps1[:, :], consts["one_row"][:, :],
                affr[0:1, n * 512:(n + 1) * 512], start=True, stop=True))
            p.op("vector", lambda e, n=n: e.tensor_copy(
                gbc[:, E + n * 512:E + (n + 1) * 512], ps1[:, :]))
        p.dma(b2row[0:1, 0:E], b2_d[l:l + 1, :])
        for gi in (range(NT) if "D" in phases else ()):
            h_t, hb = h_t_r[gi % 2], hb_r[gi % 2]
            p.dma(h_t[:, 0:512], xf_dram[gi * 128:(gi + 1) * 128, 0:512])
            if gi % (S // 128) == 0:
                p.op("vector", lambda e, h_t=h_t: e.memset(h_t[0:1, 512:1024], 0.0))
                p.dma(h_t[1:128, 512:1024],
                      xf_dram[gi * 128:(gi + 1) * 128 - 1, 512:1024])
            else:
                p.dma(h_t[:, 512:1024],
                      xf_dram[gi * 128 - 1:(gi + 1) * 128 - 1, 512:1024])
            emit_layernorm_tile(p, nc, bufs, h_t, h_t, 1e-5,
                                pref="f_%d" % (gi % 2), sq=hb)
            p.op("vector", lambda e, h_t=h_t: e.tensor_tensor(
                h_t[:, :], h_t[:, :], gbc[:, 0:E], ALU.mult))
            p.op("vector", lambda e, h_t=h_t: e.tensor_tensor(
                h_t[:, :], h_t[:, :], gbc[:, E:2 * E], ALU.add))
            p.op("vector", lambda e, hb=hb, h_t=h_t: e.tensor_copy(hb[:, :], h_t[:, :]))
            emit_transpose_store(p, nc, bufs, hb, hT_dram, 0, gi * 128, 8,
                                 idb, pref="f_0")
        for g in (range(T // 512) if "D" in phases else ()):
            hnT = hnTs[g % 2]
            for f in range(8):
                p.dma(hnT[f][:, :],
                      hT_dram[f * 128:(f + 1) * 128, g * 512:(g + 1) * 512])
            for m in range(NZ):
                mrows = min(128, HID - m * 128)
                for part, psx in ((0, [ps1, psS][m % 2]),
                                  (1, [ps2, psmini][m % 2])):
                    col0 = l * 2 * HID + part * HID + m * 128
                    wb_ = w1sb[(m * 2 + part) % 2]
                    p.dma(wb_[:, 0:8 * mrows].rearrange(
                              "p (kt c) -> p kt c", c=mrows),
                          w1[0:1024, col0:col0 + mrows].rearrange(
                              "(kt p) c -> p kt c", p=128))
                    for kt in range(8):
                        p.op("tensor", lambda e, kt=kt, mrows=mrows, psx=psx, wb_=wb_: e.matmul(
                            psx[0:mrows, :], wb_[:, kt * mrows:(kt + 1) * mrows],
                            hnT[kt][:, :], start=(kt == 0), stop=(kt == 7)))
                psa_, psg_ = [ps1, psS][m % 2], [ps2, psmini][m % 2]
                gg_ = gg[m % 2]
                p.op("scalar", lambda e, m=m, mrows=mrows, l=l, psg_=psg_, gg_=gg_: e.activation(
                    gg_[0:mrows, :], psg_[0:mrows, :], AF.Gelu,
                    bias=b1g[l][0:mrows, m:m + 1]))
                p.op("vector", lambda e, m=m, mrows=mrows, l=l, psa_=psa_, gg_=gg_: e.scalar_tensor_tensor(
                    zt[m][0:mrows, :], psa_[0:mrows, :], b1a[l][0:mrows, m:m + 1],
                    gg_[0:mrows, :], ALU.add, ALU.mult))
            ps4 = [ps1, ps2, psS, psmini]
            for scp in range(2):
                for scl in range(2):
                    sc = scp * 2 + scl
                    gi = g * 4 + sc
                    p.dma(xres_r[scl][:, :], xf_dram[gi * 128:(gi + 1) * 128, :])
                for n in range(2):
                    for m in range(NZ):
                        mrows = min(128, HID - m * 128)
                        wb_ = w2sb[m % 2]
                        p.dma(wb_[0:mrows, n * 512:(n + 1) * 512],
                              w2[l * HID + m * 128:l * HID + m * 128 + mrows,
                                 n * 512:(n + 1) * 512])
                        for scl in range(2):
                            sc = scp * 2 + scl
                            p.op("tensor", lambda e, m=m, mrows=mrows, sc=sc, n=n, wb_=wb_, scl=scl: e.matmul(
                                ps4[scl * 2 + n][:, :],
                                zt[m][0:mrows, sc * 128:(sc + 1) * 128],
                                wb_[0:mrows, n * 512:(n + 1) * 512],
                                start=(m == 0), stop=False,
                                skip_group_check=True))
                    for scl in range(2):
                        p.op("tensor", lambda e, n=n, scl=scl: e.matmul(
                            ps4[scl * 2 + n][:, :], consts["one_row"][:, :],
                            b2row[0:1, n * 512:(n + 1) * 512],
                            start=False, stop=True, skip_group_check=True))
                        p.op("vector", lambda e, n=n, scl=scl: e.tensor_tensor(
                            xres_r[scl][:, n * 512:(n + 1) * 512],
                            ps4[scl * 2 + n][:, :],
                            xres_r[scl][:, n * 512:(n + 1) * 512], ALU.add))
                for scl in range(2):
                    sc = scp * 2 + scl
                    gi = g * 4 + sc
                    p.dma(xf_dram[gi * 128:(gi + 1) * 128, :], xres_r[scl][:, :])
        if dbg is not None and l == 0 and "xf0" in dbg:
            for i in range(NT):
                p.dma(dbg["xf0"][i * 128:(i + 1) * 128, :],
                      xf_dram[i * 128:(i + 1) * 128, :])


def emit_l0_ffn(p, nc, bufs, consts, xn_dram, wdqT, bias_d, f0_dram):
    idb = consts["identb"]
    wt = [bufs.sb(f"lf_w{k}", (128, E), BF16) for k in range(8)]
    for kt in range(8):
        p.dma(wt[kt][:, :], wdqT[kt * 128:(kt + 1) * 128, :])
    brow = bufs.sb("lf_brow", (1, E), F32)
    p.dma(brow[:, :], bias_d[0:E].rearrange("(one n) -> one n", one=1))
    a_r = [bufs.sb(f"lf_a{j}", (128, E), F32) for j in range(2)]
    ab_r = [bufs.sb(f"lf_ab{j}", (128, E), BF16) for j in range(2)]
    xdT_r = [bufs.sb(f"lf_xdT{j}", (128, E), BF16) for j in range(2)]
    res_r = [bufs.sb(f"lf_res{j}", (128, E), F32) for j in range(2)]
    pst_r = [bufs.ps(f"lf_pst{j}", (128, 128), BF16) for j in range(2)]
    ps_all = [bufs.ps(f"lf_ps{n}", (128, 512), F32) for n in range(4)]
    for i in range(NT):
        j = i % 2
        a, ab, xdT, res, pst = a_r[j], ab_r[j], xdT_r[j], res_r[j], pst_r[j]
        ps = ps_all[j * 2:j * 2 + 2]
        p.dma(a[:, :], xn_dram[i * 128:(i + 1) * 128, :])
        emit_quant_tile(p, nc, bufs, a, a, pref="lf_q%d_" % j)
        p.op("vector", lambda e, ab=ab, a=a: e.tensor_copy(ab[:, :], a[:, :]))
        for f in range(8):
            p.op("tensor", lambda e, f=f, pst=pst, ab=ab: e.transpose(
                pst[:, :], ab[:, f * 128:(f + 1) * 128], idb[:, :]))
            p.op("vector", lambda e, f=f, xdT=xdT, pst=pst: e.tensor_copy(
                xdT[:, f * 128:(f + 1) * 128], pst[:, :]))
        for n in range(2):
            for kt in range(8):
                p.op("tensor", lambda e, n=n, kt=kt, ps=ps, xdT=xdT: e.matmul(
                    ps[n][:, :], xdT[:, kt * 128:(kt + 1) * 128],
                    wt[kt][:, n * 512:(n + 1) * 512],
                    start=(kt == 0), stop=False, skip_group_check=True))
            p.op("tensor", lambda e, n=n, ps=ps: e.matmul(
                ps[n][:, :], consts["one_row"][:, :],
                brow[:, n * 512:(n + 1) * 512],
                start=False, stop=True, skip_group_check=True))
            p.op("vector", lambda e, n=n, ps=ps, res=res: e.tensor_copy(
                res[:, n * 512:(n + 1) * 512], ps[n][:, :]))
        p.dma(f0_dram[i * 128:(i + 1) * 128, :], res[:, :])


def emit_prep(p, nc, bufs, consts, ins, outs):
    idb = consts["identb"]
    for nm in ("qw", "kw", "vw", "outw"):
        emit_cast_bf16(p, nc, bufs, ins[nm], outs[nm + "_bf"], E, E)
    emit_cast_bf16(p, nc, bufs, ins["wqkv"], outs["wqkv_bf"], 2 * E, 1536)
    emit_cast_bf16(p, nc, bufs, ins["wo"], outs["wo_bf"], E, E)
    emit_cast_bf16(p, nc, bufs, ins["w1"], outs["w1_bf"], E, 2 * 5460)
    emit_cast_bf16(p, nc, bufs, ins["w2"], outs["w2_bf"], 5460, E)
    emit_transpose_bf16(p, nc, bufs, ins["ffnw"], outs["wdqT"], E, E, idb,
                        quant=True, pref="pq_")
    emit_transpose_bf16(p, nc, bufs, ins["W_in"], outs["WinT"], 2048, E, idb,
                        pref="p1_")
    emit_transpose_bf16(p, nc, bufs, ins["W_x"], outs["WxT"], 320, 2048, idb,
                        pref="p2_")
    emit_transpose_bf16(p, nc, bufs, ins["W_dt"], outs["WdtT"], 128, 64, idb,
                        pref="p3_")
    emit_transpose_bf16(p, nc, bufs, ins["W_out"], outs["WoutT"], E, 2048, idb,
                        pref="p4_")


from contextlib import ExitStack


def _dram(nc, name, shape, dt, kind):
    return nc.dram_tensor(name, list(shape), dt, kind=kind).ap()


def _run_launch(build_fn, in_specs, out_specs, in_arrays_per_core, cneed,
                cvals):
    nc = bass.Bass("TRN2", target_bir_lowering=False)
    ins = {k: _dram(nc, k, sh, dt, "ExternalInput")
           for k, (sh, dt) in in_specs.items()}
    outs = {k: _dram(nc, k, sh, dt, "ExternalOutput")
            for k, (sh, dt) in out_specs.items()}
    p = P2(nc)
    with ExitStack() as stk:
        bufs = Bufs(nc, stk)
        consts = load_constants(p, nc, bufs, cvals, cneed)
        build_fn(p, nc, bufs, consts, ins, outs)
        p.emit()
    in_maps = []
    for c in range(NCORES):
        m = dict(in_arrays_per_core[c])
        for k in cneed:
            m["c_" + k] = cvals[k]
        in_maps.append(m)
    import os as _os
    tr = bool(_os.environ.get("TRACE_LAUNCH"))
    r = run_bass_kernel_spmd(nc, in_maps, core_ids=list(range(NCORES)),
                             trace=tr)
    if tr:
        print("LAUNCH exec_time_ns:", r.exec_time_ns, flush=True)
    return r.results


def kernel(**inputs):
    cvals = make_constants()
    x_full = np.ascontiguousarray(
        inputs["L0_attn_x"].reshape(32 * S, E).astype(np.float32))
    shards = [np.ascontiguousarray(x_full[c * T:(c + 1) * T])
              for c in range(NCORES)]

    f32c = np.ascontiguousarray
    wq = f32c(inputs["L0_attn_q_w"]); wk = f32c(inputs["L0_attn_k_w"])
    wv = f32c(inputs["L0_attn_v_w"]); wout = f32c(inputs["L0_attn_out_w"])
    ffnw = f32c(inputs["L0_ffn_weight"])
    W_in = f32c(inputs["L1_attn_W_in"])
    conv_w = f32c(inputs["L1_attn_conv_w"].reshape(2048, 4))
    W_x = f32c(inputs["L1_attn_W_x"])
    W_dt = f32c(inputs["L1_attn_W_dt"])
    wqkv = f32c(inputs["L1_ffn_wqkv"].reshape(2 * E, 1536))
    wo = f32c(inputs["L1_ffn_wo"].reshape(E, E))
    w1 = f32c(np.transpose(inputs["L1_ffn_w1"], (1, 0, 2)).reshape(E, 2 * 5460))
    w2 = f32c(inputs["L1_ffn_w2"].reshape(2 * 2730, E))
    W_out = f32c(inputs["L1_attn_W_out"])

    BF16s = lambda sh: (sh, BF16)
    F32s = lambda sh: (sh, F32)

    prep_in = {"qw": F32s([E, E]), "kw": F32s([E, E]), "vw": F32s([E, E]),
               "outw": F32s([E, E]), "ffnw": F32s([E, E]),
               "W_in": F32s([2048, E]), "W_x": F32s([320, 2048]),
               "W_dt": F32s([128, 64]), "W_out": F32s([E, 2048]),
               "wqkv": F32s([2 * E, 1536]), "wo": F32s([E, E]),
               "w1": F32s([E, 2 * 5460]), "w2": F32s([5460, E])}
    prep_out = {"qw_bf": BF16s([E, E]), "kw_bf": BF16s([E, E]),
                "vw_bf": BF16s([E, E]), "outw_bf": BF16s([E, E]),
                "wdqT": BF16s([E, E]), "WinT": BF16s([E, 2048]),
                "WxT": BF16s([2048, 320]), "WdtT": BF16s([64, 128]),
                "WoutT": BF16s([2048, E]), "wqkv_bf": BF16s([2 * E, 1536]),
                "wo_bf": BF16s([E, E]), "w1_bf": BF16s([E, 2 * 5460]),
                "w2_bf": BF16s([5460, E])}
    prep_arrs = {"qw": wq, "kw": wk, "vw": wv, "outw": wout, "ffnw": ffnw,
                 "W_in": W_in, "W_x": W_x, "W_dt": W_dt, "W_out": W_out,
                 "wqkv": wqkv, "wo": wo, "w1": w1, "w2": w2}
    rp = _run_launch(
        lambda p, nc, bufs, consts, ins, outs: emit_prep(p, nc, bufs, consts,
                                                         ins, outs),
        prep_in, prep_out, [prep_arrs] * NCORES, {"identb", "one_row"}, cvals)

    r2 = _run_launch(
        lambda p, nc, bufs, consts, ins, outs: emit_ln0(
            p, nc, bufs, consts, ins["x"], outs["xn"], outs["xnT"]),
        {"x": F32s([T, E])},
        {"xn": F32s([T, E]), "xnT": BF16s([E, T])},
        [{"x": shards[c]} for c in range(NCORES)], {"identb"}, cvals)

    r3 = _run_launch(
        lambda p, nc, bufs, consts, ins, outs: emit_l0_attn(
            p, nc, bufs, consts, ins["xnT"], ins["qw"], ins["kw"], ins["vw"],
            ins["qb"], ins["kb"], ins["vb"], ins["proj"], outs["attnT"]),
        {"xnT": BF16s([E, T]), "qw": BF16s([E, E]), "kw": BF16s([E, E]),
         "vw": BF16s([E, E]), "qb": F32s([E]), "kb": F32s([E]),
         "vb": F32s([E]), "proj": F32s([256, 64])},
        {"attnT": BF16s([E, T])},
        [{"xnT": r2[c]["xnT"], "qw": rp[c]["qw_bf"], "kw": rp[c]["kw_bf"],
          "vw": rp[c]["vw_bf"], "qb": inputs["L0_attn_q_b"],
          "kb": inputs["L0_attn_k_b"], "vb": inputs["L0_attn_v_b"],
          "proj": f32c(inputs["L0_attn_proj"])} for c in range(NCORES)],
        {"identb", "identf", "identb64", "triu_mult", "cols", "negone_row"},
        cvals)

    r4 = _run_launch(
        lambda p, nc, bufs, consts, ins, outs: emit_l0_post(
            p, nc, bufs, consts, ins["attnT"], ins["outw"], ins["outb"],
            ins["xn"], ins["gamma"], ins["beta"], outs["a0"]),
        {"attnT": BF16s([E, T]), "outw": BF16s([E, E]), "outb": F32s([E]),
         "xn": F32s([T, E]), "gamma": F32s([E]), "beta": F32s([E])},
        {"a0": F32s([T, E])},
        [{"attnT": r3[c]["attnT"], "outw": rp[c]["outw_bf"],
          "outb": inputs["L0_attn_out_b"], "xn": r2[c]["xn"],
          "gamma": inputs["L0_attn_gamma"], "beta": inputs["L0_attn_beta"]}
         for c in range(NCORES)], {"one_row"}, cvals)

    r5 = _run_launch(
        lambda p, nc, bufs, consts, ins, outs: emit_l0_ffn(
            p, nc, bufs, consts, ins["xn"], ins["wdqT"], ins["fb"],
            outs["f0"]),
        {"xn": F32s([T, E]), "wdqT": BF16s([E, E]), "fb": F32s([E])},
        {"f0": F32s([T, E])},
        [{"xn": r2[c]["xn"], "wdqT": rp[c]["wdqT"],
          "fb": inputs["L0_ffn_bias"]} for c in range(NCORES)],
        {"identb", "one_row"}, cvals)

    r6 = _run_launch(
        lambda p, nc, bufs, consts, ins, outs: emit_combine_ln(
            p, nc, bufs, consts, [ins["x"], ins["a0"], ins["f0"]],
            outs["x1"], outs["xn1"], outs["xn1T"]),
        {"x": F32s([T, E]), "a0": F32s([T, E]), "f0": F32s([T, E])},
        {"x1": F32s([T, E]), "xn1": F32s([T, E]), "xn1T": BF16s([E, T])},
        [{"x": shards[c], "a0": r4[c]["a0"], "f0": r5[c]["f0"]}
         for c in range(NCORES)], {"identb"}, cvals)

    r7 = _run_launch(
        lambda p, nc, bufs, consts, ins, outs: emit_l1_attn(
            p, nc, bufs, consts, ins["xn1T"], ins["WinT"], ins["convw"],
            ins["convb"], ins["WxT"], ins["WdtT"], ins["bdt"], ins["Alog"],
            ins["wg"], outs["zT"], outs["yw"]),
        {"xn1T": BF16s([E, T]), "WinT": BF16s([E, 2048]),
         "convw": F32s([2048, 4]), "convb": F32s([2048]),
         "WxT": BF16s([2048, 320]), "WdtT": BF16s([64, 128]),
         "bdt": F32s([128]), "Alog": F32s([128]), "wg": F32s([1])},
        {"zT": BF16s([2048, T]), "yw": F32s([BL, S])},
        [{"xn1T": r6[c]["xn1T"], "WinT": rp[c]["WinT"], "convw": conv_w,
          "convb": inputs["L1_attn_conv_b"], "WxT": rp[c]["WxT"],
          "WdtT": rp[c]["WdtT"], "bdt": inputs["L1_attn_b_dt"],
          "Alog": inputs["L1_attn_A_log"], "wg": inputs["L1_attn_w_gate"]}
         for c in range(NCORES)], {"cols", "identf"}, cvals)

    r8 = _run_launch(
        lambda p, nc, bufs, consts, ins, outs: emit_l1_attn_out(
            p, nc, bufs, consts, ins["zT"], ins["yw"], ins["WoutT"],
            outs["a1"]),
        {"zT": BF16s([2048, T]), "yw": F32s([BL, S]),
         "WoutT": BF16s([2048, E])},
        {"a1": F32s([T, E])},
        [{"zT": r7[c]["zT"], "yw": r7[c]["yw"], "WoutT": rp[c]["WoutT"]}
         for c in range(NCORES)], {"identf"}, cvals)

    r9 = _run_launch(
        lambda p, nc, bufs, consts, ins, outs: emit_l1_ffn(
            p, nc, bufs, consts, ins["xn1"], ins["wqkv"], ins["qs"],
            ins["ks"], ins["wo"], ins["lng"], ins["lnb"], ins["w1"],
            ins["b1"], ins["w2"], ins["b2"], outs["xf"]),
        {"xn1": F32s([T, E]), "wqkv": BF16s([2 * E, 1536]),
         "qs": F32s([2, 64]), "ks": F32s([2, 64]), "wo": BF16s([E, E]),
         "lng": F32s([2, E]), "lnb": F32s([2, E]),
         "w1": BF16s([E, 2 * 5460]), "b1": F32s([2 * 5460]),
         "w2": BF16s([5460, E]), "b2": F32s([2, E])},
        {"xf": F32s([T, E])},
        [{"xn1": r6[c]["xn1"], "wqkv": rp[c]["wqkv_bf"],
          "qs": f32c(inputs["L1_ffn_q_scale"]),
          "ks": f32c(inputs["L1_ffn_k_scale"]), "wo": rp[c]["wo_bf"],
          "lng": f32c(inputs["L1_ffn_ln_g"]), "lnb": f32c(inputs["L1_ffn_ln_b"]),
          "w1": rp[c]["w1_bf"], "b1": f32c(inputs["L1_ffn_b1"].reshape(-1)),
          "w2": rp[c]["w2_bf"], "b2": f32c(inputs["L1_ffn_b2"])}
         for c in range(NCORES)],
        {"identb", "identf", "identb64", "maskLb", "maskRb",
         "rot_Aq", "rot_Bq", "rot_Ak", "rot_Bk", "cols", "one_row"}, cvals)

    r10 = _run_launch(
        lambda p, nc, bufs, consts, ins, outs: emit_sum_final(
            p, nc, bufs, [ins["x1"], ins["a1"], ins["xf"]], outs["out"]),
        {"x1": F32s([T, E]), "a1": F32s([T, E]), "xf": F32s([T, E])},
        {"out": F32s([T, E])},
        [{"x1": r6[c]["x1"], "a1": r8[c]["a1"], "xf": r9[c]["xf"]}
         for c in range(NCORES)], set(), cvals)

    out = np.concatenate([r10[c]["out"] for c in range(NCORES)], axis=0)
    return out.reshape(32, S, E).astype(np.float32)


class _RecInst:
    def then_inc(self, *a, **k):
        return self


class _Recorder:

    def __init__(self):
        self.call = None

    def __getattr__(self, name):
        def f(*args, **kwargs):
            assert self.call is None, "op fn must emit exactly one instruction"
            self.call = (name, args, kwargs)
            return _RecInst()
        return f


def _ap_name(a):
    try:
        return a.tensor.name
    except Exception:
        return None


def _ap_is_psum(a):
    try:
        return "PSUM" in str(a.space).upper() or "PSUM" in str(
            getattr(a.tensor, "space", "")).upper()
    except Exception:
        return False


class P2:
    ENGINES = ("sync", "scalar", "vector", "tensor", "gpsimd")
    DMAQS = ("sync", "gpsimd")

    def __init__(self, nc, same_engine_edges=True):
        self.nc = nc
        self.ops = []
        self.last_w = {}
        self.readers = {}
        self.counts = {e: 0 for e in self.ENGINES}
        self.wcount = {}
        self.qof = {}
        self.see = same_engine_edges
        self.rr = 0
        self.barrier_pending = None
        self.barriers = []

    def _dep_events(self, reads, writes):
        deps = []
        for nm in reads:
            ev = self.last_w.get(nm)
            if ev:
                deps.append(ev)
        for nm in writes:
            ev = self.last_w.get(nm)
            if ev:
                deps.append(ev)
            deps.extend(self.readers.get(nm, ()))
        return deps

    def _record(self, engine, call, is_dma, queue=None):
        name, args, kwargs = call
        writes, reads = [], []
        out_kw = kwargs.get("out")
        if out_kw is not None and _ap_name(out_kw):
            writes.append(_ap_name(out_kw))
        acc = kwargs.get("accum_out")
        if acc is not None and _ap_name(acc):
            writes.append(_ap_name(acc))
        psum_reads = []
        for i, a in enumerate(args):
            nm = _ap_name(a)
            if nm is None:
                continue
            if i == 0 and out_kw is None:
                writes.append(nm)
            else:
                reads.append(nm)
                if _ap_is_psum(a):
                    psum_reads.append(nm)
        for k, v in kwargs.items():
            if k in ("out", "accum_out"):
                continue
            nm = _ap_name(v)
            if nm is not None:
                reads.append(nm)
                if _ap_is_psum(v):
                    psum_reads.append(nm)
        deps = self._dep_events(reads, writes)
        for nm in psum_reads:
            for r in self.readers.get(nm, ()):
                if r[0] == "c" and r[1] != engine:
                    deps.append(r)
        if self.barrier_pending is not None:
            deps = deps + [("b", self.barrier_pending, 0)]
        if is_dma:
            wn = writes[0]
            self.wcount[wn] = self.wcount.get(wn, 0) + 1
            self.qof[wn] = queue
            ev = ("t", wn, self.wcount[wn])
        else:
            self.counts[engine] += 1
            ev = ("c", engine, self.counts[engine])
        for nm in reads:
            self.readers.setdefault(nm, []).append(ev)
        for nm in writes:
            self.last_w[nm] = ev
            self.readers[nm] = []
        self.ops.append(dict(engine=engine, call=call, is_dma=is_dma,
                             queue=queue, deps=deps, ev=ev))

    def op(self, engine, fn, inc=1):
        rec = _Recorder()
        fn(rec)
        self._record(engine, rec.call, False)

    def dma(self, out, in_, engine=None, **kw):
        nm = _ap_name(out) or ""
        q = self.DMAQS[(hash(nm) & 0xffff) % len(self.DMAQS)]
        self._record(q, ("dma_start", (), dict(out=out, in_=in_, **kw)),
                     True, queue=q)

    def barrier(self):
        bid = len(self.barriers)
        snap = ([("c", e, self.counts[e]) for e in self.ENGINES
                 if self.counts[e] > 0]
                + [("t", nm, c) for nm, c in self.wcount.items() if c > 0])
        self.barriers.append(snap)
        self.barrier_pending = bid

    def emit(self):
        nc = self.nc
        assign = {}
        pool = {q: [] for q in self.DMAQS}
        sem_carried = {}
        qof_sem = {}
        n_sems = [0]
        covered = {}
        walk_count = {}
        barrier_tr = {}
        last_barrier_seen = [-1]

        def tsem(nm):
            if nm not in assign:
                q = self.qof[nm]
                if pool[q]:
                    si = pool[q].pop()
                else:
                    si = n_sems[0]
                    n_sems[0] += 1
                    sem_carried[si] = 0
                    qof_sem[si] = q
                assign[nm] = (si, sem_carried[si], covered.get(nm, 0))
            return assign[nm]

        def recycle_all():
            for nm, (si, base, sc) in list(assign.items()):
                sem_carried[si] = base + 16 * (walk_count.get(nm, 0) - sc)
                pool[qof_sem[si]].append(si)
            assign.clear()

        plan = []
        for oi, o in enumerate(self.ops):
            waits = {}

            def add_wait(key, val):
                if waits.get(key, 0) < val:
                    waits[key] = val

            bdep = None
            for d in o["deps"]:
                if d[0] == "b":
                    bdep = d[1]
            if bdep is not None and bdep not in barrier_tr:
                tr = []
                for (k2, s2, v2) in self.barriers[bdep]:
                    if k2 == "c":
                        tr.append((("c", s2), v2))
                    elif s2 in assign:
                        si, base, sc = assign[s2]
                        tr.append((("t", si), base + 16 * (v2 - sc)))
                barrier_tr[bdep] = tr
                for (k2, s2, v2) in self.barriers[bdep]:
                    if k2 == "t":
                        covered[s2] = max(covered.get(s2, 0), v2)
                last_barrier_seen[0] = bdep
                recycle_all()
            if bdep is not None:
                for (key, val) in barrier_tr[bdep]:
                    add_wait(key, val)
            for d in o["deps"]:
                if d[0] == "b":
                    continue
                if d[0] == "c":
                    if d[1] == o["engine"] and (d[1] == "tensor"
                                                or not self.see):
                        continue
                    add_wait(("c", d[1]), d[2])
                else:
                    nm, k = d[1], d[2]
                    if k <= covered.get(nm, 0):
                        continue
                    si, base, sc = tsem(nm)
                    add_wait(("t", si), base + 16 * (k - sc))
            if o["is_dma"]:
                wn = o["ev"][1]
                si, base, sc = tsem(wn)
                walk_count[wn] = walk_count.get(wn, 0) + 1
                inc = ("t", si)
            else:
                inc = ("c", o["engine"])
            plan.append((sorted(waits.items()), inc))
        final_c = {e: self.counts[e] for e in self.ENGINES if self.counts[e]}
        final_t = {}
        for nm, (si, base, sc) in assign.items():
            final_t[si] = max(final_t.get(si, 0),
                              base + 16 * (walk_count.get(nm, 0) - sc))
        for si, v in sem_carried.items():
            if v > 0:
                final_t[si] = max(final_t.get(si, 0), v)
        by_eng = {}
        for oi, o in enumerate(self.ops):
            by_eng.setdefault(o["engine"], []).append(oi)
        with ExitStack() as stk:
            csem = {e: stk.enter_context(nc.semaphore("s_" + e))
                    for e in self.ENGINES}
            tsems = [stk.enter_context(nc.semaphore(f"st_{i}"))
                     for i in range(n_sems[0])]
            with nc.Block() as block:
                def mk(ename):
                    def body(eng):
                        waited = {}
                        for oi in by_eng.get(ename, []):
                            o = self.ops[oi]
                            w8, inc = plan[oi]
                            for (key, val) in w8:
                                if waited.get(key, 0) >= val:
                                    continue
                                sem = (csem[key[1]] if key[0] == "c"
                                       else tsems[key[1]])
                                eng.wait_ge(sem, val)
                                waited[key] = val
                            name, args, kwargs = o["call"]
                            inst = getattr(eng, name)(*args, **kwargs)
                            if inc[0] == "t":
                                inst.then_inc(tsems[inc[1]], 16)
                            else:
                                inst.then_inc(csem[ename], 1)
                        if ename == "sync":
                            for e2, v in final_c.items():
                                eng.wait_ge(csem[e2], v)
                            for si, v in final_t.items():
                                eng.wait_ge(tsems[si], v)
                    body.__name__ = "body_" + ename
                    return body
                for ename in self.ENGINES:
                    if ename in by_eng or ename == "sync":
                        getattr(block, ename)(mk(ename))
        return len(self.ops)


USE_SCHED = True
def kernel_fused(**inputs):
    cvals = make_constants()
    x_full = np.ascontiguousarray(
        inputs["L0_attn_x"].reshape(32 * S, E).astype(np.float32))
    shards = [np.ascontiguousarray(x_full[c * T:(c + 1) * T])
              for c in range(NCORES)]
    f32c = np.ascontiguousarray

    nc = bass.Bass("TRN2", target_bir_lowering=False)
    EI, INT, EO = "ExternalInput", "Internal", "ExternalOutput"
    di = {}
    def ein(name, shape, dt):
        di[name] = _dram(nc, name, shape, dt, EI)
        return di[name]
    x_d = ein("x", [T, E], F32)
    qw_d = ein("qw", [E, E], F32); kw_d = ein("kw", [E, E], F32)
    vw_d = ein("vw", [E, E], F32); outw_d = ein("outw", [E, E], F32)
    qb_d = ein("qb", [E], F32); kb_d = ein("kb", [E], F32)
    vb_d = ein("vb", [E], F32); outb_d = ein("outb", [E], F32)
    proj_d = ein("proj", [256, 64], F32)
    gamma_d = ein("gamma", [E], F32); beta_d = ein("beta", [E], F32)
    ffnw_d = ein("ffnw", [E, E], F32); fb_d = ein("fb", [E], F32)
    Win_d = ein("W_in", [2048, E], F32)
    convw_d = ein("convw", [2048, 4], F32); convb_d = ein("convb", [2048], F32)
    Wx_d = ein("W_x", [320, 2048], F32); Wdt_d = ein("W_dt", [128, 64], F32)
    bdt_d = ein("bdt", [128], F32); Alog_d = ein("Alog", [128], F32)
    wg_d = ein("wg", [1], F32); Wout_d = ein("W_out", [E, 2048], F32)
    wqkv_d = ein("wqkv", [2 * E, 1536], F32)
    qs_d = ein("qs", [2, 64], F32); ks_d = ein("ks", [2, 64], F32)
    wo_d = ein("wo", [E, E], F32)
    lng_d = ein("lng", [2, E], F32); lnb_d = ein("lnb", [2, E], F32)
    w1_d = ein("w1", [E, 2 * 5460], F32); b1_d = ein("b1", [2 * 5460], F32)
    w2_d = ein("w2", [5460, E], F32); b2_d = ein("b2", [2, E], F32)
    out_d = _dram(nc, "out", [T, E], F32, EO)

    sc = {}
    for name, shape, dt in (
        ("qw_bf", [E, E], BF16), ("kw_bf", [E, E], BF16),
        ("vw_bf", [E, E], BF16), ("outw_bf", [E, E], BF16),
        ("wdqT", [E, E], BF16), ("WinT", [E, 2048], BF16),
        ("WxT", [2048, 320], BF16), ("WdtT", [64, 128], BF16),
        ("WoutT", [2048, E], BF16), ("wqkv_bf", [2 * E, 1536], BF16),
        ("wo_bf", [E, E], BF16), ("w1_bf", [E, 2 * 5460], BF16),
        ("w2_bf", [5460, E], BF16),
        ("xn", [T, E], F32), ("xnT", [E, T], BF16),
        ("attnT", [E, T], BF16), ("a0", [T, E], F32), ("f0", [T, E], F32),
        ("x1", [T, E], F32), ("xn1", [T, E], F32), ("xn1T", [E, T], BF16),
        ("zT", [2048, T], BF16), ("yw", [BL, S], F32), ("a1", [T, E], F32),
        ("xf", [T, E], F32),
    ):
        sc[name] = _dram(nc, name, shape, dt, INT)

    p = P2(nc) if USE_SCHED else P(nc)
    with ExitStack() as stk0:
        cb = Bufs(nc, stk0)
        consts = load_constants(p, nc, cb, cvals)

        def stage(fn):
            with ExitStack() as stks:
                bufs = Bufs(nc, stks)
                fn(bufs)
            if USE_SCHED:
                p.barrier()

        stage(lambda bufs: emit_prep(p, nc, bufs, consts,
              {"qw": qw_d, "kw": kw_d, "vw": vw_d, "outw": outw_d,
               "ffnw": ffnw_d, "W_in": Win_d, "W_x": Wx_d, "W_dt": Wdt_d,
               "W_out": Wout_d, "wqkv": wqkv_d, "wo": wo_d, "w1": w1_d,
               "w2": w2_d}, sc))
        stage(lambda bufs: emit_ln0(p, nc, bufs, consts, x_d, sc["xn"],
                                    sc["xnT"]))
        stage(lambda bufs: emit_l0_attn(p, nc, bufs, consts, sc["xnT"],
              sc["qw_bf"], sc["kw_bf"], sc["vw_bf"], qb_d, kb_d, vb_d,
              proj_d, sc["attnT"]))
        stage(lambda bufs: emit_l0_post(p, nc, bufs, consts, sc["attnT"],
              sc["outw_bf"], outb_d, sc["xn"], gamma_d, beta_d, sc["a0"]))
        stage(lambda bufs: emit_l0_ffn(p, nc, bufs, consts, sc["xn"],
              sc["wdqT"], fb_d, sc["f0"]))
        stage(lambda bufs: emit_combine_ln(p, nc, bufs, consts,
              [x_d, sc["a0"], sc["f0"]], sc["x1"], sc["xn1"], sc["xn1T"]))
        stage(lambda bufs: emit_l1_attn(p, nc, bufs, consts, sc["xn1T"],
              sc["WinT"], convw_d, convb_d, sc["WxT"], sc["WdtT"], bdt_d,
              Alog_d, wg_d, sc["zT"], sc["yw"]))
        stage(lambda bufs: emit_l1_attn_out(p, nc, bufs, consts, sc["zT"],
              sc["yw"], sc["WoutT"], sc["a1"]))
        stage(lambda bufs: emit_l1_ffn(p, nc, bufs, consts, sc["xn1"],
              sc["wqkv_bf"], qs_d, ks_d, sc["wo_bf"], lng_d, lnb_d,
              sc["w1_bf"], b1_d, sc["w2_bf"], b2_d, sc["xf"]))
        stage(lambda bufs: emit_sum_final(p, nc, bufs,
              [sc["x1"], sc["a1"], sc["xf"]], out_d))
        p.emit()

    shared = {
        "qw": f32c(inputs["L0_attn_q_w"]), "kw": f32c(inputs["L0_attn_k_w"]),
        "vw": f32c(inputs["L0_attn_v_w"]), "outw": f32c(inputs["L0_attn_out_w"]),
        "qb": f32c(inputs["L0_attn_q_b"]), "kb": f32c(inputs["L0_attn_k_b"]),
        "vb": f32c(inputs["L0_attn_v_b"]), "outb": f32c(inputs["L0_attn_out_b"]),
        "proj": f32c(inputs["L0_attn_proj"]),
        "gamma": f32c(inputs["L0_attn_gamma"]), "beta": f32c(inputs["L0_attn_beta"]),
        "ffnw": f32c(inputs["L0_ffn_weight"]), "fb": f32c(inputs["L0_ffn_bias"]),
        "W_in": f32c(inputs["L1_attn_W_in"]),
        "convw": f32c(inputs["L1_attn_conv_w"].reshape(2048, 4)),
        "convb": f32c(inputs["L1_attn_conv_b"]),
        "W_x": f32c(inputs["L1_attn_W_x"]), "W_dt": f32c(inputs["L1_attn_W_dt"]),
        "bdt": f32c(inputs["L1_attn_b_dt"]), "Alog": f32c(inputs["L1_attn_A_log"]),
        "wg": f32c(inputs["L1_attn_w_gate"]), "W_out": f32c(inputs["L1_attn_W_out"]),
        "wqkv": f32c(inputs["L1_ffn_wqkv"].reshape(2 * E, 1536)),
        "qs": f32c(inputs["L1_ffn_q_scale"]), "ks": f32c(inputs["L1_ffn_k_scale"]),
        "wo": f32c(inputs["L1_ffn_wo"].reshape(E, E)),
        "lng": f32c(inputs["L1_ffn_ln_g"]), "lnb": f32c(inputs["L1_ffn_ln_b"]),
        "w1": f32c(np.transpose(inputs["L1_ffn_w1"], (1, 0, 2)).reshape(E, 2 * 5460)),
        "b1": f32c(inputs["L1_ffn_b1"].reshape(-1)),
        "w2": f32c(inputs["L1_ffn_w2"].reshape(2 * 2730, E)),
        "b2": f32c(inputs["L1_ffn_b2"]),
    }
    for k2 in CONST_SPECS:
        shared["c_" + k2] = cvals[k2]
    in_maps = [dict(shared, x=shards[c]) for c in range(NCORES)]
    r = run_bass_kernel_spmd(nc, in_maps, core_ids=list(range(NCORES)))
    out = np.concatenate([r.results[c]["out"] for c in range(NCORES)], axis=0)
    return out.reshape(32, S, E).astype(np.float32)




def p2_truncate(p, K):
    p.ops = p.ops[:K]
    p.counts = {e: 0 for e in p.ENGINES}
    p.wcount = {}
    for o in p.ops:
        k, s, v = o["ev"]
        if k == "c":
            p.counts[s] = v
        else:
            p.wcount[s] = v


kernel = kernel_fused
